# version 15
# baseline (speedup 1.0000x reference)
import numpy as np
import ml_dtypes
from contextlib import ExitStack
import concourse.bass as bass
import concourse.mybir as mybir
from concourse.bass_utils import run_bass_kernel_spmd
from concourse.alu_op_type import AluOpType as OP

F32 = mybir.dt.float32
BF16 = mybir.dt.bfloat16
AF = mybir.ActivationFunctionType
AX = mybir.AxisListType

NCORES = 8
D = 1024
S = 4096
B = 4
T = 2048
NCH = D // 128
LN_EPS = 1e-5

bf16 = ml_dtypes.bfloat16





class MegaCtx:
    def __init__(self, nc, block, bar, reserved=()):
        self.nc = nc
        self.block = block
        self.bar = bar
        self.nbar = 0
        self.reserved = list(reserved)
        self.ges = None
        self._joined = False

    def begin_group(self):
        self.ges = ExitStack()
        self._joined = False

    def end_group(self):
        es, self.ges = self.ges, None
        es.close()

    def barrier(self):
        self.nbar += 1
        target = 5 * self.nbar
        bar = self.bar
        for eng_name in ("sync", "gpsimd", "tensor", "scalar", "vector"):
            dec = getattr(self.block, eng_name)

            def body(eng, bar=bar, target=target):
                eng.drain()
                eng.sem_inc(bar, 1)
                eng.wait_ge(bar, target)
            dec(body)

    def clean(self, semhs):
        if self.ges is not None and self._joined:
            return
        if self.ges is not None:
            self._joined = True
        first = self.nbar == 0
        self.barrier()
        lo = self.bar.num + 1
        if not first and self.reserved:
            lo = max([self.bar.num] + [h.num for h in self.reserved]) + 1
        rng = range(lo, self.nc._kernel_sem_range.stop)

        @self.block.gpsimd
        def _(g):
            g.dma_reset(rng)
            g.sem_clear(rng)
        self.barrier()


def _enter(ctx, es, name, semhs):
    if ctx is None:
        nc = _mk(name)
        return nc, None
    return ctx.nc, ctx.block


_UNAME_CTR = [0]

def _u(n):
    _UNAME_CTR[0] += 1
    return f"{n}__{_UNAME_CTR[0]}"


def _clean_sems(nc, hs):
    nums = sorted(h.num for h in hs)
    rng = range(nums[0], nums[-1] + 1)
    nc.gpsimd.dma_reset(rng)
    nc.gpsimd.sem_clear(rng)
    nc.all_engine_barrier()

def _mk(nc_name):
    return bass.Bass("TRN2", target_bir_lowering=False, detect_race_conditions=False)


def build_ln(Tn=T, out_f32=True, ctx=None, A=None):
    if ctx is None:
        nc = _mk("ln")
        A = dict(x=nc.dram_tensor("xT", [D, Tn], F32, kind="ExternalInput").ap(),
                 xnf=nc.dram_tensor("xnf", [D, Tn], F32, kind="ExternalOutput").ap() if out_f32 else None,
                 xnb=nc.dram_tensor("xnb", [D, Tn], BF16, kind="ExternalOutput").ap())
    else:
        nc = ctx.nc
    x, xnf, xnb = A["x"], A.get("xnf"), A["xnb"]
    NB = Tn // 512
    KOUT = NCH * (2 if out_f32 else 1)

    with ExitStack() as es:
        sb = lambda n, *a: es.enter_context(nc.sbuf_tensor(_u(n), *a))
        ps = lambda n, *a: es.enter_context(nc.psum_tensor(_u(n), *a))
        _semhs = []
        def sem(n):
            h = es.enter_context(nc.semaphore(_u(n)))
            _semhs.append(h)
            return h
        s_x = sb("s_x", [128, NCH * Tn], F32)
        s_sq = sb("s_sq", [128, NCH * 512], F32)
        s_stat = sb("s_stat", [1, 7 * 512], F32)
        s_xnf = sb("s_xnf", [128, NCH * 512], F32)
        s_xnb = sb("s_xnb", [128, NCH * 512], BF16)
        s_one = sb("s_one", [128, 1], F32)
        s_onerow = sb("s_onerow", [1, 128], F32)
        s_spcv = sb("s_spcv", [128, 512], F32)
        s_spcs = sb("s_spcs", [128, 512], F32)
        p_sum = ps("p_sum", [1, 512], F32)
        p_sq = ps("p_sq", [1, 512], F32)
        p_r = ps("p_r", [128, 512], F32)
        p_mr = ps("p_mr", [128, 512], F32)
        ld = sem("ld"); sq = sem("sq"); te = sem("te"); ve = sem("ve")
        se = sem("se"); st = sem("st"); apd = sem("apd")
        if ctx is None:
            _clean_sems(nc, _semhs)
            block = es.enter_context(nc.Block())
        else:
            ctx.clean(_semhs)
            block = ctx.block
        SL = lambda i: s_stat[:, i * 512:(i + 1) * 512]

        @block.sync
        def _(sy):
            for c in range(NCH):
                sy.dma_start(out=s_x[:, c * Tn:(c + 1) * Tn],
                             in_=x[c * 128:(c + 1) * 128, :]).then_inc(ld, 16)

        @block.scalar
        def _(s):
            s.wait_ge(ld, 16 * NCH)
            for tb in range(NB):
                if tb > 0:
                    s.wait_ge(te, 3 * (tb - 1) + 2)
                for c in range(NCH):
                    xs = s_x[:, c * Tn + tb * 512: c * Tn + (tb + 1) * 512]
                    s.activation(out=s_sq[:, c * 512:(c + 1) * 512], in_=xs,
                                 func=AF.Square).then_inc(sq, 1)
                s.wait_ge(ve, 3 * tb + 2)
                s.activation(out=SL(2), in_=SL(0), func=AF.Square)
                s.activation(out=s_spcs[:], in_=s_x[:, 0:512], func=AF.Square).then_inc(se, 1)
                s.wait_ge(ve, 3 * tb + 3)
                s.activation(out=SL(4), in_=SL(3), func=AF.Sqrt)
                s.activation(out=s_spcs[:], in_=s_x[:, 0:512], func=AF.Square).then_inc(se, 1)

        @block.tensor
        def _(t):
            t.wait_ge(ld, 16 * NCH)
            t.wait_ge(ve, 1)
            for tb in range(NB):
                if tb > 0:
                    t.wait_ge(ve, 3 * tb - 1)
                for c in range(NCH):
                    xs = s_x[:, c * Tn + tb * 512: c * Tn + (tb + 1) * 512]
                    mm = t.matmul(p_sum[:], s_one[:], xs, start=(c == 0), stop=(c == NCH - 1))
                mm.then_inc(te, 1)
                for c in range(NCH):
                    t.wait_ge(sq, NCH * tb + c + 1)
                    mm = t.matmul(p_sq[:], s_one[:], s_sq[:, c * 512:(c + 1) * 512],
                                  start=(c == 0), stop=(c == NCH - 1))
                mm.then_inc(te, 1)
                t.wait_ge(ve, 3 * tb + 4)
                if tb > 0:
                    t.wait_ge(apd, tb)
                t.matmul(p_r[:], s_onerow[:], SL(5), start=True, stop=True)
                t.matmul(p_mr[:], s_onerow[:], SL(6), start=True, stop=True).then_inc(te, 1)

        @block.vector
        def _(v):
            v.memset(s_one[:], 1.0)
            v.memset(s_onerow[:], 1.0)
            v.nop().then_inc(ve, 1)
            for tb in range(NB):
                v.wait_ge(te, 3 * tb + 2)
                v.tensor_scalar(out=SL(0), in0=p_sum[:], scalar1=1.0 / D, scalar2=None, op0=OP.mult)
                v.tensor_scalar(out=SL(1), in0=p_sq[:], scalar1=1.0 / D, scalar2=None, op0=OP.mult)
                v.tensor_copy(out=s_spcv[:], in_=s_x[:, 0:512]).then_inc(ve, 1)
                v.wait_ge(se, 2 * tb + 1)
                v.scalar_tensor_tensor(out=SL(3), in0=SL(1), scalar=LN_EPS, in1=SL(2),
                                       op0=OP.add, op1=OP.subtract)
                v.tensor_copy(out=s_spcv[:], in_=s_x[:, 0:512]).then_inc(ve, 1)
                v.wait_ge(se, 2 * tb + 2)
                v.reciprocal(out=SL(5), in_=SL(4))
                v.tensor_copy(out=s_spcv[:], in_=s_x[:, 0:512])
                v.tensor_tensor(out=SL(6), in0=SL(0), in1=SL(5), op=OP.mult)
                v.tensor_copy(out=s_spcv[:], in_=s_x[:, 0:512]).then_inc(ve, 1)
                v.wait_ge(te, 3 * tb + 3)
                if tb > 0:
                    v.wait_ge(st, 16 * KOUT * tb)
                for c in range(NCH):
                    xs = s_x[:, c * Tn + tb * 512: c * Tn + (tb + 1) * 512]
                    o = s_xnf[:, c * 512:(c + 1) * 512]
                    v.tensor_tensor(out=o, in0=xs, in1=p_r[:], op=OP.mult)
                    v.tensor_tensor(out=o, in0=o, in1=p_mr[:], op=OP.subtract)
                    v.tensor_copy(out=s_xnb[:, c * 512:(c + 1) * 512], in_=o)
                v.tensor_copy(out=s_spcv[:], in_=s_xnb[:, (NCH - 1) * 512:NCH * 512])
                v.tensor_copy(out=s_spcv[:], in_=s_xnf[:, (NCH - 1) * 512:NCH * 512]).then_inc(apd, 1)

        @block.gpsimd
        def _(g):
            for tb in range(NB):
                g.wait_ge(apd, tb + 1)
                if out_f32:
                    for c in range(NCH):
                        g.dma_start(out=xnf[c * 128:(c + 1) * 128, tb * 512:(tb + 1) * 512],
                                    in_=s_xnf[:, c * 512:(c + 1) * 512]).then_inc(st, 16)
                if xnb is not None:
                    for c in range(NCH):
                        g.dma_start(out=xnb[c * 128:(c + 1) * 128, tb * 512:(tb + 1) * 512],
                                    in_=s_xnb[:, c * 512:(c + 1) * 512]).then_inc(st, 16)
            g.wait_ge(st, 16 * NB * KOUT)
    return nc


def shard_x(x):
    out = []
    for c in range(NCORES):
        b, h = c // 2, c % 2
        out.append(np.ascontiguousarray(x[b, h * T:(h + 1) * T, :].T))
    return out


def build_l0_attn(debug=False, ctx=None, A=None):
    if ctx is None:
        nc = _mk("l0attn")
        xn = nc.dram_tensor("xnb_full", [D, S], BF16, kind="ExternalInput").ap()
        A = dict(xn_slice=lambda t0, w: xn[:, t0:t0 + w],
                 wqkvT=nc.dram_tensor("wqkvT", [D, 1536], F32, kind="ExternalInput").ap(),
                 bqkv=nc.dram_tensor("bqkv", [128, 8], F32, kind="ExternalInput").ap(),
                 bv_row=nc.dram_tensor("bv_row", [1, 512], F32, kind="ExternalInput").ap(),
                 postg=nc.dram_tensor("postg", [512, S], BF16, kind="ExternalOutput").ap())
    else:
        nc = ctx.nc
    xn_slice = A["xn_slice"]
    wqkvT, bqkv, bv_row, postg = A["wqkvT"], A["bqkv"], A["bv_row"], A["postg"]
    if debug:
        assert ctx is None
        dbg_q = nc.dram_tensor("dbg_q", [128, 4 * S], BF16, kind="ExternalOutput").ap()
        dbg_k = nc.dram_tensor("dbg_k", [128, 4 * S], BF16, kind="ExternalOutput").ap()
        dbg_v = nc.dram_tensor("dbg_v", [128, 32 * 512], BF16, kind="ExternalOutput").ap()
        dbg_negM = nc.dram_tensor("dbg_negM", [128, 1], F32, kind="ExternalOutput").ap()
        dbg_vt = nc.dram_tensor("dbg_vt", [128, 32 * 128], BF16, kind="ExternalOutput").ap()
        dbg_EA = nc.dram_tensor("dbg_EA", [128, 2 * S], BF16, kind="ExternalOutput").ap()
        dbg_nm = nc.dram_tensor("dbg_nm", [1, 64], F32, kind="ExternalOutput").ap()
        dbg_nm2 = nc.dram_tensor("dbg_nm2", [1, 16], F32, kind="ExternalOutput").ap()


    NT = S // 512
    NST = S // 128

    with ExitStack() as es:
        sb = lambda n, *a: es.enter_context(nc.sbuf_tensor(_u(n), *a))
        ps = lambda n, *a: es.enter_context(nc.psum_tensor(_u(n), *a))
        _semhs = []
        def sem(n):
            h = es.enter_context(nc.semaphore(_u(n)))
            _semhs.append(h)
            return h
        s_xnq = sb("s_xnq", [128, 2 * NCH * 512], BF16)
        s_xnv = sb("s_xnv", [128, 2 * NCH * 128], BF16)
        s_w = sb("s_w", [128, NCH * 1536], BF16)
        s_bqkv = sb("s_bqkv", [128, 8], F32)
        s_bvrow = sb("s_bvrow", [1, 512], F32)
        s_bvbc = sb("s_bvbc", [128, 512], F32)
        s_q = sb("s_q", [128, 4 * S], BF16)
        s_k = sb("s_k", [128, 4 * S], BF16)
        s_v = sb("s_v", [128, NST * 512], BF16)
        s_sqsc = sb("s_sqsc", [128, S], BF16)
        s_nm = sb("s_nm", [1, 64], F32)
        s_nm2 = sb("s_nm2", [1, 16], F32)
        s_negM = sb("s_negM", [128, 1], F32)
        s_onerow = sb("s_onerow", [1, 128], F32)
        s_one = sb("s_one", [128, 1], F32)
        s_oneb = sb("s_oneb", [128, 1], BF16)
        s_EA = sb("s_EA", [128, 2 * S], BF16)
        s_EB = sb("s_EB", [128, 2 * S], BF16)
        s_acc = sb("s_acc", [128, 32], F32)
        s_cs = sb("s_cs", [128, 32], F32)
        s_spc = sb("s_spc", [128, 512], F32)
        s_spc64 = sb("s_spc64", [128, 64], F32)
        s_vt = sb("s_vt", [128, NST * 128], BF16)
        s_pg = sb("s_pg", [128, 2 * 2048], BF16)
        ps8 = ps("ps8", [128, 4096], F32)
        bank = lambda i: ps8[:, i * 512:(i + 1) * 512]
        p_post = ps8[:, 2048:4096]

        ld = sem("ld"); gld = sem("gld"); prj = sem("prj"); prje = sem("prje")
        qlp = [sem("ql0"), sem("ql1")]
        vlp = [sem("vl0"), sem("vl1")]
        stp = [sem("sp0"), sem("sp1")]
        vev = sem("vev"); nrm = sem("nrm"); nrs = sem("nrs"); misc = sem("misc")
        sc = sem("sc"); xpA = sem("xpA"); xpB = sem("xpB")
        cs = sem("cs"); pv = sem("pv"); gl = sem("gl"); st = sem("st")
        if ctx is None:
            _clean_sems(nc, _semhs)
            block = es.enter_context(nc.Block())
        else:
            ctx.clean(_semhs)
            block = ctx.block

        def sgidx(p, r, stile, i):
            return p * 384 + (0 if r == 0 else 256) + stile * (8 if r == 0 else 4) + i

        def pvidx(p, r, stile):
            return p * 64 + r * 32 + stile

        @block.sync
        def _(sy):
            sy.dma_start(out=s_bqkv[:], in_=bqkv).then_inc(ld, 16)
            sy.dma_start(out=s_bvrow[:], in_=bv_row).then_inc(ld, 16)
            for tb in range(NT):
                if tb >= 2:
                    sy.wait_ge(prj, 2 + 8 * (tb - 1))
                par = tb % 2
                sy.dma_start(out=s_xnq[:].rearrange("p (b c t) -> p b c t", b=2, c=NCH)[:, par],
                             in_=xn_slice(tb * 512, 512).rearrange("(c p) t -> p c t", p=128)
                             ).then_inc(qlp[par], 16)
            for stile in range(NST):
                if stile >= 2:
                    sy.wait_ge(prj, 65 + stile)
                par = stile % 2
                sy.dma_start(out=s_xnv[:].rearrange("p (b c t) -> p b c t", b=2, c=NCH)[:, par],
                             in_=xn_slice(stile * 128, 128).rearrange("(c p) t -> p c t", p=128)
                             ).then_inc(vlp[par], 16)

        @block.gpsimd
        def _(g):
            for c in range(NCH):
                g.dma_start(out=s_w[:, c * 1536:(c + 1) * 1536],
                            in_=wqkvT[c * 128:(c + 1) * 128, :]).then_inc(gld, 16)
            for p in range(4):
                for r in range(2):
                    g.wait_ge(gl, p * 2 + r + 1)
                    pr2 = ((p * 2 + r) % 2) * 2048
                    g.dma_start(out=postg[p * 128:(p + 1) * 128, r * 2048:(r + 1) * 2048],
                                in_=s_pg[:, pr2:pr2 + 2048]).then_inc(stp[(p * 2 + r) % 2], 16)
            g.wait_ge(stp[0], 16 * 4)
            g.wait_ge(stp[1], 16 * 4)
            if debug:
                g.wait_ge(gl, 8)
                g.dma_start(out=dbg_q, in_=s_q[:]).then_inc(st, 16)
                g.dma_start(out=dbg_k, in_=s_k[:]).then_inc(st, 16)
                g.dma_start(out=dbg_v, in_=s_v[:]).then_inc(st, 16)
                g.dma_start(out=dbg_negM, in_=s_negM[:]).then_inc(st, 16)
                g.dma_start(out=dbg_vt, in_=s_vt[:]).then_inc(st, 16)
                g.dma_start(out=dbg_EA, in_=s_EA[:]).then_inc(st, 16)
                g.dma_start(out=dbg_nm, in_=s_nm[:]).then_inc(st, 16)
                g.dma_start(out=dbg_nm2, in_=s_nm2[:]).then_inc(st, 16)
                g.wait_ge(st, 16 * 8)

        @block.tensor
        def _(t):
            t.wait_ge(ld, 16 * 2)
            t.wait_ge(gld, 16 * NCH)
            t.wait_ge(misc, 1)
            t.matmul(bank(0), s_onerow[:], s_bvrow[:], start=True, stop=True).then_inc(prj, 1)
            for tb in range(NT):
                t.wait_ge(qlp[tb % 2], 16 * (tb // 2 + 1))
                par = tb % 2
                for j in range(8):
                    g2 = tb * 8 + j
                    if g2 == 0:
                        t.wait_ge(misc, 2)
                    if tb >= 1:
                        t.wait_ge(prje, (tb - 1) * 8 + j + 1)
                    for c in range(NCH):
                        mm = t.matmul(bank(j),
                                      s_w[:, c * 1536 + j * 128: c * 1536 + (j + 1) * 128],
                                      s_xnq[:, par * NCH * 512 + c * 512: par * NCH * 512 + (c + 1) * 512],
                                      start=(c == 0), stop=(c == NCH - 1))
                    mm.then_inc(prj, 1)
            for stile in range(NST):
                t.wait_ge(vlp[stile % 2], 16 * (stile // 2 + 1))
                par = stile % 2
                if stile < 2:
                    t.wait_ge(prje, 64)
                else:
                    t.wait_ge(vev, stile - 1)
                for c in range(NCH):
                    mm = t.matmul(bank(par),
                                  s_xnv[:, par * NCH * 128 + c * 128: par * NCH * 128 + (c + 1) * 128],
                                  s_w[:, c * 1536 + 1024: c * 1536 + 1536],
                                  start=(c == 0), stop=(c == NCH - 1))
                mm.then_inc(prj, 1)
            for pt in range(8):
                t.wait_ge(nrs, pt + 1)
                for tb in range(NT):
                    n = pt * NT + tb
                    if n >= 2:
                        t.wait_ge(nrm, n - 1)
                    t.matmul(bank(2 + n % 2)[0:1, :], s_oneb[:],
                             s_sqsc[:, tb * 512:(tb + 1) * 512],
                             start=True, stop=True).then_inc(prj, 1)
            t.wait_ge(nrm, 65)
            t.wait_ge(vev, NST)
            t.matmul(bank(0)[:, 0:1], s_onerow[:], s_nm2[:, 12:13], start=True, stop=True).then_inc(prj, 1)
            for p in range(4):
                for r in range(2):
                    blks = list(range(8)) if r == 0 else list(range(4, 8))
                    pvblks = list(range(4)) if r == 0 else list(range(4, 8))
                    for stile in range(NST + 1):
                        if stile < NST:
                            for i, blk in enumerate(blks):
                                g = sgidx(p, r, stile, i)
                                if g == 0:
                                    t.wait_ge(nrm, 66)
                                if g >= 2:
                                    t.wait_ge(xpA, g - 1)
                                    t.wait_ge(xpB, g - 1)
                                par = g % 2
                                t.matmul(bank(par), s_k[0:64, p * S + stile * 128: p * S + (stile + 1) * 128],
                                         s_q[0:64, p * S + blk * 512: p * S + (blk + 1) * 512],
                                         start=True, stop=True, tile_position=(0, 0))
                                t.matmul(bank(2 + par), s_k[64:128, p * S + stile * 128: p * S + (stile + 1) * 128],
                                         s_q[64:128, p * S + blk * 512: p * S + (blk + 1) * 512],
                                         start=True, stop=True, tile_position=(64, 0)).then_inc(sc, 1)
                        if stile >= 1:
                            st1 = stile - 1
                            epar = st1 % 2
                            for bi, blk in enumerate(pvblks):
                                if bi == 0:
                                    if r == 0:
                                        t.wait_ge(cs, p * 32 + st1 + 1)
                                    else:
                                        t.wait_ge(xpA, sgidx(p, 1, st1, 3) + 1)
                                        t.wait_ge(xpB, sgidx(p, 1, st1, 3) + 1)
                                    if st1 == 0:
                                        t.wait_ge(gl, p * 2 + r)
                                off = (blk - (0 if r == 0 else 4)) * 512
                                t.matmul(p_post[0:64, off:off + 512],
                                         s_vt[:, st1 * 128: st1 * 128 + 64],
                                         s_EA[:, epar * S + blk * 512: epar * S + (blk + 1) * 512],
                                         start=(st1 == 0), stop=(st1 == NST - 1),
                                         tile_position=(0, 0))
                                mm = t.matmul(p_post[64:128, off:off + 512],
                                              s_vt[:, st1 * 128 + 64: st1 * 128 + 128],
                                              s_EB[:, epar * S + blk * 512: epar * S + (blk + 1) * 512],
                                              start=(st1 == 0), stop=(st1 == NST - 1),
                                              tile_position=(0, 64))
                            mm.then_inc(pv, 1)

        @block.scalar
        def _(s):
            for tb in range(NT):
                for j in range(8):
                    g2 = tb * 8 + j
                    s.wait_ge(prj, g2 + 2)
                    dst = s_q if j < 4 else s_k
                    pr = j % 4
                    s.activation(out=dst[:, pr * S + tb * 512: pr * S + (tb + 1) * 512],
                                 in_=bank(j), func=AF.Identity,
                                 bias=s_bqkv[:, j:j + 1]).then_inc(prje, 1)
            for pt in range(8):
                if pt >= 1:
                    s.wait_ge(prj, 97 + 8 * pt)
                src = s_q if pt < 4 else s_k
                pr = pt % 4
                s.activation(out=s_sqsc[:], in_=src[:, pr * S:(pr + 1) * S],
                             func=AF.Square).then_inc(nrs, 1)
            for p in range(4):
                for r in range(2):
                    blks = list(range(8)) if r == 0 else list(range(4, 8))
                    for stile in range(NST):
                        epar = stile % 2
                        apar = epar * 16
                        for i, blk in enumerate(blks):
                            g = sgidx(p, r, stile, i)
                            if i == 0:
                                gate = pvidx(p, r, stile) - 1
                                if gate > 0:
                                    s.wait_ge(pv, gate)
                            s.wait_ge(sc, g + 1)
                            par = g % 2
                            if r == 0:
                                s.activation(out=s_EA[:, epar * S + blk * 512: epar * S + (blk + 1) * 512],
                                             in_=bank(par), func=AF.Exp, bias=s_negM[:], scale=0.125,
                                             accum_out=s_acc[:, apar + blk:apar + blk + 1]).then_inc(xpA, 1)
                                s.activation(out=s_EB[:, epar * S + blk * 512: epar * S + (blk + 1) * 512],
                                             in_=bank(2 + par), func=AF.Exp, bias=s_negM[:], scale=0.125,
                                             accum_out=s_acc[:, apar + 8 + blk:apar + 9 + blk]).then_inc(xpB, 1)
                            else:
                                s.activation(out=s_EA[:, epar * S + blk * 512: epar * S + (blk + 1) * 512],
                                             in_=bank(par), func=AF.Exp, bias=s_negM[:],
                                             scale=0.125).then_inc(xpA, 1)
                                s.activation(out=s_EB[:, epar * S + blk * 512: epar * S + (blk + 1) * 512],
                                             in_=bank(2 + par), func=AF.Exp, bias=s_negM[:],
                                             scale=0.125).then_inc(xpB, 1)
                    s.wait_ge(pv, pvidx(p, r, NST - 1) + 1)
                    if p * 2 + r >= 2:
                        s.wait_ge(stp[(p * 2 + r) % 2], 16 * ((p * 2 + r - 2) // 2 + 1))
                    pr2 = ((p * 2 + r) % 2) * 2048
                    for q4 in range(4):
                        s.activation(out=s_pg[:, pr2 + q4 * 512:pr2 + (q4 + 1) * 512],
                                     in_=p_post[:, q4 * 512:(q4 + 1) * 512], func=AF.Gelu)
                    s.activation(out=s_sqsc[:, 0:512], in_=s_pg[:, pr2 + 1536:pr2 + 2048],
                                 func=AF.Identity, bias=0.0).then_inc(gl, 1)

        @block.vector
        def _(v):
            v.memset(s_onerow[:], 1.0)
            v.memset(s_one[:], 1.0)
            v.memset(s_oneb[:], 1.0)
            v.nop().then_inc(misc, 1)
            v.wait_ge(prj, 1)
            v.tensor_copy(out=s_bvbc[:], in_=bank(0)).then_inc(misc, 1)
            for stile in range(NST):
                v.wait_ge(prj, 66 + stile)
                v.tensor_tensor(out=s_v[:, stile * 512:(stile + 1) * 512],
                                in0=bank(stile % 2), in1=s_bvbc[:], op=OP.add).then_inc(vev, 1)
            for n in range(64):
                v.wait_ge(prj, 98 + n)
                v.reduce_max(out=s_nm[:, n:n + 1], in_=bank(2 + n % 2)[0:1, :],
                             axis=AX.X).then_inc(nrm, 1)
            v.tensor_copy(out=s_spc[:], in_=s_bvbc[:])
            v.reduce_max(out=s_nm2[:, 0:1], in_=s_nm[:, 0:32], axis=AX.X)
            v.reduce_max(out=s_nm2[:, 4:5], in_=s_nm[:, 32:64], axis=AX.X)
            v.tensor_copy(out=s_spc[:], in_=s_bvbc[:])
            v.tensor_tensor(out=s_nm2[:, 8:9], in0=s_nm2[:, 0:1], in1=s_nm2[:, 4:5], op=OP.add)
            v.tensor_copy(out=s_spc[:], in_=s_bvbc[:])
            v.tensor_scalar(out=s_nm2[:, 12:13], in0=s_nm2[:, 8:9], scalar1=-1.0 / 16.0,
                            scalar2=None, op0=OP.mult)
            v.tensor_copy(out=s_spc[:], in_=s_bvbc[:]).then_inc(nrm, 1)
            v.wait_ge(prj, 162)
            v.tensor_copy(out=s_negM[:], in_=bank(0)[:, 0:1])
            v.tensor_copy(out=s_spc64[:], in_=s_bvbc[:, 0:64]).then_inc(nrm, 1)
            for p in range(4):
                for stile in range(NST + 1):
                    pr = (stile % 2) * 16
                    pr1 = ((stile - 1) % 2) * 16
                    if stile >= 1:
                        v.reciprocal(out=s_cs[:, pr1 + 8:pr1 + 9], in_=s_cs[:, pr1 + 0:pr1 + 1])
                        v.reciprocal(out=s_cs[:, pr1 + 12:pr1 + 13], in_=s_cs[:, pr1 + 4:pr1 + 5])
                    if stile < NST:
                        v.wait_ge(xpA, p * 384 + stile * 8 + 8)
                        v.wait_ge(xpB, p * 384 + stile * 8 + 8)
                        v.tensor_copy(out=s_spc64[:], in_=s_bvbc[:, 0:64])
                        v.reduce_sum(out=s_cs[:, pr + 0:pr + 1], in_=s_acc[:, pr + 0:pr + 8], axis=AX.X)
                        v.reduce_sum(out=s_cs[:, pr + 4:pr + 5], in_=s_acc[:, pr + 8:pr + 16], axis=AX.X)
                    if stile >= 1:
                        st1 = stile - 1
                        v.tensor_scalar(out=s_vt[:, st1 * 128: st1 * 128 + 64],
                                        in0=s_v[:, st1 * 512 + p * 128: st1 * 512 + p * 128 + 64],
                                        scalar1=s_cs[:, pr1 + 8:pr1 + 9], scalar2=None, op0=OP.mult)
                        v.tensor_scalar(out=s_vt[:, st1 * 128 + 64: st1 * 128 + 128],
                                        in0=s_v[:, st1 * 512 + p * 128 + 64: st1 * 512 + p * 128 + 128],
                                        scalar1=s_cs[:, pr1 + 12:pr1 + 13], scalar2=None, op0=OP.mult)
                        v.tensor_copy(out=s_spc64[:], in_=s_vt[:, st1 * 128: st1 * 128 + 64]).then_inc(cs, 1)
    return nc


def host_l0_attn_inputs(inputs, xnb_by_core):
    W = np.asarray(inputs["L0_attn_in_proj_w"])
    bqkv_full = np.asarray(inputs["L0_attn_in_proj_b"])
    maps = []
    for c in range(NCORES):
        b, hh = c // 2, c % 2
        o = 512 * hh
        Wq = W[o:o + 512, :]
        Wk = W[1024 + o:1024 + o + 512, :]
        Wv = W[2048 + o:2048 + o + 512, :]
        wqkvT = np.ascontiguousarray(np.concatenate([Wq, Wk, Wv], axis=0).T)
        bq = bqkv_full[o:o + 512]
        bk = bqkv_full[1024 + o:1024 + o + 512]
        bv = bqkv_full[2048 + o:2048 + o + 512]
        bqkv = np.ascontiguousarray(np.concatenate([bq, bk]).reshape(8, 128).T.astype(np.float32))
        xnb_full = np.concatenate([xnb_by_core[2 * b], xnb_by_core[2 * b + 1]], axis=1)
        maps.append(dict(xnb_full=np.ascontiguousarray(xnb_full), wqkvT=wqkvT,
                         bqkv=bqkv,
                         bv_row=np.ascontiguousarray(bv.reshape(1, 512).astype(np.float32))))
    return maps


def build_gemm(name, Kd, Nout, Tn, act="identity", scale=1.0, out_f32=False,
               nbias=None, w_bf16=False, ctx=None, A=None):
    NO = Nout // 128
    NK = Kd // 128
    NTB = Tn // 512
    use_bias = nbias is not None
    if ctx is None:
        nc = _mk(name)
        out = nc.dram_tensor("out", [Nout, Tn], BF16, kind="ExternalOutput").ap()
        outf = nc.dram_tensor("outf", [Nout, Tn], F32, kind="ExternalOutput").ap() if out_f32 else None
        A = dict(wT=nc.dram_tensor("wT", [Kd, Nout], BF16 if w_bf16 else F32, kind="ExternalInput").ap(),
                 xb=nc.dram_tensor("xb", [Kd, Tn], BF16, kind="ExternalInput").ap(),
                 bias=nc.dram_tensor("bias", [128, NO], F32, kind="ExternalInput").ap() if use_bias else None,
                 out_slice=lambda o, tb: out[o * 128:(o + 1) * 128, tb * 512:(tb + 1) * 512],
                 outf_slice=(lambda o, tb: outf[o * 128:(o + 1) * 128, tb * 512:(tb + 1) * 512]) if out_f32 else None)
    else:
        nc = ctx.nc
    wT, xb, bias = A.get("wT"), A["xb"], A.get("bias")
    wTt = A.get("wTt")
    out_slice, outf_slice = A.get("out_slice"), A.get("outf_slice")
    fuse_add = A.get("fuse_add")
    fuse_out = A.get("fuse_out")
    no_bf16 = A.get("no_bf16", False) or fuse_add is not None
    fn = {"identity": AF.Identity, "gelu": AF.Gelu}[act]

    with ExitStack() as es:
        if ctx is not None and ctx.ges is not None:
            es = ctx.ges
        sb = lambda n, *a: es.enter_context(nc.sbuf_tensor(_u(n), *a))
        ps = lambda n, *a: es.enter_context(nc.psum_tensor(_u(n), *a))
        _semhs = []
        def sem(n):
            h = es.enter_context(nc.semaphore(_u(n)))
            _semhs.append(h)
            return h
        s_x = sb("s_x", [128, NK * Tn], BF16)
        s_w = sb("s_w", [128, 2 * NK * 128], BF16)
        s_b = sb("s_b", [128, NO], F32) if use_bias else None
        s_o = sb("s_o", [128, 2 * NTB * 512], BF16) if not no_bf16 else None
        s_fl = sb("s_fl", [128, 512], F32)
        s_of = sb("s_of", [128, 2 * NTB * 512], F32) if out_f32 else None
        if fuse_add is not None:
            s_a1 = sb("s_a1", [128, 2 * 512], F32)
            s_a2 = sb("s_a2", [128, 2 * 512], F32)
            s_f1 = sb("s_f1", [128, 2 * 512], F32)
            s_f2 = sb("s_f2", [128, 2 * 512], F32)
        ps8 = ps("ps8", [128, 2048], F32)
        bank = lambda i: ps8[:, i * 512:(i + 1) * 512]
        ld = sem("ld"); te = sem("te"); ev = sem("ev")
        gwp = [sem("gw0"), sem("gw1")]
        stp = [sem("st0"), sem("st1")]
        if "export" in A:
            A["export"].update(stp=stp, ev=ev, NTB=NTB, NSPT_=None)
        if fuse_add is not None:
            alp = [sem("al0"), sem("al1")]
            av = sem("av")
        NSPT = (0 if no_bf16 else 1) + (1 if out_f32 and fuse_add is None else 0)
        if ctx is None:
            _clean_sems(nc, _semhs)
            block = es.enter_context(nc.Block())
        else:
            ctx.clean(_semhs)
            block = ctx.block
        KOUT = (2 if out_f32 else 1)

        @block.sync
        def _(sy):
            if use_bias:
                sy.dma_start(out=s_b[:], in_=bias).then_inc(ld, 16)
            for c in range(NK):
                sy.dma_start(out=s_x[:, c * Tn:(c + 1) * Tn],
                             in_=xb[c * 128:(c + 1) * 128, :]).then_inc(ld, 16)

        @block.gpsimd
        def _(g):
            for o in range(NO):
                if o >= 2:
                    g.wait_ge(te, (o - 1) * NTB)
                src = (wTt[o].rearrange("p (c f) -> p c f", c=NK) if wTt is not None
                       else wT[:, o * 128:(o + 1) * 128].rearrange("(c p) f -> p c f", p=128))
                g.dma_start(out=s_w[:].rearrange("p (b c f) -> p b c f", b=2, c=NK)[:, o % 2],
                            in_=src).then_inc(gwp[o % 2], 16)

        @block.tensor
        def _(t):
            t.wait_ge(ld, 16 * (NK + (1 if use_bias else 0)))
            for o in range(NO):
                t.wait_ge(gwp[o % 2], 16 * (o // 2 + 1))
                wv = s_w[:].rearrange("p (b c f) -> p b c f", b=2, c=NK)[:, o % 2]
                for tb in range(NTB):
                    gidx = o * NTB + tb
                    if gidx >= 4:
                        t.wait_ge(ev, gidx - 3)
                    for c in range(NK):
                        mm = t.matmul(bank(gidx % 4), wv[:, c],
                                      s_x[:, c * Tn + tb * 512: c * Tn + (tb + 1) * 512],
                                      start=(c == 0), stop=(c == NK - 1))
                    mm.then_inc(te, 1)

        @block.scalar
        def _(s):
            for o in range(NO):
                for tb in range(NTB):
                    gidx = o * NTB + tb
                    s.wait_ge(te, gidx + 1)
                    if o >= 2:
                        if fuse_add is not None:
                            s.wait_ge(av, (o - 1) * NTB)
                        else:
                            s.wait_ge(stp[o % 2], 16 * NSPT * NTB * ((o - 2) // 2 + 1))
                    kw = dict(func=fn, scale=scale)
                    if use_bias:
                        kw["bias"] = s_b[:, o:o + 1]
                    outs = []
                    if not no_bf16:
                        outs.append(s_o[:, (o % 2) * NTB * 512 + tb * 512:(o % 2) * NTB * 512 + (tb + 1) * 512])
                    if out_f32:
                        outs.append(s_of[:, (o % 2) * NTB * 512 + tb * 512:(o % 2) * NTB * 512 + (tb + 1) * 512])
                    for dst_ in outs:
                        s.activation(out=dst_, in_=bank(gidx % 4), **kw)
                    s.activation(out=s_fl[:], in_=outs[0], func=AF.Identity, bias=0.0)
                    s.activation(out=s_fl[:], in_=outs[-1], func=AF.Identity,
                                 bias=0.0).then_inc(ev, 1)

        if fuse_add is None:
            @block.sync
            def _(sy):
                for o in range(NO):
                    sy.wait_ge(ev, o * NTB + NTB)
                    for tb in range(NTB):
                        if not no_bf16:
                            src = s_o[:, (o % 2) * NTB * 512 + tb * 512:(o % 2) * NTB * 512 + (tb + 1) * 512]
                            sy.dma_start(out=out_slice(o, tb), in_=src).then_inc(stp[o % 2], 16)
                        if out_f32:
                            srcf = s_of[:, (o % 2) * NTB * 512 + tb * 512:(o % 2) * NTB * 512 + (tb + 1) * 512]
                            sy.dma_start(out=outf_slice(o, tb), in_=srcf).then_inc(stp[o % 2], 16)
                sy.wait_ge(stp[0], 16 * NSPT * NTB * ((NO + 1) // 2))
                sy.wait_ge(stp[1], 16 * NSPT * NTB * (NO // 2))
        else:
            a1, a2 = fuse_add
            NG = NO * NTB
            s_flv = sb("s_flv", [128, 512], F32)

            @block.sync
            def _(sy):
                for g in range(NG + 1):
                    if g < NG:
                        o, tb = g // NTB, g % NTB
                        if g >= 2:
                            sy.wait_ge(av, g - 1)
                        for src_d, dst_s in ((a1, s_a1), (a2, s_a2)):
                            sy.dma_start(out=dst_s[:, (g % 2) * 512:(g % 2 + 1) * 512],
                                         in_=src_d[o * 128:(o + 1) * 128, tb * 512:(tb + 1) * 512]
                                         ).then_inc(alp[g % 2], 16)
                    if g >= 1:
                        j = g - 1
                        o, tb = j // NTB, j % NTB
                        sy.wait_ge(av, j + 1)
                        sy.dma_start(out=fuse_out[o * 128:(o + 1) * 128, tb * 512:(tb + 1) * 512],
                                     in_=s_f2[:, (j % 2) * 512:(j % 2 + 1) * 512]
                                     ).then_inc(stp[j % 2], 16)
                sy.wait_ge(stp[0], 16 * ((NG + 1) // 2))
                sy.wait_ge(stp[1], 16 * (NG // 2))

            @block.vector
            def _(v):
                for g in range(NG):
                    o, tb = g // NTB, g % NTB
                    v.wait_ge(ev, g + 1)
                    v.wait_ge(alp[g % 2], 32 * (g // 2 + 1))
                    if g >= 2:
                        v.wait_ge(stp[g % 2], 16 * (g // 2))
                    oft = s_of[:, (o % 2) * NTB * 512 + tb * 512:(o % 2) * NTB * 512 + (tb + 1) * 512]
                    a1t = s_a1[:, (g % 2) * 512:(g % 2 + 1) * 512]
                    a2t = s_a2[:, (g % 2) * 512:(g % 2 + 1) * 512]
                    f1t = s_f1[:, (g % 2) * 512:(g % 2 + 1) * 512]
                    f2t = s_f2[:, (g % 2) * 512:(g % 2 + 1) * 512]
                    v.tensor_tensor(out=f1t, in0=oft, in1=a1t, op=OP.add)
                    v.tensor_tensor(out=f2t, in0=f1t, in1=a2t, op=OP.add)
                    v.tensor_copy(out=s_flv[:], in_=f2t)
                    v.tensor_copy(out=s_flv[:, 0:64], in_=f2t[:, 448:512]).then_inc(av, 1)
    return nc


def build_addln(name, n_in, Tn=T, gb=False, out_sum=False, out_f32=True,
                dtypes=("f32", "f32", "f32"), ctx=None, A=None):
    if ctx is None:
        nc = _mk(name)
        A = dict(xs=[nc.dram_tensor(f"x{i}", [D, Tn], F32 if dtypes[i] == "f32" else BF16,
                                    kind="ExternalInput").ap() for i in range(n_in)],
                 params=nc.dram_tensor("params", [128, 16], F32, kind="ExternalInput").ap() if gb else None,
                 xnf=nc.dram_tensor("xnf", [D, Tn], F32, kind="ExternalOutput").ap() if out_f32 else None,
                 xnb=nc.dram_tensor("xnb", [D, Tn], BF16, kind="ExternalOutput").ap(),
                 sumf=nc.dram_tensor("sumf", [D, Tn], F32, kind="ExternalOutput").ap() if out_sum else None)
    else:
        nc = ctx.nc
    xs, params = A["xs"], A.get("params")
    xnf, xnb, sumf = A.get("xnf"), A.get("xnb"), A.get("sumf")
    NB = Tn // 512
    KTB = NCH * ((1 if out_f32 else 0) + (1 if xnb is not None else 0))

    with ExitStack() as es:
        sb = lambda n, *a: es.enter_context(nc.sbuf_tensor(_u(n), *a))
        ps = lambda n, *a: es.enter_context(nc.psum_tensor(_u(n), *a))
        _semhs = []
        def sem(n):
            h = es.enter_context(nc.semaphore(_u(n)))
            _semhs.append(h)
            return h
        s_x = sb("s_x", [128, NCH * Tn], F32)
        s_sq = sb("s_sq", [128, NCH * 512], F32)
        s_stat = sb("s_stat", [1, 7 * 512], F32)
        s_xnf = sb("s_xnf", [128, NCH * 512], F32)
        s_xnb = sb("s_xnb", [128, NCH * 512], BF16) if xnb is not None else None
        s_par = sb("s_par", [128, 16], F32) if gb else None
        s_one = sb("s_one", [128, 1], F32)
        s_onerow = sb("s_onerow", [1, 128], F32)
        s_spcv = sb("s_spcv", [128, 512], F32)
        s_spcs = sb("s_spcs", [128, 512], F32)
        p_sum = ps("p_sum", [1, 512], F32)
        p_sq = ps("p_sq", [1, 512], F32)
        p_r = ps("p_r", [128, 512], F32)
        p_mr = ps("p_mr", [128, 512], F32)
        ld = sem("ld"); sq = sem("sq"); te = sem("te"); ve = sem("ve")
        se = sem("se"); st = sem("st"); apd = sem("apd"); sm = sem("sm")
        if ctx is None:
            _clean_sems(nc, _semhs)
            block = es.enter_context(nc.Block())
        else:
            ctx.clean(_semhs)
            block = ctx.block
        SL = lambda i: s_stat[:, i * 512:(i + 1) * 512]
        nld = n_in * NCH + (1 if gb else 0)

        @block.sync
        def _(sy):
            if gb:
                sy.dma_start(out=s_par[:], in_=params).then_inc(ld, 16)

        @block.gpsimd
        def _(g):
            cnt = 0
            for c in range(NCH):
                for i in range(n_in):
                    kw = {} if i == 0 else dict(accum_op=OP.add)
                    g.dma_start(out=s_x[:, c * Tn:(c + 1) * Tn],
                                in_=xs[i][c * 128:(c + 1) * 128, :], **kw).then_inc(sm, 16)
                    cnt += 16
                    g.wait_ge(sm, cnt)

        @block.vector
        def _(v):
            v.memset(s_one[:], 1.0)
            v.memset(s_onerow[:], 1.0)
            if gb:
                v.wait_ge(ld, 16)
            v.wait_ge(sm, 16 * n_in * NCH)
            v.nop().then_inc(ve, 1)
            for tb in range(NB):
                vb = 1 + 3 * tb
                v.wait_ge(te, 3 * tb + 2)
                v.tensor_scalar(out=SL(0), in0=p_sum[:], scalar1=1.0 / D, scalar2=None, op0=OP.mult)
                v.tensor_scalar(out=SL(1), in0=p_sq[:], scalar1=1.0 / D, scalar2=None, op0=OP.mult)
                v.tensor_copy(out=s_spcv[:], in_=s_x[:, 0:512]).then_inc(ve, 1)
                v.wait_ge(se, 2 * tb + 1)
                v.scalar_tensor_tensor(out=SL(3), in0=SL(1), scalar=LN_EPS, in1=SL(2),
                                       op0=OP.add, op1=OP.subtract)
                v.tensor_copy(out=s_spcv[:], in_=s_x[:, 0:512]).then_inc(ve, 1)
                v.wait_ge(se, 2 * tb + 2)
                v.reciprocal(out=SL(5), in_=SL(4))
                v.tensor_copy(out=s_spcv[:], in_=s_x[:, 0:512])
                v.tensor_tensor(out=SL(6), in0=SL(0), in1=SL(5), op=OP.mult)
                v.tensor_copy(out=s_spcv[:], in_=s_x[:, 0:512]).then_inc(ve, 1)
                v.wait_ge(te, 3 * tb + 3)
                if tb > 0:
                    v.wait_ge(st, 16 * KTB * tb)
                for c in range(NCH):
                    xsl = s_x[:, c * Tn + tb * 512: c * Tn + (tb + 1) * 512]
                    o = s_xnf[:, c * 512:(c + 1) * 512]
                    v.tensor_tensor(out=o, in0=xsl, in1=p_r[:], op=OP.mult)
                    v.tensor_tensor(out=o, in0=o, in1=p_mr[:], op=OP.subtract)
                    if gb:
                        v.tensor_scalar(out=o, in0=o, scalar1=s_par[:, c:c + 1],
                                        scalar2=s_par[:, 8 + c:9 + c], op0=OP.mult, op1=OP.add)
                    if xnb is not None:
                        v.tensor_copy(out=s_xnb[:, c * 512:(c + 1) * 512], in_=o)
                if xnb is not None:
                    v.tensor_copy(out=s_spcv[:], in_=s_xnb[:, (NCH - 1) * 512:NCH * 512])
                v.tensor_copy(out=s_spcv[:], in_=s_xnf[:, (NCH - 1) * 512:NCH * 512]).then_inc(apd, 1)

        @block.scalar
        def _(s):
            s.wait_ge(ve, 1)
            for tb in range(NB):
                if tb > 0:
                    s.wait_ge(te, 3 * (tb - 1) + 2)
                for c in range(NCH):
                    xsl = s_x[:, c * Tn + tb * 512: c * Tn + (tb + 1) * 512]
                    s.activation(out=s_sq[:, c * 512:(c + 1) * 512], in_=xsl,
                                 func=AF.Square).then_inc(sq, 1)
                s.wait_ge(ve, 1 + 3 * tb + 1)
                s.activation(out=SL(2), in_=SL(0), func=AF.Square)
                s.activation(out=s_spcs[:], in_=s_x[:, 0:512], func=AF.Square).then_inc(se, 1)
                s.wait_ge(ve, 1 + 3 * tb + 2)
                s.activation(out=SL(4), in_=SL(3), func=AF.Sqrt)
                s.activation(out=s_spcs[:], in_=s_x[:, 0:512], func=AF.Square).then_inc(se, 1)

        @block.tensor
        def _(t):
            t.wait_ge(ve, 1)
            for tb in range(NB):
                if tb > 0:
                    t.wait_ge(ve, 1 + 3 * tb - 1)
                for c in range(NCH):
                    xsl = s_x[:, c * Tn + tb * 512: c * Tn + (tb + 1) * 512]
                    mm = t.matmul(p_sum[:], s_one[:], xsl, start=(c == 0), stop=(c == NCH - 1))
                mm.then_inc(te, 1)
                for c in range(NCH):
                    t.wait_ge(sq, NCH * tb + c + 1)
                    mm = t.matmul(p_sq[:], s_one[:], s_sq[:, c * 512:(c + 1) * 512],
                                  start=(c == 0), stop=(c == NCH - 1))
                mm.then_inc(te, 1)
                t.wait_ge(ve, 1 + 3 * tb + 3)
                if tb > 0:
                    t.wait_ge(apd, tb)
                t.matmul(p_r[:], s_onerow[:], SL(5), start=True, stop=True)
                t.matmul(p_mr[:], s_onerow[:], SL(6), start=True, stop=True).then_inc(te, 1)

        @block.gpsimd
        def _(g):
            for tb in range(NB):
                g.wait_ge(apd, tb + 1)
                if out_f32:
                    for c in range(NCH):
                        g.dma_start(out=xnf[c * 128:(c + 1) * 128, tb * 512:(tb + 1) * 512],
                                    in_=s_xnf[:, c * 512:(c + 1) * 512]).then_inc(st, 16)
                if xnb is not None:
                    for c in range(NCH):
                        g.dma_start(out=xnb[c * 128:(c + 1) * 128, tb * 512:(tb + 1) * 512],
                                    in_=s_xnb[:, c * 512:(c + 1) * 512]).then_inc(st, 16)
            if out_sum:
                g.wait_ge(ve, 1)
                for c in range(NCH):
                    g.dma_start(out=sumf[c * 128:(c + 1) * 128, :],
                                in_=s_x[:, c * Tn:(c + 1) * Tn]).then_inc(st, 16)
            g.wait_ge(st, 16 * (NB * KTB + (NCH if out_sum else 0)))
    return nc


def build_add(name, n_in, Tn=T, dtypes=("f32", "f32", "f32"), ctx=None, A=None):
    if ctx is None:
        nc = _mk(name)
        A = dict(xs=[nc.dram_tensor(f"x{i}", [D, Tn], F32 if dtypes[i] == "f32" else BF16,
                                    kind="ExternalInput").ap() for i in range(n_in)],
                 y=nc.dram_tensor("y", [D, Tn], F32, kind="ExternalOutput").ap())
    else:
        nc = ctx.nc
    xs, y = A["xs"], A["y"]
    with ExitStack() as es:
        sb = lambda *a: es.enter_context(nc.sbuf_tensor(*a))
        _semhs = []
        def sem(n):
            h = es.enter_context(nc.semaphore(_u(n)))
            _semhs.append(h)
            return h
        s_y = sb("s_y", [128, NCH * Tn], F32)
        sm = sem("sm"); st = sem("st")
        if ctx is None:
            _clean_sems(nc, _semhs)
            block = es.enter_context(nc.Block())
        else:
            ctx.clean(_semhs)
            block = ctx.block

        @block.gpsimd
        def _(g):
            cnt = 0
            for c in range(NCH):
                for i in range(n_in):
                    kw = {} if i == 0 else dict(accum_op=OP.add)
                    g.dma_start(out=s_y[:, c * Tn:(c + 1) * Tn],
                                in_=xs[i][c * 128:(c + 1) * 128, :], **kw).then_inc(sm, 16)
                    cnt += 16
                    g.wait_ge(sm, cnt)
                g.dma_start(out=y[c * 128:(c + 1) * 128, :],
                            in_=s_y[:, c * Tn:(c + 1) * Tn]).then_inc(st, 16)
            g.wait_ge(st, 16 * NCH)
    return nc


def build_moe(debug=False, ctx=None, A=None):
    if ctx is None:
        nc = _mk("moe")
        A = dict(xnb=nc.dram_tensor("xnb", [D, T], BF16, kind="ExternalInput").ap(),
                 xnf=nc.dram_tensor("xnf", [D, T], F32, kind="ExternalInput").ap(),
                 sw=nc.dram_tensor("sw", [D, 8], F32, kind="ExternalInput").ap(),
                 swb=nc.dram_tensor("swb", [128, 8], F32, kind="ExternalInput").ap(),
                 w8t=nc.dram_tensor("w8t", [128, 128], F32, kind="ExternalInput").ap(),
                 ident=nc.dram_tensor("ident", [128, 128], F32, kind="ExternalInput").ap(),
                 sel=nc.dram_tensor("sel", [8, 1024], F32, kind="ExternalInput").ap(),
                 w1=nc.dram_tensor("w1", [8, 32, 128, 1024], F32, kind="ExternalInput").ap(),
                 b1c=nc.dram_tensor("b1c", [128, 256], F32, kind="ExternalInput").ap(),
                 w2=nc.dram_tensor("w2", [8, 8, 128, 4096], F32, kind="ExternalInput").ap(),
                 b2c=nc.dram_tensor("b2c", [128, 64], F32, kind="ExternalInput").ap(),
                 moef=nc.dram_tensor("moef", [D, T], BF16, kind="ExternalOutput").ap())
    else:
        nc = ctx.nc
    xnb, xnf, sw, swb, w8t, ident, sel = (A["xnb"], A["xnf"], A["sw"], A["swb"],
                                          A["w8t"], A["ident"], A["sel"])
    w1, b1c, w2, b2c, moef = A["w1"], A["b1c"], A["w2"], A["b2c"], A["moef"]
    if debug:
        assert ctx is None
        dbg_lg = nc.dram_tensor("dbg_lg", [128, 128], F32, kind="ExternalOutput").ap()
        dbg_g8t = nc.dram_tensor("dbg_g8t", [128, 128], F32, kind="ExternalOutput").ap()
        dbg_g8 = nc.dram_tensor("dbg_g8", [8, 2048], F32, kind="ExternalOutput").ap()
        dbg_rp = nc.dram_tensor("dbg_rp", [128, 16], F32, kind="ExternalOutput").ap()
        dbg_lm = nc.dram_tensor("dbg_lm", [128, 16], F32, kind="ExternalOutput").ap()
        dbg_h = nc.dram_tensor("dbg_h", [128, 32 * 1024], BF16, kind="ExternalOutput").ap()
        dbg_esum = nc.dram_tensor("dbg_esum", [128, 16], F32, kind="ExternalOutput").ap()

    NTT = T // 128
    NF = 32
    NE = 8
    TE_RT = 16

    def te_h_base(e, H):
        return TE_RT + 84 * e + 4 + 40 * H
    def te_y_base(e, H):
        return te_h_base(e, H) + 32
    def gh_base(e, H):
        return 64 * (2 * e + H)
    def ev_base(e, H):
        return 32 * e + 16 * H

    with ExitStack() as es:
        sb = lambda n, *a: es.enter_context(nc.sbuf_tensor(_u(n), *a))
        ps = lambda n, *a: es.enter_context(nc.psum_tensor(_u(n), *a))
        _semhs = []
        def sem(n):
            h = es.enter_context(nc.semaphore(_u(n)))
            _semhs.append(h)
            return h
        s_moe = sb("s_moe", [128, NCH * T], BF16)
        s_h = sb("s_h", [128, NF * 1024], BF16)
        s_xnh = sb("s_xnh", [128, 2 * NCH * 1024], BF16)
        s_xr = sb("s_xr", [128, 2 * NCH * 128], F32)
        s_w1b = sb("s_w1b", [128, 2 * NCH * 128], BF16)
        s_w2b = sb("s_w2b", [128, 2 * 32 * 128], BF16)
        s_G = sb("s_G", [128, T], F32)
        s_sw = sb("s_sw", [128, NCH * 8], F32)
        s_swb = sb("s_swb", [128, 8], F32)
        s_w8t = sb("s_w8t", [128, 128], F32)
        s_id = sb("s_id", [128, 128], F32)
        s_sel = sb("s_sel", [8, 1024], F32)
        s_b1 = sb("s_b1", [128, 256], F32)
        s_b2 = sb("s_b2", [128, 64], F32)
        s_lg = sb("s_lg", [128, 128], F32)
        s_ex = sb("s_ex", [128, 128], F32)
        s_lm = sb("s_lm", [128, 16], F32)
        s_nlm = sb("s_nlm", [128, 16], F32)
        s_esum = sb("s_esum", [128, 16], F32)
        s_rp = sb("s_rp", [128, 16], F32)
        s_sc = sb("s_sc", [128, 128], F32)
        s_m2 = sb("s_m2", [128, 16], F32)
        s_mask = sb("s_mask", [128, 128], F32)
        s_g8t = sb("s_g8t", [128, 128], F32)
        s_g8 = sb("s_g8", [8, 2048], F32)
        s_tmp = sb("s_tmp", [128, 512], F32)
        s_spc = sb("s_spc", [128, 128], F32)
        ps8 = ps("ps8", [128, 4096], F32)
        bank = lambda i: ps8[:, i * 512:(i + 1) * 512]

        ld = sem("ld")
        xlp = [sem("xl0"), sem("xl1")]
        rlp = [sem("rl0"), sem("rl1")]
        g1p = [sem("g10"), sem("g11")]
        g2p = [sem("g20"), sem("g21")]
        rt = sem("rt"); te = sem("te"); gh = sem("gh"); gv = sem("gv")
        ev = sem("ev"); st = sem("st")
        if ctx is None:
            _clean_sems(nc, _semhs)
            block = es.enter_context(nc.Block())
        else:
            ctx.clean(_semhs)
            block = ctx.block

        @block.sync
        def _(sy):
            sy.dma_start(out=s_sw[:].rearrange("p (c e) -> p c e", c=NCH),
                         in_=sw.rearrange("(c p) e -> p c e", p=128)).then_inc(ld, 16)
            for t_, d_ in [(s_swb, swb), (s_w8t, w8t), (s_id, ident), (s_sel, sel),
                           (s_b1, b1c), (s_b2, b2c)]:
                sy.dma_start(out=t_[:], in_=d_).then_inc(ld, 16)
            for tt in range(NTT):
                if tt >= 2:
                    sy.wait_ge(te, tt - 1)
                sy.dma_start(out=s_xr[:].rearrange("p (b c t) -> p b c t", b=2, c=NCH)[:, tt % 2],
                             in_=xnf[:, tt * 128:(tt + 1) * 128].rearrange("(c p) t -> p c t", p=128)
                             ).then_inc(rlp[tt % 2], 16)
            for hidx in range(2 * NE):
                e, H = hidx // 2, hidx % 2
                if hidx >= 2:
                    sy.wait_ge(te, te_h_base(e - 1, H) + 32)
                sy.dma_start(out=s_xnh[:].rearrange("p (b c t) -> p b c t", b=2, c=NCH)[:, hidx % 2],
                             in_=xnb[:, H * 1024:(H + 1) * 1024].rearrange("(c p) t -> p c t", p=128)
                             ).then_inc(xlp[hidx % 2], 16)

        @block.gpsimd
        def _(g):
            for e in range(NE):
                for H in range(2):
                    for f in range(NF):
                        if f >= 2:
                            g.wait_ge(te, te_h_base(e, H) + f - 1)
                        else:
                            prev = te_h_base(e, H - 1) + 31 + f if H == 1 else \
                                   (te_h_base(e - 1, 1) + 31 + f if e > 0 else 0)
                            g.wait_ge(te, prev)
                        g.dma_start(out=s_w1b[:].rearrange("p (b c f) -> p b c f", b=2, c=NCH)[:, f % 2],
                                    in_=w1[e, f].rearrange("p (c f) -> p c f", c=NCH)
                                    ).then_inc(g1p[f % 2], 16)
                    for o in range(NCH):
                        if o >= 2:
                            g.wait_ge(te, te_y_base(e, H) + o - 1)
                        else:
                            prev = te_y_base(e, H - 1) + 7 + o if H == 1 else \
                                   (te_y_base(e - 1, 1) + 7 + o if e > 0 else 0)
                            g.wait_ge(te, prev)
                        g.dma_start(out=s_w2b[:].rearrange("p (b c f) -> p b c f", b=2, c=32)[:, o % 2],
                                    in_=w2[e, o].rearrange("p (c f) -> p c f", c=32)
                                    ).then_inc(g2p[o % 2], 16)

        @block.tensor
        def _(t):
            t.wait_ge(ld, 16 * 7)
            for tt in range(NTT):
                t.wait_ge(rlp[tt % 2], 16 * (tt // 2 + 1))
                xv = s_xr[:].rearrange("p (b c t) -> p b c t", b=2, c=NCH)[:, tt % 2]
                for c in range(NCH):
                    mm = t.matmul(ps8[:, tt * 8:(tt + 1) * 8], xv[:, c], s_sw[:, c * 8:(c + 1) * 8],
                                  start=(c == 0), stop=(c == NCH - 1))
                mm.then_inc(te, 1)
            t.wait_ge(rt, 3)
            for tt in range(NTT):
                tp = t.transpose(ps8[0:8, 2048 + tt * 128: 2048 + (tt + 1) * 128],
                                 s_g8t[:, tt * 8:(tt + 1) * 8], s_id[:])
            tp.then_inc(rt, 1)
            for e in range(NE):
                t.wait_ge(rt, 5)
                if e > 0:
                    t.wait_ge(ev, ev_base(e, 0))
                for blkg in range(4):
                    if blkg >= 2:
                        t.wait_ge(gv, 4 * e + blkg - 1)
                    mm = t.matmul(bank(5 + blkg % 2), s_sel[:, e * 128:(e + 1) * 128],
                                  s_g8[:, blkg * 512:(blkg + 1) * 512], start=True, stop=True)
                    mm.then_inc(te, 1)
                for H in range(2):
                    hidx = 2 * e + H
                    t.wait_ge(xlp[hidx % 2], 16 * (hidx // 2 + 1))
                    xv = s_xnh[:].rearrange("p (b c t) -> p b c t", b=2, c=NCH)[:, hidx % 2]
                    for f in range(NF):
                        t.wait_ge(g1p[f % 2], 16 * ((2 * e + H) * 16 + f // 2 + 1))
                        if f >= 2:
                            t.wait_ge(gh, gh_base(e, H) + 2 * (f - 1))
                        elif (e, H) != (0, 0):
                            t.wait_ge(gh, gh_base(e, H))
                        else:
                            t.wait_ge(rt, 1)
                        wv = s_w1b[:].rearrange("p (b c f) -> p b c f", b=2, c=NCH)[:, f % 2]
                        for c in range(NCH):
                            for tbh in range(2):
                                mm = t.matmul(bank((f % 2) * 2 + tbh), wv[:, c],
                                              xv[:, c, tbh * 512:(tbh + 1) * 512],
                                              start=(c == 0), stop=(c == NCH - 1))
                        mm.then_inc(te, 1)
                    for o in range(NCH):
                        t.wait_ge(g2p[o % 2], 16 * ((2 * e + H) * 4 + o // 2 + 1))
                        t.wait_ge(gh, gh_base(e, H) + 64)
                        if o >= 2:
                            t.wait_ge(ev, ev_base(e, H) + 2 * (o - 1))
                        elif e == 0 and H == 0:
                            t.wait_ge(rt, 5)
                        wv = s_w2b[:].rearrange("p (b c f) -> p b c f", b=2, c=32)[:, o % 2]
                        for fc in range(32):
                            for tbh in range(2):
                                mm = t.matmul(bank(4 + (o % 2) * 2 + tbh), wv[:, fc],
                                              s_h[:, fc * 1024 + tbh * 512: fc * 1024 + (tbh + 1) * 512],
                                              start=(fc == 0), stop=(fc == 31))
                        mm.then_inc(te, 1)

        @block.scalar
        def _(s):
            s.wait_ge(rt, 1)
            for tt in range(NTT):
                s.activation(out=s_ex[:, tt * 8:(tt + 1) * 8], in_=s_lg[:, tt * 8:(tt + 1) * 8],
                             func=AF.Exp, bias=s_nlm[:, tt:tt + 1],
                             accum_out=s_esum[:, tt:tt + 1])
            s.activation(out=s_spc[:], in_=s_w8t[:], func=AF.Identity,
                         bias=0.0).then_inc(rt, 1)
            s.wait_ge(rt, 4)
            for q4 in range(4):
                act = s.activation(out=s_g8[:, q4 * 512:(q4 + 1) * 512],
                                   in_=ps8[0:8, 2048 + q4 * 512: 2048 + (q4 + 1) * 512],
                                   func=AF.Identity, bias=0.0)
            act.then_inc(rt, 1)
            for e in range(NE):
                for blkg in range(4):
                    s.wait_ge(te, TE_RT + 84 * e + blkg + 1)
                    if e > 0 and blkg == 0:
                        s.wait_ge(ev, ev_base(e, 0))
                    s.activation(out=s_G[:, blkg * 512:(blkg + 1) * 512],
                                 in_=bank(5 + blkg % 2), func=AF.Identity,
                                 bias=0.0).then_inc(gv, 1)
                s.activation(out=s_spc[:, 0:128], in_=s_w8t[:, 0:128], func=AF.Identity,
                             bias=0.0).then_inc(rt, 1)
                for H in range(2):
                    for f in range(NF):
                        s.wait_ge(te, te_h_base(e, H) + f + 1)
                        for tbh in range(2):
                            s.activation(out=s_h[:, f * 1024 + tbh * 512: f * 1024 + (tbh + 1) * 512],
                                         in_=bank((f % 2) * 2 + tbh), func=AF.Gelu_apprx_tanh,
                                         bias=s_b1[:, e * 32 + f:e * 32 + f + 1]).then_inc(gh, 1)

        @block.vector
        def _(v):
            v.memset(s_moe[:], 0.0)
            v.wait_ge(te, 16)
            v.wait_ge(ld, 16 * 7)
            v.tensor_tensor(out=s_lg[:], in0=ps8[:, 0:128],
                            in1=s_w8t[:], op=OP.mult) if False else None
            for tt in range(NTT):
                v.tensor_tensor(out=s_lg[:, tt * 8:(tt + 1) * 8], in0=ps8[:, tt * 8:(tt + 1) * 8],
                                in1=s_swb[:], op=OP.add)
            for tt in range(NTT):
                v.reduce_max(out=s_lm[:, tt:tt + 1], in_=s_lg[:, tt * 8:(tt + 1) * 8], axis=AX.X)
            v.tensor_copy(out=s_spc[:], in_=s_w8t[:])
            v.tensor_scalar(out=s_nlm[:], in0=s_lm[:], scalar1=-1.0, scalar2=None, op0=OP.mult)
            v.tensor_copy(out=s_spc[:], in_=s_w8t[:])
            v.nop().then_inc(rt, 1)
            v.wait_ge(rt, 2)
            v.tensor_copy(out=s_spc[:], in_=s_w8t[:])
            for tt in range(NTT):
                v.tensor_scalar(out=s_sc[:, tt * 8:(tt + 1) * 8], in0=s_lg[:, tt * 8:(tt + 1) * 8],
                                scalar1=s_lm[:, tt:tt + 1], scalar2=None, op0=OP.is_equal)
            v.tensor_copy(out=s_spc[:], in_=s_w8t[:])
            v.tensor_tensor(out=s_sc[:], in0=s_sc[:], in1=s_w8t[:], op=OP.mult)
            for tt in range(NTT):
                v.reduce_max(out=s_m2[:, tt:tt + 1], in_=s_sc[:, tt * 8:(tt + 1) * 8], axis=AX.X)
            for tt in range(NTT):
                v.tensor_scalar(out=s_mask[:, tt * 8:(tt + 1) * 8], in0=s_sc[:, tt * 8:(tt + 1) * 8],
                                scalar1=s_m2[:, tt:tt + 1], scalar2=None, op0=OP.is_equal)
            v.reciprocal(out=s_rp[:], in_=s_esum[:])
            v.tensor_copy(out=s_spc[:], in_=s_w8t[:])
            for tt in range(NTT):
                v.tensor_scalar(out=s_g8t[:, tt * 8:(tt + 1) * 8], in0=s_mask[:, tt * 8:(tt + 1) * 8],
                                scalar1=s_rp[:, tt:tt + 1], scalar2=None, op0=OP.mult)
            v.tensor_copy(out=s_spc[:], in_=s_w8t[:]).then_inc(rt, 1)
            for e in range(NE):
                for H in range(2):
                    for o in range(NCH):
                        for tbh in range(2):
                            v.wait_ge(te, te_y_base(e, H) + o + 1)
                            if H == 0 and o == 0 and tbh == 0:
                                v.wait_ge(rt, 6 + e)
                            gslice = s_G[:, H * 1024 + tbh * 512: H * 1024 + (tbh + 1) * 512]
                            v.scalar_tensor_tensor(out=s_tmp[:], in0=bank(4 + (o % 2) * 2 + tbh),
                                                   scalar=s_b2[:, e * 8 + o:e * 8 + o + 1],
                                                   in1=gslice, op0=OP.add, op1=OP.mult)
                            dst = s_moe[:, o * T + H * 1024 + tbh * 512: o * T + H * 1024 + (tbh + 1) * 512]
                            v.tensor_tensor(out=dst, in0=dst, in1=s_tmp[:], op=OP.add).then_inc(ev, 1)
            v.tensor_copy(out=s_spc[:], in_=s_moe[:, NCH * T - 128:NCH * T]).then_inc(ev, 1)

        @block.gpsimd
        def _(g):
            g.wait_ge(ev, 32 * NE + 1)
            for c in range(NCH):
                g.dma_start(out=moef[c * 128:(c + 1) * 128, :],
                            in_=s_moe[:, c * T:(c + 1) * T]).then_inc(st, 16)
            nst = NCH
            if debug:
                for t_, d_ in [(s_lg, dbg_lg), (s_g8t, dbg_g8t), (s_g8, dbg_g8),
                               (s_rp, dbg_rp), (s_lm, dbg_lm), (s_h, dbg_h),
                               (s_esum, dbg_esum)]:
                    g.dma_start(out=d_, in_=t_[:]).then_inc(st, 16)
                    nst += 1
            g.wait_ge(st, 16 * nst)
    return nc


def build_l1_attn(odd, ctx=None, A=None, uniform=False):
    Sk = 4096 if (odd or uniform) else 2048
    IB0 = 4 if odd else 0
    NJ = Sk // 128
    NKB = Sk // 512
    if ctx is None:
        nc = _mk("l1attn_o" if odd else "l1attn_e")
        kT = nc.dram_tensor("kT", [512, Sk], BF16, kind="ExternalInput").ap()
        A = dict(qT=nc.dram_tensor("qT", [512, T], BF16, kind="ExternalInput").ap(),
                 k_slice=lambda c, j0, w: kT[c * 128:(c + 1) * 128, j0:j0 + w],
                 vtok=nc.dram_tensor("vtok", [Sk, 512], BF16, kind="ExternalInput").ap(),
                 w_outT=nc.dram_tensor("w_outT", [512, D], F32, kind="ExternalInput").ap(),
                 bout=nc.dram_tensor("bout", [128, 8], F32, kind="ExternalInput").ap(),
                 masks=nc.dram_tensor("masks", [128, (80 if uniform else 4) * 512], BF16,
                                      kind="ExternalInput").ap(),
                 attn1=nc.dram_tensor("attn1", [D, T], F32, kind="ExternalOutput").ap())
    else:
        nc = ctx.nc
    qT, k_slice, vtok = A["qT"], A["k_slice"], A["vtok"]
    w_outT, bout, masks, attn1 = A["w_outT"], A["bout"], A["masks"], A["attn1"]
    SC = float(512 ** -0.5)

    if uniform:
        NJT = [4 * (ib + 5) for ib in range(4)]
    else:
        NJT = [4 * (IB0 + ib + 1) for ib in range(4)]
    CUM = [sum(NJT[:i]) for i in range(5)]

    def diag(ib, jt):
        if uniform:
            return jt >= 4 * ib
        return jt >= 4 * (IB0 + ib)

    def nmasks_before(ib, jt):
        n = sum(1 for ib2 in range(ib) for j2 in range(NJT[ib2]) if diag(ib2, j2))
        return n + sum(1 for j2 in range(jt) if diag(ib, j2))

    with ExitStack() as es:
        sb = lambda n, *a: es.enter_context(nc.sbuf_tensor(_u(n), *a))
        ps = lambda n, *a: es.enter_context(nc.psum_tensor(_u(n), *a))
        _semhs = []
        def sem(n):
            h = es.enter_context(nc.semaphore(_u(n)))
            _semhs.append(h)
            return h
        s_q = sb("s_q", [128, 4 * T], BF16)
        s_k = sb("s_k", [128, 4 * Sk], BF16)
        s_v = sb("s_v", [128, NJ * 512], BF16)
        s_wo = sb("s_wo", [128, 4 * D], BF16)
        s_bout = sb("s_bout", [128, 8], F32)
        s_mask = sb("s_mask", [128, (2 * 20 if uniform else 4) * 512], BF16)
        s_sqsc = sb("s_sqsc", [128, 4 * Sk], BF16)
        s_nm = sb("s_nm", [1, 16], F32)
        s_nm2 = sb("s_nm2", [1, 16], F32)
        s_negM = sb("s_negM", [128, 1], F32)
        s_oneb = sb("s_oneb", [128, 1], BF16)
        s_onerow = sb("s_onerow", [1, 128], F32)
        s_E = sb("s_E", [128, 2 * 512], BF16)
        s_rr = sb("s_rr", [1, 512], F32)
        s_rbc = sb("s_rbc", [128, 512], F32)
        s_post = sb("s_post", [128, 4 * 512], BF16)
        s_tmp = sb("s_tmp", [128, 2 * 512], F32)
        s_og = sb("s_og", [128, 2 * 512], F32)
        s_spc = sb("s_spc", [128, 512], F32)
        ps8 = ps("ps8", [128, 4096], F32)
        bank = lambda i: ps8[:, i * 512:(i + 1) * 512]
        nb6 = lambda par: ps8[0:1, 3072 + par * 512: 3072 + par * 512 + 512]

        ld = sem("ld"); gld = sem("gld"); nr = sem("nr"); nrs = sem("nrs")
        mlp = [sem("ml0"), sem("ml1")]
        NLD = (4 + (8 if uniform else 4) + 2 + (0 if uniform else 1))
        tsc = sem("tsc"); xp = sem("xp"); msk = sem("msk"); pvc = sem("pvc")
        csd = sem("csd"); rbc = sem("rbc"); ope = sem("ope"); oevt = sem("oevt")
        stwp = [sem("stw0"), sem("stw1")]
        if ctx is None:
            _clean_sems(nc, _semhs)
            block = es.enter_context(nc.Block())
        else:
            ctx.clean(_semhs)
            block = ctx.block
        NNM = 4 + NKB

        @block.sync
        def _(sy):
            for c in range(4):
                sy.dma_start(out=s_q[:, c * T:(c + 1) * T],
                             in_=qT[c * 128:(c + 1) * 128, :]).then_inc(ld, 16)
                if uniform:
                    for half in range(2):
                        sy.dma_start(out=s_k[:, c * Sk + half * 2048: c * Sk + (half + 1) * 2048],
                                     in_=k_slice(c, half * 2048, 2048)).then_inc(ld, 16)
                else:
                    sy.dma_start(out=s_k[:, c * Sk:(c + 1) * Sk],
                                 in_=k_slice(c, 0, Sk)).then_inc(ld, 16)
            sy.dma_start(out=s_v[:].rearrange("p (j d) -> p j d", j=NJ),
                         in_=vtok.rearrange("(j p) d -> p j d", p=128)).then_inc(ld, 16)
            sy.dma_start(out=s_bout[:], in_=bout).then_inc(ld, 16)
            if uniform:
                for ib in range(4):
                    if ib >= 2:
                        sy.wait_ge(msk, 1 + NNM + 20 * (ib - 1))
                    sy.dma_start(out=s_mask[:, (ib % 2) * 20 * 512:(ib % 2 + 1) * 20 * 512],
                                 in_=masks[:, ib * 20 * 512:(ib + 1) * 20 * 512]).then_inc(mlp[ib % 2], 16)
            else:
                sy.dma_start(out=s_mask[:], in_=masks).then_inc(ld, 16)

        @block.gpsimd
        def _(g):
            for c in range(4):
                g.dma_start(out=s_wo[:, c * D:(c + 1) * D],
                            in_=w_outT[c * 128:(c + 1) * 128, :]).then_inc(gld, 16)
            for ib in range(4):
                for ot in range(NCH):
                    n = ib * NCH + ot
                    g.wait_ge(oevt, n + 1)
                    g.dma_start(out=attn1[ot * 128:(ot + 1) * 128, ib * 512:(ib + 1) * 512],
                                in_=s_og[:, (n % 2) * 512:(n % 2) * 512 + 512]
                                ).then_inc(stwp[n % 2], 16)
            g.wait_ge(stwp[0], 16 * 16)
            g.wait_ge(stwp[1], 16 * 16)

        @block.scalar
        def _(s):
            s.wait_ge(ld, 16 * NLD)
            s.wait_ge(gld, 16 * 4)
            s.activation(out=s_sqsc[:, 0:4 * T], in_=s_q[:], func=AF.Square).then_inc(nrs, 1)
            s.wait_ge(nr, 4)
            s.activation(out=s_sqsc[:, 0:4 * Sk], in_=s_k[:], func=AF.Square).then_inc(nrs, 1)
            for ib in range(4):
                for jt in range(NJT[ib]):
                    it = CUM[ib] + jt
                    s.wait_ge(tsc, it + 1)
                    if it >= 2:
                        s.wait_ge(pvc, it - 1)
                    s.activation(out=s_E[:, (it % 2) * 512:(it % 2) * 512 + 512],
                                 in_=bank(it % 2), func=AF.Exp, bias=s_negM[:],
                                 scale=SC).then_inc(xp, 1)
                s.wait_ge(pvc, CUM[ib + 1])
                for dc in range(4):
                    s.activation(out=s_post[:, dc * 512:(dc + 1) * 512], in_=bank(2 + dc),
                                 func=AF.Identity, bias=0.0).then_inc(csd, 1)
                s.wait_ge(rbc, ib * 2 + 1)
                s.activation(out=s_rbc[:], in_=bank(7), func=AF.Identity,
                             bias=0.0).then_inc(rbc, 1)

        @block.vector
        def _(v):
            v.memset(s_oneb[:], 1.0)
            v.memset(s_onerow[:], 1.0)
            v.memset(s_spc[:], 0.0)
            v.nop().then_inc(msk, 1)
            for i in range(NNM):
                v.wait_ge(nr, i + 1)
                v.reduce_max(out=s_nm[:, i:i + 1], in_=nb6(i % 2), axis=AX.X)
                v.nop().then_inc(msk, 1)
            v.tensor_copy(out=s_spc[:], in_=s_q[:, 0:512])
            v.reduce_max(out=s_nm2[:, 0:1], in_=s_nm[:, 0:4], axis=AX.X)
            v.reduce_max(out=s_nm2[:, 4:5], in_=s_nm[:, 4:4 + NKB], axis=AX.X)
            v.tensor_copy(out=s_spc[:], in_=s_q[:, 0:512])
            v.tensor_tensor(out=s_nm2[:, 8:9], in0=s_nm2[:, 0:1], in1=s_nm2[:, 4:5], op=OP.add)
            v.tensor_copy(out=s_spc[:], in_=s_q[:, 0:512])
            v.tensor_scalar(out=s_nm2[:, 12:13], in0=s_nm2[:, 8:9], scalar1=-0.5 * SC,
                            scalar2=None, op0=OP.mult)
            v.tensor_copy(out=s_spc[:], in_=s_q[:, 0:512]).then_inc(nr, 100)
            v.wait_ge(nr, 200 + NNM)
            v.tensor_copy(out=s_negM[:], in_=ps8[:, 3072:3073])
            v.tensor_copy(out=s_spc[:], in_=s_q[:, 0:512]).then_inc(nr, 100)
            for ib in range(4):
                for jt in range(NJT[ib]):
                    it = CUM[ib] + jt
                    if diag(ib, jt):
                        v.wait_ge(xp, it + 1)
                        if uniform:
                            if jt == 4 * ib:
                                v.wait_ge(mlp[ib % 2], 16 * (ib // 2 + 1))
                            dk = (ib % 2) * 20 + (jt - 4 * ib)
                        else:
                            dk = jt - 4 * (IB0 + ib)
                        epar = (it % 2) * 512
                        v.tensor_tensor(out=s_E[:, epar:epar + 512], in0=s_E[:, epar:epar + 512],
                                        in1=s_mask[:, dk * 512:(dk + 1) * 512],
                                        op=OP.mult).then_inc(msk, 1)
                v.wait_ge(pvc, CUM[ib + 1])
                v.tensor_copy(out=s_spc[:], in_=s_q[:, 0:512])
                v.reciprocal(out=s_rr[:], in_=ps8[0:1, 3072:3584])
                v.tensor_copy(out=s_spc[:], in_=s_q[:, 0:512]).then_inc(csd, 1)
                for ot in range(NCH):
                    n = ib * NCH + ot
                    v.wait_ge(ope, n + 1)
                    if n >= 2:
                        v.wait_ge(stwp[n % 2], 16 * (n // 2))
                    v.wait_ge(rbc, ib * 2 + 2)
                    v.scalar_tensor_tensor(out=s_tmp[:, (n % 2) * 512:(n % 2) * 512 + 512],
                                           in0=bank(2 + ot % 2), scalar=1.0, in1=s_rbc[:],
                                           op0=OP.mult, op1=OP.mult)
                    v.tensor_scalar(out=s_og[:, (n % 2) * 512:(n % 2) * 512 + 512],
                                    in0=s_tmp[:, (n % 2) * 512:(n % 2) * 512 + 512],
                                    scalar1=s_bout[:, ot:ot + 1], scalar2=None, op0=OP.add)
                    v.tensor_copy(out=s_spc[:], in_=s_og[:, (n % 2) * 512:(n % 2) * 512 + 512]
                                  ).then_inc(oevt, 1)

        @block.tensor
        def _(t):
            t.wait_ge(ld, 16 * NLD)
            t.wait_ge(gld, 16 * 4)
            t.wait_ge(nrs, 1)
            t.wait_ge(msk, 1)
            for i in range(NNM):
                if i == 4:
                    t.wait_ge(nrs, 2)
                if i >= 2:
                    t.wait_ge(msk, i)
                W = T if i < 4 else Sk
                blk = i if i < 4 else i - 4
                for c in range(4):
                    mm = t.matmul(nb6(i % 2), s_oneb[:],
                                  s_sqsc[:, c * W + blk * 512: c * W + blk * 512 + 512],
                                  start=(c == 0), stop=(c == 3))
                mm.then_inc(nr, 1)
            t.wait_ge(nr, 100 + NNM)
            t.matmul(ps8[:, 3072:3073], s_onerow[:], s_nm2[:, 12:13],
                     start=True, stop=True).then_inc(nr, 100)
            for ib in range(4):
                for jt in range(NJT[ib]):
                    it = CUM[ib] + jt
                    if it == 0:
                        t.wait_ge(nr, 300 + NNM)
                    if jt == 0 and ib >= 1:
                        t.wait_ge(oevt, 8 * ib)
                        t.wait_ge(csd, 5 * ib)
                    if it >= 2:
                        t.wait_ge(xp, it - 1)
                    for c in range(4):
                        mm = t.matmul(bank(it % 2),
                                      s_k[:, c * Sk + jt * 128: c * Sk + (jt + 1) * 128],
                                      s_q[:, c * T + ib * 512: c * T + (ib + 1) * 512],
                                      start=(c == 0), stop=(c == 3))
                    mm.then_inc(tsc, 1)
                    if diag(ib, jt):
                        t.wait_ge(msk, 1 + NNM + nmasks_before(ib, jt) + 1)
                    else:
                        t.wait_ge(xp, it + 1)
                    epar = (it % 2) * 512
                    t.matmul(ps8[0:1, 3072:3584], s_oneb[:], s_E[:, epar:epar + 512],
                             start=(jt == 0), stop=(jt == NJT[ib] - 1))
                    for dc in range(4):
                        mm = t.matmul(bank(2 + dc), s_v[:, jt * 512 + dc * 128: jt * 512 + (dc + 1) * 128],
                                      s_E[:, epar:epar + 512],
                                      start=(jt == 0), stop=(jt == NJT[ib] - 1))
                    mm.then_inc(pvc, 1)
                t.wait_ge(csd, ib * 5 + 5)
                t.matmul(bank(7), s_onerow[:], s_rr[:], start=True, stop=True).then_inc(rbc, 1)
                for ot in range(NCH):
                    n = ib * NCH + ot
                    if n >= 2:
                        t.wait_ge(oevt, n - 1)
                    t.wait_ge(rbc, ib * 2 + 2)
                    for c in range(4):
                        mm = t.matmul(bank(2 + ot % 2), s_wo[:, c * D + ot * 128: c * D + (ot + 1) * 128],
                                      s_post[:, c * 512:(c + 1) * 512],
                                      start=(c == 0), stop=(c == 3))
                    mm.then_inc(ope, 1)
    return nc


def build_tpartials(ctx=None, A=None):
    if ctx is None:
        nc = _mk("tpart")
        A = dict(w1s=nc.dram_tensor("w1s", [128, 4096], F32, kind="ExternalInput").ap(),
                 w2s=nc.dram_tensor("w2s", [128, 4096], F32, kind="ExternalInput").ap(),
                 part=nc.dram_tensor("part", [1, 2], F32, kind="ExternalOutput").ap())
    else:
        nc = ctx.nc
    w1s, w2s, part = A["w1s"], A["w2s"], A["part"]
    with ExitStack() as es:
        sb = lambda n, *a: es.enter_context(nc.sbuf_tensor(_u(n), *a))
        ps = lambda n, *a: es.enter_context(nc.psum_tensor(_u(n), *a))
        _semhs = []
        def sem(n):
            h = es.enter_context(nc.semaphore(_u(n)))
            _semhs.append(h)
            return h
        s_w = sb("s_w", [128, 2 * 4096], F32)
        s_r = sb("s_r", [128, 8], F32)
        s_one = sb("s_one", [128, 1], F32)
        s_o = sb("s_o", [1, 2], F32)
        s_spc = sb("s_spc", [128, 512], F32)
        p = ps("p", [1, 512], F32)
        ld = sem("ld"); ve = sem("ve"); te = sem("te"); se = sem("se"); st = sem("st")
        if ctx is None:
            _clean_sems(nc, _semhs)
            block = es.enter_context(nc.Block())
        else:
            ctx.clean(_semhs)
            block = ctx.block

        @block.sync
        def _(sy):
            sy.dma_start(out=s_w[:, 0:4096], in_=w1s).then_inc(ld, 16)
            sy.dma_start(out=s_w[:, 4096:8192], in_=w2s).then_inc(ld, 16)

        @block.vector
        def _(v):
            v.memset(s_one[:], 1.0)
            v.wait_ge(ld, 32)
            v.tensor_reduce(out=s_r[:, 0:1], in_=s_w[:, 0:4096], axis=AX.X, op=OP.add,
                            apply_absolute_value=True)
            v.tensor_reduce(out=s_r[:, 4:5], in_=s_w[:, 4096:8192], axis=AX.X, op=OP.add,
                            apply_absolute_value=True)
            v.tensor_copy(out=s_spc[:], in_=s_w[:, 0:512])
            v.tensor_copy(out=s_spc[:], in_=s_w[:, 0:512]).then_inc(ve, 1)

        @block.tensor
        def _(t):
            t.wait_ge(ve, 1)
            t.matmul(p[:, 0:1], s_one[:], s_r[:, 0:1], start=True, stop=True)
            t.matmul(p[:, 1:2], s_one[:], s_r[:, 4:5], start=True, stop=True).then_inc(te, 1)

        @block.scalar
        def _(s):
            s.wait_ge(te, 1)
            s.activation(out=s_o[:], in_=p[0:1, 0:2], func=AF.Identity, bias=0.0)
            s.activation(out=s_spc[:], in_=s_w[:, 0:512], func=AF.Identity, bias=0.0)
            s.activation(out=s_spc[0:1, 0:2], in_=s_o[:], func=AF.Identity, bias=0.0).then_inc(se, 1)

        @block.gpsimd
        def _(g):
            g.wait_ge(se, 1)
            g.dma_start(out=part, in_=s_o[:]).then_inc(st, 16)
            g.wait_ge(st, 16)
    return nc


def build_ternarize(debug=False, ctx=None, A=None, ptot_mode=False):
    if ctx is None:
        nc = _mk("tern")
        A = dict(w1s=nc.dram_tensor("w1s", [128, 4096], F32, kind="ExternalInput").ap(),
                 w2s=nc.dram_tensor("w2s", [128, 4096], F32, kind="ExternalInput").ap(),
                 pa=nc.dram_tensor("pa", [1, 8], F32, kind="ExternalInput").ap(),
                 pb=nc.dram_tensor("pb", [1, 8], F32, kind="ExternalInput").ap(),
                 t1=nc.dram_tensor("t1", [128, 4096], BF16, kind="ExternalOutput").ap(),
                 t2=nc.dram_tensor("t2", [128, 4096], BF16, kind="ExternalOutput").ap())
    else:
        nc = ctx.nc
    w1s, w2s, t1, t2 = A["w1s"], A["w2s"], A["t1"], A["t2"]
    pa, pb = A.get("pa"), A.get("pb")
    ptot = A.get("ptot")
    if debug:
        assert ctx is None
        dbg_st = nc.dram_tensor("dbg_st", [1, 32], F32, kind="ExternalOutput").ap()
        dbg_thr = nc.dram_tensor("dbg_thr", [128, 4], F32, kind="ExternalOutput").ap()
        dbg_ge = nc.dram_tensor("dbg_ge", [128, 512], F32, kind="ExternalOutput").ap()
        dbg_p = nc.dram_tensor("dbg_p", [1, 16], F32, kind="ExternalOutput").ap()
        dbg_w = nc.dram_tensor("dbg_w", [128, 512], F32, kind="ExternalOutput").ap()
    NTOT = float(1024 * 4096)
    with ExitStack() as es:
        sb = lambda n, *a: es.enter_context(nc.sbuf_tensor(_u(n), *a))
        ps = lambda n, *a: es.enter_context(nc.psum_tensor(_u(n), *a))
        _semhs = []
        def sem(n):
            h = es.enter_context(nc.semaphore(_u(n)))
            _semhs.append(h)
            return h
        s_w = sb("s_w", [128, 2 * 4096], F32)
        s_p = sb("s_p", [1, 16], F32)
        s_st = sb("s_st", [1, 32], F32)
        s_thr = sb("s_thr", [128, 4], F32)
        s_ge = sb("s_ge", [128, 4096], F32)
        s_le = sb("s_le", [128, 4096], F32)
        s_t = sb("s_t", [128, 2 * 4096], BF16)
        s_onerow = sb("s_onerow", [1, 128], F32)
        s_spc = sb("s_spc", [128, 512], F32)
        p8 = ps("p8", [128, 512], F32)
        ld = sem("ld"); ve = sem("ve"); te = sem("te"); se = sem("se"); st = sem("st")
        if ctx is None:
            _clean_sems(nc, _semhs)
            block = es.enter_context(nc.Block())
        else:
            ctx.clean(_semhs)
            block = ctx.block

        @block.sync
        def _(sy):
            sy.dma_start(out=s_w[:, 0:4096], in_=w1s).then_inc(ld, 16)
            sy.dma_start(out=s_w[:, 4096:8192], in_=w2s).then_inc(ld, 16)
            if ptot_mode:
                sy.dma_start(out=s_p[0:1, 0:2], in_=ptot).then_inc(ld, 16)
                sy.dma_start(out=s_p[0:1, 8:10], in_=ptot).then_inc(ld, 16)
            else:
                sy.dma_start(out=s_p[0:1, 0:8], in_=pa).then_inc(ld, 16)
                sy.dma_start(out=s_p[0:1, 8:16], in_=pb).then_inc(ld, 16)

        @block.vector
        def _(v):
            v.memset(s_onerow[:], 1.0)
            v.wait_ge(ld, 64)
            SP = lambda: v.tensor_copy(out=s_spc[:], in_=s_w[:, 0:512])
            if ptot_mode:
                v.tensor_copy(out=s_st[:, 0:1], in_=s_p[0:1, 0:1])
                v.tensor_copy(out=s_st[:, 4:5], in_=s_p[0:1, 1:2])
            else:
                v.reduce_sum(out=s_st[:, 0:1], in_=s_p[0:1, 0:8], axis=AX.X)
                v.reduce_sum(out=s_st[:, 4:5], in_=s_p[0:1, 8:16], axis=AX.X)
            SP()
            v.tensor_scalar(out=s_st[:, 8:9], in0=s_st[:, 0:1], scalar1=1.0 / NTOT,
                            scalar2=None, op0=OP.mult)
            v.tensor_scalar(out=s_st[:, 12:13], in0=s_st[:, 4:5], scalar1=1.0 / NTOT,
                            scalar2=None, op0=OP.mult)
            SP()
            v.tensor_scalar(out=s_st[:, 16:17], in0=s_st[:, 8:9], scalar1=1e-8, scalar2=0.5,
                            op0=OP.max, op1=OP.mult)
            v.tensor_scalar(out=s_st[:, 20:21], in0=s_st[:, 12:13], scalar1=1e-8, scalar2=0.5,
                            op0=OP.max, op1=OP.mult)
            SP()
            v.tensor_scalar(out=s_st[:, 24:25], in0=s_st[:, 16:17], scalar1=-1.0, scalar2=None,
                            op0=OP.mult)
            v.tensor_scalar(out=s_st[:, 28:29], in0=s_st[:, 20:21], scalar1=-1.0, scalar2=None,
                            op0=OP.mult)
            SP(); SP().then_inc(ve, 1)
            v.wait_ge(se, 1)
            for m in range(2):
                wsl = s_w[:, m * 4096:(m + 1) * 4096]
                v.tensor_scalar(out=s_ge[:], in0=wsl, scalar1=s_thr[:, 2 * m:2 * m + 1],
                                scalar2=None, op0=OP.is_ge)
                v.tensor_scalar(out=s_le[:], in0=wsl, scalar1=s_thr[:, 2 * m + 1:2 * m + 2],
                                scalar2=None, op0=OP.is_le)
                v.tensor_tensor(out=s_t[:, m * 4096:(m + 1) * 4096], in0=s_ge[:], in1=s_le[:],
                                op=OP.subtract)
            v.tensor_copy(out=s_spc[:], in_=s_t[:, 7680:8192])
            v.tensor_copy(out=s_spc[:], in_=s_t[:, 3584:4096]).then_inc(ve, 1)

        @block.tensor
        def _(t):
            t.wait_ge(ve, 1)
            for i, sl in enumerate([16, 24, 20, 28]):
                mm = t.matmul(p8[:, i:i + 1], s_onerow[:], s_st[:, sl:sl + 1],
                              start=True, stop=True)
            mm.then_inc(te, 1)

        @block.scalar
        def _(s):
            s.wait_ge(te, 1)
            s.activation(out=s_thr[:], in_=p8[:, 0:4], func=AF.Identity, bias=0.0).then_inc(se, 1)

        @block.gpsimd
        def _(g):
            g.wait_ge(ve, 2)
            g.dma_start(out=t1, in_=s_t[:, 0:4096]).then_inc(st, 16)
            g.dma_start(out=t2, in_=s_t[:, 4096:8192]).then_inc(st, 16)
            nst = 2
            if debug:
                for t_, d_ in [(s_st[:], dbg_st), (s_thr[:], dbg_thr), (s_ge[:, 0:512], dbg_ge),
                               (s_p[:], dbg_p), (s_w[:, 0:512], dbg_w)]:
                    g.dma_start(out=d_, in_=t_).then_inc(st, 16)
                    nst += 1
            g.wait_ge(st, 16 * nst)
    return nc


_NC_CACHE = {}

def _cached(key, fn):
    if key not in _NC_CACHE:
        _NC_CACHE[key] = fn()
    return _NC_CACHE[key]


def _run(nc, maps, ncores=None):
    res = run_bass_kernel_spmd(nc, maps, core_ids=list(range(len(maps))))
    return res.results


def _cols(v, k):
    v = np.asarray(v, dtype=np.float32)
    return np.ascontiguousarray(v.reshape(-1, 128).T)


def kernel_multilaunch(**inputs):
    gi = lambda n: np.asarray(inputs[n])
    x = gi("L0_attn_src").astype(np.float32)
    xsh = shard_x(x)

    ncA = _cached("A", build_ln)
    resA = _run(ncA, [{"xT": s} for s in xsh])
    xn0f = [r["xnf"] for r in resA]
    xn0b = [r["xnb"] for r in resA]

    ncB = _cached("B", build_l0_attn)
    resB = _run(ncB, host_l0_attn_inputs(inputs, xn0b))
    postg = [r["postg"] for r in resB]

    out_wT = np.ascontiguousarray(gi("L0_attn_out_w").T)
    boutc = _cols(gi("L0_attn_out_b"), 8)
    ncC1 = _cached("C1", lambda: build_gemm("outproj", D, D, T, act="identity",
                                            scale=1.0 / 64.0, out_f32=True, nbias=8))
    mapsC1 = []
    for c in range(NCORES):
        b, hh = c // 2, c % 2
        pg = np.concatenate([postg[2 * b], postg[2 * b + 1]], axis=0)[:, hh * T:(hh + 1) * T]
        mapsC1.append(dict(wT=out_wT, xb=np.ascontiguousarray(pg), bias=boutc))
    attnf = [r["outf"] for r in _run(ncC1, mapsC1)]

    ln1p = np.concatenate([_cols(gi("L0_attn_ln1_g"), 8), _cols(gi("L0_attn_ln1_b"), 8)], axis=1)
    ncC1b = _cached("C1b", lambda: build_addln("uln1", 2, gb=True, out_f32=True))
    resC1b = _run(ncC1b, [dict(x0=xn0f[c], x1=attnf[c], params=ln1p) for c in range(NCORES)])
    xpf = [r["xnf"] for r in resC1b]
    xpb = [r["xnb"] for r in resC1b]

    lin1_wT = _tile_wT(gi("L0_attn_lin1_w").T)
    lin2_wT = _tile_wT(gi("L0_attn_lin2_w").T)
    ncC2 = _cached("C2", lambda: build_gemm("lin1", D, 4096, T, act="gelu", nbias=32))
    b1c = _cols(gi("L0_attn_lin1_b"), 32)
    h = [r["out"] for r in _run(ncC2, [dict(wT=lin1_wT, xb=xpb[c], bias=b1c)
                                       for c in range(NCORES)])]
    ncC3 = _cached("C3", lambda: build_gemm("lin2", 4096, D, T, act="identity",
                                            out_f32=True, nbias=8))
    b2c = _cols(gi("L0_attn_lin2_b"), 8)
    fff = [r["outf"] for r in _run(ncC3, [dict(wT=lin2_wT, xb=h[c], bias=b2c)
                                          for c in range(NCORES)])]
    ln2p = np.concatenate([_cols(gi("L0_attn_ln2_g"), 8), _cols(gi("L0_attn_ln2_b"), 8)], axis=1)
    ncC3b = _cached("C3b", lambda: build_addln("ln2", 2, gb=True, out_f32=True))
    y0f = [r["xnf"] for r in _run(ncC3b, [dict(x0=xpf[c], x1=fff[c], params=ln2p)
                                          for c in range(NCORES)])]

    ncD = _cached("D", build_moe)
    swb_t = np.ascontiguousarray(np.tile(gi("L0_ffn_switch_b").astype(np.float32), (128, 1)))
    w8t = np.ascontiguousarray(np.tile(np.arange(8, 0, -1, dtype=np.float32), (128, 16)))
    ident = np.eye(128, dtype=np.float32)
    sel = np.ascontiguousarray(np.concatenate(
        [np.eye(8, dtype=np.float32)[:, e:e + 1] @ np.ones((1, 128), np.float32)
         for e in range(8)], axis=1))
    moe_b1c = np.ascontiguousarray(gi("L0_ffn_b1").astype(np.float32).reshape(256, 128).T)
    moe_b2c = np.ascontiguousarray(gi("L0_ffn_b2").astype(np.float32).reshape(64, 128).T)
    w1 = gi("L0_ffn_w1").astype(np.float32)
    w2 = gi("L0_ffn_w2").astype(np.float32)
    sw = gi("L0_ffn_switch_w").astype(np.float32)
    mapsD = [dict(xnb=xn0b[c], xnf=xn0f[c], sw=sw, swb=swb_t, w8t=w8t, ident=ident,
                  sel=sel, w1=w1, b1c=moe_b1c, w2=w2, b2c=moe_b2c)
             for c in range(NCORES)]
    moef = [r["moef"] for r in _run(ncD, mapsD)]

    ncX1 = _cached("X1", lambda: build_addln("lnx1", 3, gb=False, out_f32=False,
                                             out_sum=True, dtypes=("bf16", "f32", "f32")))
    resX1 = _run(ncX1, [dict(x0=moef[c], x1=xsh[c], x2=y0f[c]) for c in range(NCORES)])
    xn1b = [r["xnb"] for r in resX1]
    x1f = [r["sumf"] for r in resX1]

    wqkvT = np.ascontiguousarray(gi("L1_attn_w_qkv").T)
    ncE = _cached("E", lambda: build_gemm("qkv1", D, 1536, T, act="identity"))
    qkvT = [r["out"] for r in _run(ncE, [dict(wT=wqkvT, xb=xn1b[c]) for c in range(NCORES)])]

    w_outT1 = np.ascontiguousarray(gi("L1_attn_w_out").T)
    bout1 = _cols(gi("L1_attn_b_out"), 8)
    masks = np.zeros((128, 4 * 512), np.float32)
    for k_ in range(4):
        for pj in range(128):
            masks[pj, k_ * 512:(k_ + 1) * 512] = (np.arange(512) >= pj + 128 * k_)
    masks = masks.astype(bf16)
    ncFe = _cached("Fe", lambda: build_l1_attn(False))
    ncFo = _cached("Fo", lambda: build_l1_attn(True))
    mapsE, mapsO = [], []
    for b in range(B):
        qk0, qk1 = qkvT[2 * b], qkvT[2 * b + 1]
        kT_f = np.ascontiguousarray(np.concatenate([qk0[512:1024], qk1[512:1024]], axis=1))
        v_f = np.ascontiguousarray(np.concatenate([qk0[1024:1536].T, qk1[1024:1536].T], axis=0))
        mapsE.append(dict(qT=np.ascontiguousarray(qk0[0:512]),
                          kT=np.ascontiguousarray(qk0[512:1024]),
                          vtok=np.ascontiguousarray(qk0[1024:1536].T),
                          w_outT=w_outT1, bout=bout1, masks=masks))
        mapsO.append(dict(qT=np.ascontiguousarray(qk1[0:512]), kT=kT_f, vtok=v_f,
                          w_outT=w_outT1, bout=bout1, masks=masks))
    attn1_e = [r["attn1"] for r in _run(ncFe, mapsE)]
    attn1_o = [r["attn1"] for r in _run(ncFo, mapsO)]
    attn1 = []
    for c in range(NCORES):
        attn1.append(attn1_e[c // 2] if c % 2 == 0 else attn1_o[c // 2])

    W1T = np.ascontiguousarray(gi("L1_ffn_W1").astype(np.float32).T)
    W2T = np.ascontiguousarray(gi("L1_ffn_W2").astype(np.float32).T)
    ncG = _cached("G", build_tpartials)
    mapsG = [dict(w1s=np.ascontiguousarray(W1T[c * 128:(c + 1) * 128]),
                  w2s=np.ascontiguousarray(W2T[c * 512:(c + 1) * 512].reshape(128, 4096)))
             for c in range(NCORES)]
    parts = [r["part"] for r in _run(ncG, mapsG)]
    pa = np.ascontiguousarray(np.concatenate([p[:, 0:1] for p in parts], 1).astype(np.float32))
    pb = np.ascontiguousarray(np.concatenate([p[:, 1:2] for p in parts], 1).astype(np.float32))
    ncH = _cached("H", build_ternarize)
    resH = _run(ncH, [dict(pa=pa, pb=pb, **mapsG[c]) for c in range(NCORES)])
    W1tT = np.concatenate([r["t1"] for r in resH], axis=0)
    W2tT = np.concatenate([r["t2"].reshape(512, 1024) for r in resH], axis=0)

    ncL1a = _cached("L1a", lambda: build_gemm("l1ffn1", D, 4096, T, act="gelu",
                                              nbias=32, w_bf16=True))
    fb1 = _cols(gi("L1_ffn_b1"), 32)
    h1 = [r["out"] for r in _run(ncL1a, [dict(wT=W1tT, xb=xn1b[c], bias=fb1)
                                         for c in range(NCORES)])]
    ncL1b = _cached("L1b", lambda: build_gemm("l1ffn2", 4096, D, T, act="identity",
                                              out_f32=True, nbias=8, w_bf16=True))
    fb2 = _cols(gi("L1_ffn_b2"), 8)
    ffn1 = [r["outf"] for r in _run(ncL1b, [dict(wT=W2tT, xb=h1[c], bias=fb2)
                                            for c in range(NCORES)])]

    ncFin = _cached("Fin", lambda: build_add("fin", 3))
    resF = _run(ncFin, [dict(x0=x1f[c], x1=attn1[c], x2=ffn1[c]) for c in range(NCORES)])
    out = np.empty((B, S, D), np.float32)
    for c in range(NCORES):
        b, hh = c // 2, c % 2
        out[b, hh * T:(hh + 1) * T, :] = resF[c]["y"].T
    return out


def build_vtok(ctx=None, A=None):
    if ctx is None:
        nc = _mk("vtok")
        A = dict(xnb=nc.dram_tensor("xnb", [D, T], BF16, kind="ExternalInput").ap(),
                 wvT=nc.dram_tensor("wvT", [D, 512], F32, kind="ExternalInput").ap(),
                 vtok=nc.dram_tensor("vtok", [T, 512], BF16, kind="ExternalOutput").ap())
    else:
        nc = ctx.nc
    xnb, wvT, vtok = A["xnb"], A["wvT"], A["vtok"]
    NST = T // 128

    with ExitStack() as es:
        sb = lambda n, *a: es.enter_context(nc.sbuf_tensor(_u(n), *a))
        ps = lambda n, *a: es.enter_context(nc.psum_tensor(_u(n), *a))
        _semhs = []
        def sem(n):
            h = es.enter_context(nc.semaphore(_u(n)))
            _semhs.append(h)
            return h
        s_xv = sb("s_xv", [128, 2 * NCH * 128], BF16)
        s_w = sb("s_w", [128, NCH * 512], BF16)
        s_o = sb("s_o", [128, 2 * 512], BF16)
        p2 = ps("p2", [128, 1024], F32)
        xl = [sem("xl0"), sem("xl1")]
        wl = sem("wl"); te = sem("te"); ev = sem("ev")
        stv = [sem("sv0"), sem("sv1")]
        if ctx is None:
            _clean_sems(nc, _semhs)
            block = es.enter_context(nc.Block())
        else:
            ctx.clean(_semhs)
            block = ctx.block

        @block.gpsimd
        def _(g):
            for c in range(NCH):
                g.dma_start(out=s_w[:, c * 512:(c + 1) * 512],
                            in_=wvT[c * 128:(c + 1) * 128, :]).then_inc(wl, 16)
            for stile in range(NST):
                g.wait_ge(ev, stile + 1)
                g.dma_start(out=vtok[stile * 128:(stile + 1) * 128, :],
                            in_=s_o[:, (stile % 2) * 512:(stile % 2 + 1) * 512]
                            ).then_inc(stv[stile % 2], 16)
            g.wait_ge(stv[0], 16 * 8)
            g.wait_ge(stv[1], 16 * 8)

        @block.sync
        def _(sy):
            for stile in range(NST):
                if stile >= 2:
                    sy.wait_ge(te, stile - 1)
                sy.dma_start(out=s_xv[:].rearrange("p (b c t) -> p b c t", b=2, c=NCH)[:, stile % 2],
                             in_=xnb[:, stile * 128:(stile + 1) * 128].rearrange("(c p) t -> p c t", p=128)
                             ).then_inc(xl[stile % 2], 16)

        @block.tensor
        def _(t):
            t.wait_ge(wl, 16 * NCH)
            for stile in range(NST):
                t.wait_ge(xl[stile % 2], 16 * (stile // 2 + 1))
                if stile >= 2:
                    t.wait_ge(ev, stile - 1)
                xv = s_xv[:].rearrange("p (b c t) -> p b c t", b=2, c=NCH)[:, stile % 2]
                for c in range(NCH):
                    mm = t.matmul(p2[:, (stile % 2) * 512:(stile % 2 + 1) * 512],
                                  xv[:, c], s_w[:, c * 512:(c + 1) * 512],
                                  start=(c == 0), stop=(c == NCH - 1))
                mm.then_inc(te, 1)

        @block.scalar
        def _(s):
            for stile in range(NST):
                s.wait_ge(te, stile + 1)
                if stile >= 2:
                    s.wait_ge(stv[stile % 2], 16 * (stile // 2))
                s.activation(out=s_o[:, (stile % 2) * 512:(stile % 2 + 1) * 512],
                             in_=p2[:, (stile % 2) * 512:(stile % 2 + 1) * 512],
                             func=AF.Identity, bias=0.0)
                s.activation(out=s_o[:, (stile % 2) * 512:(stile % 2 + 1) * 512],
                             in_=p2[:, (stile % 2) * 512:(stile % 2 + 1) * 512],
                             func=AF.Identity, bias=0.0).then_inc(ev, 1)
    return nc


def _cc_phase(ctx, ops):
    nc = ctx.nc
    with ExitStack() as es:
        cc = es.enter_context(nc.semaphore(_u("cc")))
        ctx.clean([cc])

        @ctx.block.gpsimd
        def _(g):
            n = 0
            for kind, op, groups, i_, o_ in ops:
                g.collective_compute(kind, op, replica_groups=groups,
                                     ins=[i_.opt()], outs=[o_.opt()]).then_inc(cc)
                n += 1
            g.wait_ge(cc, n)


PAIRS = [[0, 1], [2, 3], [4, 5], [6, 7]]
ALL8 = [[0, 1, 2, 3, 4, 5, 6, 7]]


def build_ternchain_A(ctx, A):
    nc = ctx.nc
    w1s, w2s, rowd = A["w1s"], A["w2s"], A["rowd"]
    part, ptot, cc = A["part"], A["ptot"], A["cc"]
    es = ctx.ges
    sb = lambda n, *a: es.enter_context(nc.sbuf_tensor(_u(n), *a))
    _semhs = []
    def sem(n):
        h = es.enter_context(nc.semaphore(_u(n)))
        _semhs.append(h)
        return h
    s_w = sb("tca_w", [128, 2 * 4096], F32)
    s_r = sb("tca_r", [128, 8], F32)
    s_row = sb("tca_row", [1, 256], F32)
    s_pq = sb("tca_pq", [1, 8], F32)
    s_spc = sb("tca_spc", [128, 512], F32)
    ld = sem("ld"); ve = sem("ve"); st = sem("st")
    ctx.clean(_semhs)
    block = ctx.block

    @block.sync
    def _(sy):
        sy.dma_start(out=s_w[:, 0:4096], in_=w1s).then_inc(ld, 16)
        sy.dma_start(out=s_w[:, 4096:8192], in_=w2s).then_inc(ld, 16)
        sy.wait_ge(ve, 1)
        sy.dma_start(out=rowd[0:1, 0:128].rearrange("a (p b) -> (a p) b", p=128),
                     in_=s_r[:, 0:1]).then_inc(st, 16)
        sy.dma_start(out=rowd[0:1, 128:256].rearrange("a (p b) -> (a p) b", p=128),
                     in_=s_r[:, 4:5]).then_inc(st, 16)
        sy.wait_ge(st, 32)
        sy.dma_start(out=s_row[:], in_=rowd).then_inc(ld, 16)
        sy.wait_ge(ve, 2)
        sy.dma_start(out=part, in_=s_pq[0:1, 0:2]).then_inc(st, 16)

    @block.vector
    def _(v):
        v.wait_ge(ld, 32)
        v.tensor_reduce(out=s_r[:, 0:1], in_=s_w[:, 0:4096], axis=AX.X, op=OP.add,
                        apply_absolute_value=True)
        v.tensor_reduce(out=s_r[:, 4:5], in_=s_w[:, 4096:8192], axis=AX.X, op=OP.add,
                        apply_absolute_value=True)
        v.tensor_copy(out=s_spc[:], in_=s_w[:, 0:512])
        v.tensor_copy(out=s_spc[:, 0:1], in_=s_r[:, 0:1])
        v.tensor_copy(out=s_spc[:, 4:5], in_=s_r[:, 4:5])
        v.tensor_copy(out=s_spc[:], in_=s_w[:, 512:1024]).then_inc(ve, 1)
        v.wait_ge(ld, 48)
        v.reduce_sum(out=s_pq[0:1, 0:1], in_=s_row[0:1, 0:128], axis=AX.X)
        v.reduce_sum(out=s_pq[0:1, 4:5], in_=s_row[0:1, 128:256], axis=AX.X)
        v.tensor_copy(out=s_spc[:], in_=s_w[:, 0:512])
        v.tensor_copy(out=s_pq[0:1, 1:2], in_=s_pq[0:1, 4:5])
        v.tensor_copy(out=s_spc[:], in_=s_w[:, 512:1024])
        v.tensor_copy(out=s_spc[0:1, 0:2], in_=s_pq[0:1, 0:2])
        v.tensor_copy(out=s_spc[:], in_=s_w[:, 1024:1536]).then_inc(ve, 1)

    @block.gpsimd
    def _(g):
        g.wait_ge(st, 48)
        g.collective_compute("AllReduce", OP.add, replica_groups=ALL8,
                             ins=[part.opt()], outs=[ptot.opt()]).then_inc(cc)


def build_ternchain_B(ctx, A):
    nc = ctx.nc
    w1s, w2s, thrd = A["w1s"], A["w2s"], A["thrd"]
    ptot, cc = A["ptot"], A["cc"]
    t1, t2, w1tg, w2tg = A["t1"], A["t2"], A["w1tg"], A["w2tg"]
    NTOT = float(1024 * 4096)
    es = ctx.ges
    sb = lambda n, *a: es.enter_context(nc.sbuf_tensor(_u(n), *a))
    _semhs = []
    def sem(n):
        h = es.enter_context(nc.semaphore(_u(n)))
        _semhs.append(h)
        return h
    s_w = sb("tcb_w", [128, 2 * 512], F32)
    s_le = sb("tcb_le", [128, 512], F32)
    s_t = sb("tcb_t", [128, 2 * 512], BF16)
    s_p = sb("tcb_p", [1, 8], F32)
    s_st = sb("tcb_st", [1, 32], F32)
    s_throw = sb("tcb_throw", [1, 512], F32)
    s_thr = sb("tcb_thr", [128, 4], F32)
    s_onerow = sb("tcb_onerow", [1, 128], F32)
    s_spc = sb("tcb_spc", [128, 1024], F32)
    ld = sem("ld"); ve = sem("ve"); vc = sem("vc"); st = sem("st")
    ldw = [sem("lw0"), sem("lw1")]
    ctx.clean(_semhs)
    block = ctx.block
    NQ = 8

    @block.sync
    def _(sy):
        sy.wait_ge(cc, 1)
        sy.dma_start(out=s_p[0:1, 0:2], in_=ptot).then_inc(ld, 16)
        sy.wait_ge(ve, 1)
        sy.dma_start(out=thrd, in_=s_throw[:]).then_inc(st, 16)
        sy.wait_ge(st, 16)
        sy.dma_start(out=s_thr[:],
                     in_=thrd[0:1, :].rearrange("a (b p) -> (a p) b", p=128),
                     allow_slow_non_contiguous=True).then_inc(ld, 16)
        def t_store(j):
            tdst = t1 if j < NQ else t2
            sy.wait_ge(vc, j + 1)
            sy.dma_start(out=tdst[:, (j % NQ) * 512:(j % NQ + 1) * 512],
                         in_=s_t[:, (j % 2) * 512:(j % 2 + 1) * 512]
                         ).then_inc(st, 16)
        for idx in range(2 * NQ):
            if idx >= 2:
                sy.wait_ge(vc, idx - 1)
            wsrc = w1s if idx < NQ else w2s
            q = idx % NQ
            sy.dma_start(out=s_w[:, (idx % 2) * 512:(idx % 2 + 1) * 512],
                         in_=wsrc[:, q * 512:(q + 1) * 512]
                         ).then_inc(ldw[idx % 2], 16)
            if idx >= 1:
                t_store(idx - 1)
        t_store(2 * NQ - 1)

    @block.vector
    def _(v):
        v.memset(s_onerow[:], 1.0)
        v.memset(s_spc[:], 0.0)
        v.wait_ge(ld, 16)
        SP = lambda: v.tensor_copy(out=s_spc[:, 0:512], in_=s_spc[:, 512:1024])
        v.tensor_scalar(out=s_st[0:1, 0:1], in0=s_p[0:1, 0:1], scalar1=1.0 / NTOT,
                        scalar2=None, op0=OP.mult)
        v.tensor_scalar(out=s_st[0:1, 4:5], in0=s_p[0:1, 1:2], scalar1=1.0 / NTOT,
                        scalar2=None, op0=OP.mult)
        SP()
        v.tensor_scalar(out=s_st[0:1, 8:9], in0=s_st[0:1, 0:1], scalar1=1e-8,
                        scalar2=0.5, op0=OP.max, op1=OP.mult)
        v.tensor_scalar(out=s_st[0:1, 12:13], in0=s_st[0:1, 4:5], scalar1=1e-8,
                        scalar2=0.5, op0=OP.max, op1=OP.mult)
        SP()
        v.tensor_scalar(out=s_st[0:1, 16:17], in0=s_st[0:1, 8:9], scalar1=-1.0,
                        scalar2=None, op0=OP.mult)
        v.tensor_scalar(out=s_st[0:1, 20:21], in0=s_st[0:1, 12:13], scalar1=-1.0,
                        scalar2=None, op0=OP.mult)
        SP()
        for i, sl in enumerate([8, 16, 12, 20]):
            v.tensor_scalar(out=s_throw[0:1, i * 128:(i + 1) * 128],
                            in0=s_onerow[:], scalar1=s_st[0:1, sl:sl + 1],
                            scalar2=None, op0=OP.mult)
        v.tensor_copy(out=s_spc[0:1, 0:512], in_=s_throw[0:1, 0:512])
        SP().then_inc(ve, 1)
        v.wait_ge(ld, 32)
        for m in range(2):
            for q in range(NQ):
                idx = m * NQ + q
                v.wait_ge(ldw[idx % 2], 16 * (idx // 2 + 1))
                if idx >= 2:
                    v.wait_ge(st, 16 * idx)
                wv = s_w[:, (idx % 2) * 512:(idx % 2 + 1) * 512]
                tv = s_t[:, (idx % 2) * 512:(idx % 2 + 1) * 512]
                v.tensor_scalar(out=s_le[:], in0=wv, scalar1=s_thr[:, 2 * m + 1:2 * m + 2],
                                scalar2=None, op0=OP.is_le)
                v.scalar_tensor_tensor(out=tv, in0=wv, scalar=s_thr[:, 2 * m:2 * m + 1],
                                       in1=s_le[:], op0=OP.is_ge, op1=OP.subtract)
                v.tensor_copy(out=s_spc[:, 0:512], in_=tv)
                v.tensor_copy(out=s_spc[:, 0:64], in_=tv[:, 448:512]).then_inc(vc, 1)

    @block.gpsimd
    def _(g):
        g.wait_ge(st, 16 + 16 * 2 * NQ)
        g.collective_compute("AllGather", OP.bypass, replica_groups=ALL8,
                             ins=[t1.opt()], outs=[w1tg.opt()]).then_inc(cc)
        g.collective_compute("AllGather", OP.bypass, replica_groups=ALL8,
                             ins=[t2.opt()], outs=[w2tg.opt()]).then_inc(cc)


def build_mega(nphases=99):
    nc = _mk("mega")
    EI = lambda n, shp, dt: nc.dram_tensor(n, shp, dt, kind="ExternalInput").ap()
    IN = lambda n, shp, dt: nc.dram_tensor(n, shp, dt, kind="Internal").ap()
    xT = EI("xT", [D, T], F32)
    wqkvT = EI("wqkvT", [D, 1536], F32)
    bqkv = EI("bqkv", [128, 8], F32)
    bv_row = EI("bv_row", [1, 512], F32)
    owT = EI("owT", [8, 128, 512], F32)
    owb = EI("owb", [128, 8], F32)
    ln1p = EI("ln1p", [128, 16], F32)
    lin1_wT = EI("lin1_wT", [32, 128, 1024], F32)
    b1c = EI("b1c", [128, 32], F32)
    lin2_wT = EI("lin2_wT", [8, 128, 4096], F32)
    b2c = EI("b2c", [128, 8], F32)
    ln2p = EI("ln2p", [128, 16], F32)
    sw = EI("sw", [D, 8], F32)
    swb = EI("swb", [128, 8], F32)
    w8t = EI("w8t", [128, 128], F32)
    ident = EI("ident", [128, 128], F32)
    sel = EI("sel", [8, 1024], F32)
    mw1 = EI("mw1", [8, 32, 128, 1024], F32)
    mb1 = EI("mb1", [128, 256], F32)
    mw2 = EI("mw2", [8, 8, 128, 4096], F32)
    mb2 = EI("mb2", [128, 64], F32)
    qkvT1w = EI("qkvT1w", [D, 1536], F32)
    qkv1t = EI("qkv1t", [8, 128, 1024], F32)
    w_outT1 = EI("w_outT1", [512, D], F32)
    bout1 = EI("bout1", [128, 8], F32)
    l1masks = EI("l1masks", [128, 4 * 20 * 512], BF16)
    fb1 = EI("fb1", [128, 32], F32)
    fb2 = EI("fb2", [128, 8], F32)
    w1s = EI("w1s", [128, 4096], F32)
    w2s = EI("w2s", [128, 4096], F32)
    outT = nc.dram_tensor("outT", [D, T], F32, kind="ExternalOutput").ap()
    xn0f = IN("xn0f", [D, T], F32)
    xn0b = IN("xn0b", [D, T], BF16)
    xg = IN("xg", [2 * D, T], BF16)
    postg = IN("postg_i", [512, S], BF16)
    rsin = IN("rsin", [2 * D, T], F32)
    rsout = IN("rsout", [D, T], F32)
    dumpb = IN("dumpb", [D, S], BF16)
    xpf = IN("xpf", [D, T], F32)
    xpb = IN("xpb", [D, T], BF16)
    hbuf = IN("hbuf", [4096, T], BF16)
    fff = IN("fff", [D, T], F32)
    y0f = IN("y0f", [D, T], F32)
    moef = IN("moef_i", [D, T], BF16)
    x1f = IN("x1f", [D, T], F32)
    xn1b = IN("xn1b", [D, T], BF16)
    qTb = IN("qTb", [512, T], BF16)
    kpart = IN("kpart", [512, T], BF16)
    vtokb = IN("vtokb", [T, 512], BF16)
    kg = IN("kg", [2 * 512, T], BF16)
    vg = IN("vg", [S, 512], BF16)
    part = IN("part_i", [1, 2], F32)
    ptot = IN("ptot", [1, 2], F32)
    t1 = IN("t1_i", [128, 4096], BF16)
    t2 = IN("t2_i", [128, 4096], BF16)
    w1tg = IN("w1tg", [D, 4096], BF16)
    w2tg = IN("w2tg", [D, 4096], BF16)
    attn1 = IN("attn1_i", [D, T], F32)
    h1 = IN("h1", [4096, T], BF16)
    ffn1 = IN("ffn1", [D, T], F32)
    rowd = IN("rowd", [1, 256], F32)
    thrd = IN("thrd", [1, 512], F32)

    with ExitStack() as oes:
        bar = oes.enter_context(nc.semaphore("mega_bar"))
        cc_ar = oes.enter_context(nc.semaphore("cc_res"))
        assert cc_ar.num == bar.num + 1
        block = oes.enter_context(nc.Block())
        ctx = MegaCtx(nc, block, bar, reserved=[cc_ar])
        _ph = [0]
        def _go():
            _ph[0] += 1
            return _ph[0] <= nphases
        if _go():
            build_ln(ctx=ctx, A=dict(x=xT, xnf=xn0f, xnb=xn0b))
        if _go():
            _cc_phase(ctx, [("AllGather", OP.bypass, PAIRS, xn0b, xg)])
        def xn_slice(t0, w):
            r = t0 // T
            return xg[r * D:(r + 1) * D][:, t0 % T: t0 % T + w]
        if _go():
            build_l0_attn(ctx=ctx, A=dict(xn_slice=xn_slice, wqkvT=wqkvT, bqkv=bqkv,
                                          bv_row=bv_row, postg=postg))
        def op_out_slice(o, tb):
            return dumpb[o * 128:(o + 1) * 128, tb * 512:(tb + 1) * 512]
        def op_outf_slice(o, tb):
            h_ = tb // 4
            return rsin[h_ * D + o * 128: h_ * D + (o + 1) * 128,
                        (tb % 4) * 512:(tb % 4 + 1) * 512]
        if _go():
            build_gemm("outproj", 512, D, S, act="identity", scale=1.0 / 64.0, out_f32=True,
                       nbias=8, ctx=ctx,
                       A=dict(wTt=owT, xb=postg, bias=owb, no_bf16=True,
                              outf_slice=op_outf_slice))
        if _go():
            _cc_phase(ctx, [("ReduceScatter", OP.add, PAIRS,
                             rsin.rearrange("(h o) t -> h (o t)", h=2),
                             rsout.rearrange("(a o) t -> a (o t)", a=1))])
        if _go():
            build_addln("uln1", 2, gb=True, out_f32=True, ctx=ctx,
                        A=dict(xs=[xn0f, rsout], params=ln1p, xnf=xpf, xnb=xpb, sumf=None))
        if _go():
            build_gemm("lin1", D, 4096, T, act="gelu", nbias=32, ctx=ctx,
                       A=dict(wTt=lin1_wT, xb=xpb, bias=b1c,
                              out_slice=lambda o, tb: hbuf[o * 128:(o + 1) * 128, tb * 512:(tb + 1) * 512],
                              outf_slice=None))
        if _go():
            build_gemm("lin2", 4096, D, T, act="identity", out_f32=True, nbias=8, ctx=ctx,
                       A=dict(wTt=lin2_wT, xb=hbuf, bias=b2c, no_bf16=True,
                              outf_slice=lambda o, tb: fff[o * 128:(o + 1) * 128, tb * 512:(tb + 1) * 512]))
        if _go():
            build_addln("ln2", 2, gb=True, out_f32=True, ctx=ctx,
                        A=dict(xs=[xpf, fff], params=ln2p, xnf=y0f, xnb=None, sumf=None))
        if _go():
            build_moe(ctx=ctx, A=dict(xnb=xn0b, xnf=xn0f, sw=sw, swb=swb, w8t=w8t,
                                      ident=ident, sel=sel, w1=mw1, b1c=mb1, w2=mw2,
                                      b2c=mb2, moef=moef))
        if _go():
            ctx.begin_group()
            build_ternchain_A(ctx, A=dict(w1s=w1s, w2s=w2s, rowd=rowd,
                                          part=part, ptot=ptot, cc=cc_ar))
            build_addln("lnx1", 3, gb=False, out_f32=False, out_sum=True,
                        dtypes=("bf16", "f32", "f32"), ctx=ctx,
                        A=dict(xs=[moef, xT, y0f], params=None, xnf=None, xnb=xn1b, sumf=x1f))
            ctx.end_group()
        def qk_out_slice(o, tb):
            dst = qTb if o < 4 else kpart
            oo = o % 4
            return dst[oo * 128:(oo + 1) * 128, tb * 512:(tb + 1) * 512]
        if _go():
            ctx.begin_group()
            build_gemm("qkv1", D, 1024, T, act="identity", ctx=ctx,
                       A=dict(wTt=qkv1t, xb=xn1b, bias=None,
                              out_slice=qk_out_slice, outf_slice=None))
            build_vtok(ctx=ctx, A=dict(xnb=xn1b, wvT=qkvT1w[:, 1024:1536], vtok=vtokb))
            ctx.end_group()
        if _go():
            _cc_phase(ctx, [("AllGather", OP.bypass, PAIRS, kpart, kg),
                            ("AllGather", OP.bypass, PAIRS, vtokb, vg)])
        def k_slice(c, j0, w):
            hh = j0 // T
            return kg[hh * 512 + c * 128: hh * 512 + (c + 1) * 128,
                      j0 % T: j0 % T + w]
        if _go():
            ctx.begin_group()
            build_ternchain_B(ctx, A=dict(w1s=w1s, w2s=w2s, thrd=thrd, ptot=ptot,
                                          cc=cc_ar, t1=t1, t2=t2, w1tg=w1tg, w2tg=w2tg))
            build_l1_attn(False, uniform=True, ctx=ctx,
                          A=dict(qT=qTb, k_slice=k_slice, vtok=vg, w_outT=w_outT1,
                                 bout=bout1, masks=l1masks, attn1=attn1))

            @ctx.block.gpsimd
            def _(g):
                g.wait_ge(cc_ar, 3)
            ctx.end_group()
        if _go():
            build_gemm("l1ffn1", D, 4096, T, act="gelu", nbias=32, w_bf16=True, ctx=ctx,
                       A=dict(wT=w1tg, xb=xn1b, bias=fb1,
                              out_slice=lambda o, tb: h1[o * 128:(o + 1) * 128, tb * 512:(tb + 1) * 512],
                              outf_slice=None))
        if _go():
            build_gemm("l1ffn2", 4096, D, T, act="identity", out_f32=True, nbias=8,
                       w_bf16=True, ctx=ctx,
                       A=dict(wT=w2tg.rearrange("p (a c) -> (p a) c", a=4), xb=h1, bias=fb2,
                              fuse_add=[x1f, attn1], fuse_out=outT))
    return nc


def _uniform_masks(odd):
    m = np.zeros((128, 4 * 20 * 512), np.float32)
    ar = np.arange(512)
    for ib in range(4):
        for mi in range(20):
            sl = m[:, (ib * 20 + mi) * 512:(ib * 20 + mi + 1) * 512]
            if not odd:
                if mi < 4:
                    d = 128 * mi
                    for pj in range(128):
                        sl[pj, :] = (ar >= pj + d)
            else:
                if mi < 16:
                    sl[:, :] = 1.0
                else:
                    d = 128 * (mi - 16)
                    for pj in range(128):
                        sl[pj, :] = (ar >= pj + d)
    return np.ascontiguousarray(m.astype(bf16))


def _tile_wT(wT):
    K, M = wT.shape
    NK, NO = K // 128, M // 128
    return np.ascontiguousarray(
        wT.reshape(NK, 128, NO, 128).transpose(2, 1, 0, 3).reshape(NO, 128, NK * 128))


def kernel(**inputs):
    gi = lambda n: np.asarray(inputs[n])
    x = gi("L0_attn_src").astype(np.float32)
    xsh = shard_x(x)
    nc = _cached("MEGA", build_mega)

    W = gi("L0_attn_in_proj_w")
    bqkv_full = gi("L0_attn_in_proj_b")
    out_wT_full = np.ascontiguousarray(gi("L0_attn_out_w").T)
    boutc = _cols(gi("L0_attn_out_b"), 8)
    zbout = np.zeros_like(boutc)
    ln1p = np.concatenate([_cols(gi("L0_attn_ln1_g"), 8), _cols(gi("L0_attn_ln1_b"), 8)], axis=1)
    ln2p = np.concatenate([_cols(gi("L0_attn_ln2_g"), 8), _cols(gi("L0_attn_ln2_b"), 8)], axis=1)
    lin1_wT = _tile_wT(gi("L0_attn_lin1_w").T)
    lin2_wT = _tile_wT(gi("L0_attn_lin2_w").T)
    b1c = _cols(gi("L0_attn_lin1_b"), 32)
    b2c = _cols(gi("L0_attn_lin2_b"), 8)
    sw = gi("L0_ffn_switch_w").astype(np.float32)
    swb_t = np.ascontiguousarray(np.tile(gi("L0_ffn_switch_b").astype(np.float32), (128, 1)))
    w8t = np.ascontiguousarray(np.tile(np.arange(8, 0, -1, dtype=np.float32), (128, 16)))
    ident = np.eye(128, dtype=np.float32)
    sel = np.ascontiguousarray(np.concatenate(
        [np.eye(8, dtype=np.float32)[:, e:e + 1] @ np.ones((1, 128), np.float32)
         for e in range(8)], axis=1))
    mw1 = np.ascontiguousarray(gi("L0_ffn_w1").astype(np.float32)
                               .reshape(8, 8, 128, 32, 128).transpose(0, 3, 2, 1, 4)
                               .reshape(8, 32, 128, 1024))
    mw2 = np.ascontiguousarray(gi("L0_ffn_w2").astype(np.float32)
                               .reshape(8, 32, 128, 8, 128).transpose(0, 3, 2, 1, 4)
                               .reshape(8, 8, 128, 4096))
    mb1 = np.ascontiguousarray(gi("L0_ffn_b1").astype(np.float32).reshape(256, 128).T)
    mb2 = np.ascontiguousarray(gi("L0_ffn_b2").astype(np.float32).reshape(64, 128).T)
    qkvT1w = np.ascontiguousarray(gi("L1_attn_w_qkv").T)
    qkv1t = _tile_wT(qkvT1w[:, 0:1024])
    w_outT1 = np.ascontiguousarray(gi("L1_attn_w_out").T)
    bout1 = _cols(gi("L1_attn_b_out"), 8)
    fb1 = _cols(gi("L1_ffn_b1"), 32)
    fb2 = _cols(gi("L1_ffn_b2"), 8)
    W1T = np.ascontiguousarray(gi("L1_ffn_W1").astype(np.float32).T)
    W2T = np.ascontiguousarray(gi("L1_ffn_W2").astype(np.float32).T)
    me = _uniform_masks(False)
    mo = _uniform_masks(True)

    maps = []
    for c in range(NCORES):
        b, hh = c // 2, c % 2
        o = 512 * hh
        wqkvT = np.ascontiguousarray(np.concatenate(
            [W[o:o + 512], W[1024 + o:1024 + o + 512], W[2048 + o:2048 + o + 512]], axis=0).T)
        bq = np.ascontiguousarray(np.concatenate(
            [bqkv_full[o:o + 512], bqkv_full[1024 + o:1024 + o + 512]]).reshape(8, 128).T.astype(np.float32))
        bv = np.ascontiguousarray(
            bqkv_full[2048 + o:2048 + o + 512].reshape(1, 512).astype(np.float32))
        maps.append(dict(
            xT=xsh[c], wqkvT=wqkvT, bqkv=bq, bv_row=bv,
            owT=_tile_wT(out_wT_full[o:o + 512, :]),
            owb=(boutc if hh == 0 else zbout),
            ln1p=ln1p, lin1_wT=lin1_wT, b1c=b1c, lin2_wT=lin2_wT, b2c=b2c, ln2p=ln2p,
            sw=sw, swb=swb_t, w8t=w8t, ident=ident, sel=sel,
            mw1=mw1, mb1=mb1, mw2=mw2, mb2=mb2,
            qkvT1w=qkvT1w, qkv1t=qkv1t, w_outT1=w_outT1, bout1=bout1,
            l1masks=(me if hh == 0 else mo), fb1=fb1, fb2=fb2,
            w1s=np.ascontiguousarray(W1T[c * 128:(c + 1) * 128]),
            w2s=np.ascontiguousarray(W2T[c * 512:(c + 1) * 512].reshape(128, 4096)),
        ))
    res = run_bass_kernel_spmd(nc, maps, core_ids=list(range(NCORES)))
    out = np.empty((B, S, D), np.float32)
    for c in range(NCORES):
        b, hh = c // 2, c % 2
        out[b, hh * T:(hh + 1) * T, :] = res.results[c]["outT"].T
    return out


# version 16 (final)
# speedup vs baseline: 1.0930x; 1.0930x over previous
import numpy as np
import ml_dtypes
from contextlib import ExitStack
import concourse.bass as bass
import concourse.mybir as mybir
from concourse.bass_utils import run_bass_kernel_spmd
from concourse.alu_op_type import AluOpType as OP

F32 = mybir.dt.float32
BF16 = mybir.dt.bfloat16
AF = mybir.ActivationFunctionType
AX = mybir.AxisListType

NCORES = 8
D = 1024
S = 4096
B = 4
T = 2048
NCH = D // 128
LN_EPS = 1e-5

bf16 = ml_dtypes.bfloat16





class MegaCtx:
    def __init__(self, nc, block, bar, reserved=()):
        self.nc = nc
        self.block = block
        self.bar = bar
        self.nbar = 0
        self.reserved = list(reserved)
        self.ges = None
        self._joined = False

    def begin_group(self):
        self.ges = ExitStack()
        self._joined = False

    def end_group(self):
        es, self.ges = self.ges, None
        es.close()

    def barrier(self):
        self.nbar += 1
        target = 5 * self.nbar
        bar = self.bar
        for eng_name in ("sync", "gpsimd", "tensor", "scalar", "vector"):
            dec = getattr(self.block, eng_name)

            def body(eng, bar=bar, target=target):
                eng.drain()
                eng.sem_inc(bar, 1)
                eng.wait_ge(bar, target)
            dec(body)

    def clean(self, semhs):
        if self.ges is not None and self._joined:
            return
        if self.ges is not None:
            self._joined = True
        first = self.nbar == 0
        self.barrier()
        lo = self.bar.num + 1
        if not first and self.reserved:
            lo = max([self.bar.num] + [h.num for h in self.reserved]) + 1
        rng = range(lo, self.nc._kernel_sem_range.stop)

        @self.block.gpsimd
        def _(g):
            g.dma_reset(rng)
            g.sem_clear(rng)
        self.barrier()


def _enter(ctx, es, name, semhs):
    if ctx is None:
        nc = _mk(name)
        return nc, None
    return ctx.nc, ctx.block


_UNAME_CTR = [0]

def _u(n):
    _UNAME_CTR[0] += 1
    return f"{n}__{_UNAME_CTR[0]}"


def _clean_sems(nc, hs):
    nums = sorted(h.num for h in hs)
    rng = range(nums[0], nums[-1] + 1)
    nc.gpsimd.dma_reset(rng)
    nc.gpsimd.sem_clear(rng)
    nc.all_engine_barrier()

def _mk(nc_name):
    return bass.Bass("TRN2", target_bir_lowering=False, detect_race_conditions=False)


def build_ln(Tn=T, out_f32=True, ctx=None, A=None):
    if ctx is None:
        nc = _mk("ln")
        A = dict(x=nc.dram_tensor("xT", [D, Tn], F32, kind="ExternalInput").ap(),
                 xnf=nc.dram_tensor("xnf", [D, Tn], F32, kind="ExternalOutput").ap() if out_f32 else None,
                 xnb=nc.dram_tensor("xnb", [D, Tn], BF16, kind="ExternalOutput").ap())
    else:
        nc = ctx.nc
    x, xnf, xnb = A["x"], A.get("xnf"), A["xnb"]
    NB = Tn // 512
    KOUT = NCH * (2 if out_f32 else 1)

    with ExitStack() as es:
        sb = lambda n, *a: es.enter_context(nc.sbuf_tensor(_u(n), *a))
        ps = lambda n, *a: es.enter_context(nc.psum_tensor(_u(n), *a))
        _semhs = []
        def sem(n):
            h = es.enter_context(nc.semaphore(_u(n)))
            _semhs.append(h)
            return h
        s_x = sb("s_x", [128, NCH * Tn], F32)
        s_sq = sb("s_sq", [128, NCH * 512], F32)
        s_stat = sb("s_stat", [1, 7 * 512], F32)
        s_xnf = sb("s_xnf", [128, NCH * 512], F32)
        s_xnb = sb("s_xnb", [128, NCH * 512], BF16)
        s_one = sb("s_one", [128, 1], F32)
        s_onerow = sb("s_onerow", [1, 128], F32)
        s_spcv = sb("s_spcv", [128, 512], F32)
        s_spcs = sb("s_spcs", [128, 512], F32)
        p_sum = ps("p_sum", [1, 512], F32)
        p_sq = ps("p_sq", [1, 512], F32)
        p_r = ps("p_r", [128, 512], F32)
        p_mr = ps("p_mr", [128, 512], F32)
        ld = sem("ld"); sq = sem("sq"); te = sem("te"); ve = sem("ve")
        se = sem("se"); st = sem("st"); apd = sem("apd")
        if ctx is None:
            _clean_sems(nc, _semhs)
            block = es.enter_context(nc.Block())
        else:
            ctx.clean(_semhs)
            block = ctx.block
        SL = lambda i: s_stat[:, i * 512:(i + 1) * 512]

        @block.sync
        def _(sy):
            for c in range(NCH):
                sy.dma_start(out=s_x[:, c * Tn:(c + 1) * Tn],
                             in_=x[c * 128:(c + 1) * 128, :]).then_inc(ld, 16)

        @block.scalar
        def _(s):
            s.wait_ge(ld, 16 * NCH)
            for tb in range(NB):
                if tb > 0:
                    s.wait_ge(te, 3 * (tb - 1) + 2)
                for c in range(NCH):
                    xs = s_x[:, c * Tn + tb * 512: c * Tn + (tb + 1) * 512]
                    s.activation(out=s_sq[:, c * 512:(c + 1) * 512], in_=xs,
                                 func=AF.Square).then_inc(sq, 1)
                s.wait_ge(ve, 3 * tb + 2)
                s.activation(out=SL(2), in_=SL(0), func=AF.Square)
                s.activation(out=s_spcs[:], in_=s_x[:, 0:512], func=AF.Square).then_inc(se, 1)
                s.wait_ge(ve, 3 * tb + 3)
                s.activation(out=SL(4), in_=SL(3), func=AF.Sqrt)
                s.activation(out=s_spcs[:], in_=s_x[:, 0:512], func=AF.Square).then_inc(se, 1)

        @block.tensor
        def _(t):
            t.wait_ge(ld, 16 * NCH)
            t.wait_ge(ve, 1)
            for tb in range(NB):
                if tb > 0:
                    t.wait_ge(ve, 3 * tb - 1)
                for c in range(NCH):
                    xs = s_x[:, c * Tn + tb * 512: c * Tn + (tb + 1) * 512]
                    mm = t.matmul(p_sum[:], s_one[:], xs, start=(c == 0), stop=(c == NCH - 1))
                mm.then_inc(te, 1)
                for c in range(NCH):
                    t.wait_ge(sq, NCH * tb + c + 1)
                    mm = t.matmul(p_sq[:], s_one[:], s_sq[:, c * 512:(c + 1) * 512],
                                  start=(c == 0), stop=(c == NCH - 1))
                mm.then_inc(te, 1)
                t.wait_ge(ve, 3 * tb + 4)
                if tb > 0:
                    t.wait_ge(apd, tb)
                t.matmul(p_r[:], s_onerow[:], SL(5), start=True, stop=True)
                t.matmul(p_mr[:], s_onerow[:], SL(6), start=True, stop=True).then_inc(te, 1)

        @block.vector
        def _(v):
            v.memset(s_one[:], 1.0)
            v.memset(s_onerow[:], 1.0)
            v.nop().then_inc(ve, 1)
            for tb in range(NB):
                v.wait_ge(te, 3 * tb + 2)
                v.tensor_scalar(out=SL(0), in0=p_sum[:], scalar1=1.0 / D, scalar2=None, op0=OP.mult)
                v.tensor_scalar(out=SL(1), in0=p_sq[:], scalar1=1.0 / D, scalar2=None, op0=OP.mult)
                v.tensor_copy(out=s_spcv[:], in_=s_x[:, 0:512]).then_inc(ve, 1)
                v.wait_ge(se, 2 * tb + 1)
                v.scalar_tensor_tensor(out=SL(3), in0=SL(1), scalar=LN_EPS, in1=SL(2),
                                       op0=OP.add, op1=OP.subtract)
                v.tensor_copy(out=s_spcv[:], in_=s_x[:, 0:512]).then_inc(ve, 1)
                v.wait_ge(se, 2 * tb + 2)
                v.reciprocal(out=SL(5), in_=SL(4))
                v.tensor_copy(out=s_spcv[:], in_=s_x[:, 0:512])
                v.tensor_tensor(out=SL(6), in0=SL(0), in1=SL(5), op=OP.mult)
                v.tensor_copy(out=s_spcv[:], in_=s_x[:, 0:512]).then_inc(ve, 1)
                v.wait_ge(te, 3 * tb + 3)
                if tb > 0:
                    v.wait_ge(st, 16 * KOUT * tb)
                for c in range(NCH):
                    xs = s_x[:, c * Tn + tb * 512: c * Tn + (tb + 1) * 512]
                    o = s_xnf[:, c * 512:(c + 1) * 512]
                    v.tensor_tensor(out=o, in0=xs, in1=p_r[:], op=OP.mult)
                    v.tensor_tensor(out=o, in0=o, in1=p_mr[:], op=OP.subtract)
                    v.tensor_copy(out=s_xnb[:, c * 512:(c + 1) * 512], in_=o)
                v.tensor_copy(out=s_spcv[:], in_=s_xnb[:, (NCH - 1) * 512:NCH * 512])
                v.tensor_copy(out=s_spcv[:], in_=s_xnf[:, (NCH - 1) * 512:NCH * 512]).then_inc(apd, 1)

        @block.gpsimd
        def _(g):
            for tb in range(NB):
                g.wait_ge(apd, tb + 1)
                if out_f32:
                    for c in range(NCH):
                        g.dma_start(out=xnf[c * 128:(c + 1) * 128, tb * 512:(tb + 1) * 512],
                                    in_=s_xnf[:, c * 512:(c + 1) * 512]).then_inc(st, 16)
                if xnb is not None:
                    for c in range(NCH):
                        g.dma_start(out=xnb[c * 128:(c + 1) * 128, tb * 512:(tb + 1) * 512],
                                    in_=s_xnb[:, c * 512:(c + 1) * 512]).then_inc(st, 16)
            g.wait_ge(st, 16 * NB * KOUT)
    return nc


def shard_x(x):
    out = []
    for c in range(NCORES):
        b, h = c // 2, c % 2
        out.append(np.ascontiguousarray(x[b, h * T:(h + 1) * T, :].T))
    return out


def build_l0_attn(debug=False, ctx=None, A=None):
    if ctx is None:
        nc = _mk("l0attn")
        xn = nc.dram_tensor("xnb_full", [D, S], BF16, kind="ExternalInput").ap()
        A = dict(xn_slice=lambda t0, w: xn[:, t0:t0 + w],
                 wqkvT=nc.dram_tensor("wqkvT", [D, 1536], F32, kind="ExternalInput").ap(),
                 bqkv=nc.dram_tensor("bqkv", [128, 8], F32, kind="ExternalInput").ap(),
                 bv_row=nc.dram_tensor("bv_row", [1, 512], F32, kind="ExternalInput").ap(),
                 postg=nc.dram_tensor("postg", [512, S], BF16, kind="ExternalOutput").ap())
    else:
        nc = ctx.nc
    xn_slice = A["xn_slice"]
    wqkvT, bqkv, bv_row, postg = A["wqkvT"], A["bqkv"], A["bv_row"], A["postg"]
    if debug:
        assert ctx is None
        dbg_q = nc.dram_tensor("dbg_q", [128, 4 * S], BF16, kind="ExternalOutput").ap()
        dbg_k = nc.dram_tensor("dbg_k", [128, 4 * S], BF16, kind="ExternalOutput").ap()
        dbg_v = nc.dram_tensor("dbg_v", [128, 32 * 512], BF16, kind="ExternalOutput").ap()
        dbg_negM = nc.dram_tensor("dbg_negM", [128, 1], F32, kind="ExternalOutput").ap()
        dbg_vt = nc.dram_tensor("dbg_vt", [128, 32 * 128], BF16, kind="ExternalOutput").ap()
        dbg_EA = nc.dram_tensor("dbg_EA", [128, 2 * S], BF16, kind="ExternalOutput").ap()
        dbg_nm = nc.dram_tensor("dbg_nm", [1, 64], F32, kind="ExternalOutput").ap()
        dbg_nm2 = nc.dram_tensor("dbg_nm2", [1, 16], F32, kind="ExternalOutput").ap()


    NT = S // 512
    NST = S // 128

    with ExitStack() as es:
        sb = lambda n, *a: es.enter_context(nc.sbuf_tensor(_u(n), *a))
        ps = lambda n, *a: es.enter_context(nc.psum_tensor(_u(n), *a))
        _semhs = []
        def sem(n):
            h = es.enter_context(nc.semaphore(_u(n)))
            _semhs.append(h)
            return h
        s_xnq = sb("s_xnq", [128, 2 * NCH * 512], BF16)
        s_xnv = sb("s_xnv", [128, 2 * NCH * 128], BF16)
        s_w = sb("s_w", [128, NCH * 1536], BF16)
        s_bqkv = sb("s_bqkv", [128, 8], F32)
        s_bvrow = sb("s_bvrow", [1, 512], F32)
        s_bvbc = sb("s_bvbc", [128, 512], F32)
        s_q = sb("s_q", [128, 4 * S], BF16)
        s_k = sb("s_k", [128, 4 * S], BF16)
        s_v = sb("s_v", [128, NST * 512], BF16)
        s_sqsc = sb("s_sqsc", [128, S], BF16)
        s_nm = sb("s_nm", [1, 64], F32)
        s_nm2 = sb("s_nm2", [1, 16], F32)
        s_negM = sb("s_negM", [128, 1], F32)
        s_onerow = sb("s_onerow", [1, 128], F32)
        s_one = sb("s_one", [128, 1], F32)
        s_oneb = sb("s_oneb", [128, 1], BF16)
        s_EA = sb("s_EA", [128, 2 * S], BF16)
        s_EB = sb("s_EB", [128, 2 * S], BF16)
        s_acc = sb("s_acc", [128, 32], F32)
        s_cs = sb("s_cs", [128, 32], F32)
        s_spc = sb("s_spc", [128, 512], F32)
        s_spc64 = sb("s_spc64", [128, 64], F32)
        s_vt = sb("s_vt", [128, NST * 128], BF16)
        s_pg = sb("s_pg", [128, 2 * 2048], BF16)
        ps8 = ps("ps8", [128, 4096], F32)
        bank = lambda i: ps8[:, i * 512:(i + 1) * 512]
        p_post = ps8[:, 2048:4096]

        ld = sem("ld"); gld = sem("gld"); prj = sem("prj"); prje = sem("prje")
        qlp = [sem("ql0"), sem("ql1")]
        vlp = [sem("vl0"), sem("vl1")]
        stp = [sem("sp0"), sem("sp1")]
        vev = sem("vev"); nrm = sem("nrm"); nrs = sem("nrs"); misc = sem("misc")
        sc = sem("sc"); xpA = sem("xpA"); xpB = sem("xpB")
        cs = sem("cs"); pv = sem("pv"); gl = sem("gl"); st = sem("st")
        if ctx is None:
            _clean_sems(nc, _semhs)
            block = es.enter_context(nc.Block())
        else:
            ctx.clean(_semhs)
            block = ctx.block

        def sgidx(p, r, stile, i):
            return p * 384 + (0 if r == 0 else 256) + stile * (8 if r == 0 else 4) + i

        def pvidx(p, r, stile):
            return p * 64 + r * 32 + stile

        @block.sync
        def _(sy):
            sy.dma_start(out=s_bqkv[:], in_=bqkv).then_inc(ld, 16)
            sy.dma_start(out=s_bvrow[:], in_=bv_row).then_inc(ld, 16)
            for tb in range(NT):
                if tb >= 2:
                    sy.wait_ge(prj, 1 + 8 * (tb - 1))
                par = tb % 2
                sy.dma_start(out=s_xnq[:].rearrange("p (b c t) -> p b c t", b=2, c=NCH)[:, par],
                             in_=xn_slice(tb * 512, 512).rearrange("(c p) t -> p c t", p=128)
                             ).then_inc(qlp[par], 16)
            for stile in range(NST):
                if stile >= 2:
                    sy.wait_ge(prj, 64 + stile)
                par = stile % 2
                sy.dma_start(out=s_xnv[:].rearrange("p (b c t) -> p b c t", b=2, c=NCH)[:, par],
                             in_=xn_slice(stile * 128, 128).rearrange("(c p) t -> p c t", p=128)
                             ).then_inc(vlp[par], 16)

        @block.gpsimd
        def _(g):
            for c in range(NCH):
                g.dma_start(out=s_w[:, c * 1536:(c + 1) * 1536],
                            in_=wqkvT[c * 128:(c + 1) * 128, :]).then_inc(gld, 16)
            for p in range(4):
                for r in range(2):
                    g.wait_ge(gl, p * 2 + r + 1)
                    pr2 = ((p * 2 + r) % 2) * 2048
                    g.dma_start(out=postg[p * 128:(p + 1) * 128, r * 2048:(r + 1) * 2048],
                                in_=s_pg[:, pr2:pr2 + 2048]).then_inc(stp[(p * 2 + r) % 2], 16)
            g.wait_ge(stp[0], 16 * 4)
            g.wait_ge(stp[1], 16 * 4)
            if debug:
                g.wait_ge(gl, 8)
                g.dma_start(out=dbg_q, in_=s_q[:]).then_inc(st, 16)
                g.dma_start(out=dbg_k, in_=s_k[:]).then_inc(st, 16)
                g.dma_start(out=dbg_v, in_=s_v[:]).then_inc(st, 16)
                g.dma_start(out=dbg_negM, in_=s_negM[:]).then_inc(st, 16)
                g.dma_start(out=dbg_vt, in_=s_vt[:]).then_inc(st, 16)
                g.dma_start(out=dbg_EA, in_=s_EA[:]).then_inc(st, 16)
                g.dma_start(out=dbg_nm, in_=s_nm[:]).then_inc(st, 16)
                g.dma_start(out=dbg_nm2, in_=s_nm2[:]).then_inc(st, 16)
                g.wait_ge(st, 16 * 8)

        @block.tensor
        def _(t):
            t.wait_ge(ld, 16 * 2)
            t.wait_ge(gld, 16 * NCH)
            t.wait_ge(misc, 1)
            t.matmul(bank(0), s_onerow[:], s_bvrow[:], start=True, stop=True).then_inc(prj, 1)
            for tb in range(NT):
                t.wait_ge(qlp[tb % 2], 16 * (tb // 2 + 1))
                par = tb % 2
                for j in range(8):
                    g2 = tb * 8 + j
                    if g2 == 0:
                        t.wait_ge(misc, 2)
                    if tb >= 1:
                        t.wait_ge(prje, (tb - 1) * 8 + j + 1)
                    for c in range(NCH):
                        mm = t.matmul(bank(j),
                                      s_w[:, c * 1536 + j * 128: c * 1536 + (j + 1) * 128],
                                      s_xnq[:, par * NCH * 512 + c * 512: par * NCH * 512 + (c + 1) * 512],
                                      start=(c == 0), stop=(c == NCH - 1))
                    mm.then_inc(prj, 1)
            for stile in range(NST):
                t.wait_ge(vlp[stile % 2], 16 * (stile // 2 + 1))
                par = stile % 2
                if stile < 2:
                    t.wait_ge(prje, 64)
                else:
                    t.wait_ge(vev, stile - 1)
                for c in range(NCH):
                    mm = t.matmul(bank(par),
                                  s_xnv[:, par * NCH * 128 + c * 128: par * NCH * 128 + (c + 1) * 128],
                                  s_w[:, c * 1536 + 1024: c * 1536 + 1536],
                                  start=(c == 0), stop=(c == NCH - 1))
                mm.then_inc(prj, 1)
            for pt in range(8):
                t.wait_ge(nrs, pt + 1)
                for tb in range(NT):
                    n = pt * NT + tb
                    if n >= 2:
                        t.wait_ge(nrm, n - 1)
                    t.matmul(bank(2 + n % 2)[0:1, :], s_oneb[:],
                             s_sqsc[:, tb * 512:(tb + 1) * 512],
                             start=True, stop=True).then_inc(prj, 1)
            t.wait_ge(nrm, 65)
            t.wait_ge(vev, NST)
            t.matmul(bank(0)[:, 0:1], s_onerow[:], s_nm2[:, 12:13], start=True, stop=True).then_inc(prj, 1)
            for p in range(4):
                for r in range(2):
                    blks = list(range(8)) if r == 0 else list(range(4, 8))
                    pvblks = list(range(4)) if r == 0 else list(range(4, 8))
                    for stile in range(NST + 1):
                        if stile < NST:
                            for i, blk in enumerate(blks):
                                g = sgidx(p, r, stile, i)
                                if g == 0:
                                    t.wait_ge(nrm, 66)
                                if g >= 2:
                                    t.wait_ge(xpA, g - 1)
                                    t.wait_ge(xpB, g - 1)
                                par = g % 2
                                t.matmul(bank(par), s_k[0:64, p * S + stile * 128: p * S + (stile + 1) * 128],
                                         s_q[0:64, p * S + blk * 512: p * S + (blk + 1) * 512],
                                         start=True, stop=True, tile_position=(0, 0))
                                t.matmul(bank(2 + par), s_k[64:128, p * S + stile * 128: p * S + (stile + 1) * 128],
                                         s_q[64:128, p * S + blk * 512: p * S + (blk + 1) * 512],
                                         start=True, stop=True, tile_position=(64, 0)).then_inc(sc, 1)
                        if stile >= 1:
                            st1 = stile - 1
                            epar = st1 % 2
                            for bi, blk in enumerate(pvblks):
                                if bi == 0:
                                    if r == 0:
                                        t.wait_ge(cs, p * 32 + st1 + 1)
                                    else:
                                        t.wait_ge(xpA, sgidx(p, 1, st1, 3) + 1)
                                        t.wait_ge(xpB, sgidx(p, 1, st1, 3) + 1)
                                    if st1 == 0:
                                        t.wait_ge(gl, p * 2 + r)
                                off = (blk - (0 if r == 0 else 4)) * 512
                                t.matmul(p_post[0:64, off:off + 512],
                                         s_vt[:, st1 * 128: st1 * 128 + 64],
                                         s_EA[:, epar * S + blk * 512: epar * S + (blk + 1) * 512],
                                         start=(st1 == 0), stop=(st1 == NST - 1),
                                         tile_position=(0, 0))
                                mm = t.matmul(p_post[64:128, off:off + 512],
                                              s_vt[:, st1 * 128 + 64: st1 * 128 + 128],
                                              s_EB[:, epar * S + blk * 512: epar * S + (blk + 1) * 512],
                                              start=(st1 == 0), stop=(st1 == NST - 1),
                                              tile_position=(0, 64))
                            mm.then_inc(pv, 1)

        @block.scalar
        def _(s):
            for tb in range(NT):
                for j in range(8):
                    g2 = tb * 8 + j
                    s.wait_ge(prj, g2 + 2)
                    dst = s_q if j < 4 else s_k
                    pr = j % 4
                    s.activation(out=dst[:, pr * S + tb * 512: pr * S + (tb + 1) * 512],
                                 in_=bank(j), func=AF.Identity,
                                 bias=s_bqkv[:, j:j + 1]).then_inc(prje, 1)
            for pt in range(8):
                if pt >= 1:
                    s.wait_ge(prj, 97 + 8 * pt)
                src = s_q if pt < 4 else s_k
                pr = pt % 4
                s.activation(out=s_sqsc[:], in_=src[:, pr * S:(pr + 1) * S],
                             func=AF.Square).then_inc(nrs, 1)
            for p in range(4):
                for r in range(2):
                    blks = list(range(8)) if r == 0 else list(range(4, 8))
                    for stile in range(NST):
                        epar = stile % 2
                        apar = epar * 16
                        for i, blk in enumerate(blks):
                            g = sgidx(p, r, stile, i)
                            if i == 0:
                                gate = pvidx(p, r, stile) - 1
                                if gate > 0:
                                    s.wait_ge(pv, gate)
                            s.wait_ge(sc, g + 1)
                            par = g % 2
                            if r == 0:
                                s.activation(out=s_EA[:, epar * S + blk * 512: epar * S + (blk + 1) * 512],
                                             in_=bank(par), func=AF.Exp, bias=s_negM[:], scale=0.125,
                                             accum_out=s_acc[:, apar + blk:apar + blk + 1]).then_inc(xpA, 1)
                                s.activation(out=s_EB[:, epar * S + blk * 512: epar * S + (blk + 1) * 512],
                                             in_=bank(2 + par), func=AF.Exp, bias=s_negM[:], scale=0.125,
                                             accum_out=s_acc[:, apar + 8 + blk:apar + 9 + blk]).then_inc(xpB, 1)
                            else:
                                s.activation(out=s_EA[:, epar * S + blk * 512: epar * S + (blk + 1) * 512],
                                             in_=bank(par), func=AF.Exp, bias=s_negM[:],
                                             scale=0.125).then_inc(xpA, 1)
                                s.activation(out=s_EB[:, epar * S + blk * 512: epar * S + (blk + 1) * 512],
                                             in_=bank(2 + par), func=AF.Exp, bias=s_negM[:],
                                             scale=0.125).then_inc(xpB, 1)
                    s.wait_ge(pv, pvidx(p, r, NST - 1) + 1)
                    if p * 2 + r >= 2:
                        s.wait_ge(stp[(p * 2 + r) % 2], 16 * ((p * 2 + r - 2) // 2 + 1))
                    pr2 = ((p * 2 + r) % 2) * 2048
                    for q4 in range(4):
                        s.activation(out=s_pg[:, pr2 + q4 * 512:pr2 + (q4 + 1) * 512],
                                     in_=p_post[:, q4 * 512:(q4 + 1) * 512], func=AF.Gelu)
                    s.activation(out=s_sqsc[:, 0:512], in_=s_pg[:, pr2 + 1536:pr2 + 2048],
                                 func=AF.Identity, bias=0.0).then_inc(gl, 1)

        @block.vector
        def _(v):
            v.memset(s_onerow[:], 1.0)
            v.memset(s_one[:], 1.0)
            v.memset(s_oneb[:], 1.0)
            v.nop().then_inc(misc, 1)
            v.wait_ge(prj, 1)
            v.tensor_copy(out=s_bvbc[:], in_=bank(0)).then_inc(misc, 1)
            for stile in range(NST):
                v.wait_ge(prj, 66 + stile)
                v.tensor_tensor(out=s_v[:, stile * 512:(stile + 1) * 512],
                                in0=bank(stile % 2), in1=s_bvbc[:], op=OP.add).then_inc(vev, 1)
            for n in range(64):
                v.wait_ge(prj, 98 + n)
                v.reduce_max(out=s_nm[:, n:n + 1], in_=bank(2 + n % 2)[0:1, :],
                             axis=AX.X).then_inc(nrm, 1)
            v.tensor_copy(out=s_spc[:], in_=s_bvbc[:])
            v.reduce_max(out=s_nm2[:, 0:1], in_=s_nm[:, 0:32], axis=AX.X)
            v.reduce_max(out=s_nm2[:, 4:5], in_=s_nm[:, 32:64], axis=AX.X)
            v.tensor_copy(out=s_spc[:], in_=s_bvbc[:])
            v.tensor_tensor(out=s_nm2[:, 8:9], in0=s_nm2[:, 0:1], in1=s_nm2[:, 4:5], op=OP.add)
            v.tensor_copy(out=s_spc[:], in_=s_bvbc[:])
            v.tensor_scalar(out=s_nm2[:, 12:13], in0=s_nm2[:, 8:9], scalar1=-1.0 / 16.0,
                            scalar2=None, op0=OP.mult)
            v.tensor_copy(out=s_spc[:], in_=s_bvbc[:]).then_inc(nrm, 1)
            v.wait_ge(prj, 162)
            v.tensor_copy(out=s_negM[:], in_=bank(0)[:, 0:1])
            v.tensor_copy(out=s_spc64[:], in_=s_bvbc[:, 0:64]).then_inc(nrm, 1)
            for p in range(4):
                for stile in range(NST + 1):
                    pr = (stile % 2) * 16
                    pr1 = ((stile - 1) % 2) * 16
                    if stile >= 1:
                        v.reciprocal(out=s_cs[:, pr1 + 8:pr1 + 9], in_=s_cs[:, pr1 + 0:pr1 + 1])
                        v.reciprocal(out=s_cs[:, pr1 + 12:pr1 + 13], in_=s_cs[:, pr1 + 4:pr1 + 5])
                    if stile < NST:
                        v.wait_ge(xpA, p * 384 + stile * 8 + 8)
                        v.wait_ge(xpB, p * 384 + stile * 8 + 8)
                        v.tensor_copy(out=s_spc64[:], in_=s_bvbc[:, 0:64])
                        v.reduce_sum(out=s_cs[:, pr + 0:pr + 1], in_=s_acc[:, pr + 0:pr + 8], axis=AX.X)
                        v.reduce_sum(out=s_cs[:, pr + 4:pr + 5], in_=s_acc[:, pr + 8:pr + 16], axis=AX.X)
                    if stile >= 1:
                        st1 = stile - 1
                        v.tensor_scalar(out=s_vt[:, st1 * 128: st1 * 128 + 64],
                                        in0=s_v[:, st1 * 512 + p * 128: st1 * 512 + p * 128 + 64],
                                        scalar1=s_cs[:, pr1 + 8:pr1 + 9], scalar2=None, op0=OP.mult)
                        v.tensor_scalar(out=s_vt[:, st1 * 128 + 64: st1 * 128 + 128],
                                        in0=s_v[:, st1 * 512 + p * 128 + 64: st1 * 512 + p * 128 + 128],
                                        scalar1=s_cs[:, pr1 + 12:pr1 + 13], scalar2=None, op0=OP.mult)
                        v.tensor_copy(out=s_spc64[:], in_=s_vt[:, st1 * 128: st1 * 128 + 64]).then_inc(cs, 1)
    return nc


def host_l0_attn_inputs(inputs, xnb_by_core):
    W = np.asarray(inputs["L0_attn_in_proj_w"])
    bqkv_full = np.asarray(inputs["L0_attn_in_proj_b"])
    maps = []
    for c in range(NCORES):
        b, hh = c // 2, c % 2
        o = 512 * hh
        Wq = W[o:o + 512, :]
        Wk = W[1024 + o:1024 + o + 512, :]
        Wv = W[2048 + o:2048 + o + 512, :]
        wqkvT = np.ascontiguousarray(np.concatenate([Wq, Wk, Wv], axis=0).T)
        bq = bqkv_full[o:o + 512]
        bk = bqkv_full[1024 + o:1024 + o + 512]
        bv = bqkv_full[2048 + o:2048 + o + 512]
        bqkv = np.ascontiguousarray(np.concatenate([bq, bk]).reshape(8, 128).T.astype(np.float32))
        xnb_full = np.concatenate([xnb_by_core[2 * b], xnb_by_core[2 * b + 1]], axis=1)
        maps.append(dict(xnb_full=np.ascontiguousarray(xnb_full), wqkvT=wqkvT,
                         bqkv=bqkv,
                         bv_row=np.ascontiguousarray(bv.reshape(1, 512).astype(np.float32))))
    return maps


def build_gemm(name, Kd, Nout, Tn, act="identity", scale=1.0, out_f32=False,
               nbias=None, w_bf16=False, ctx=None, A=None):
    NO = Nout // 128
    NK = Kd // 128
    NTB = Tn // 512
    use_bias = nbias is not None
    if ctx is None:
        nc = _mk(name)
        out = nc.dram_tensor("out", [Nout, Tn], BF16, kind="ExternalOutput").ap()
        outf = nc.dram_tensor("outf", [Nout, Tn], F32, kind="ExternalOutput").ap() if out_f32 else None
        A = dict(wT=nc.dram_tensor("wT", [Kd, Nout], BF16 if w_bf16 else F32, kind="ExternalInput").ap(),
                 xb=nc.dram_tensor("xb", [Kd, Tn], BF16, kind="ExternalInput").ap(),
                 bias=nc.dram_tensor("bias", [128, NO], F32, kind="ExternalInput").ap() if use_bias else None,
                 out_slice=lambda o, tb: out[o * 128:(o + 1) * 128, tb * 512:(tb + 1) * 512],
                 outf_slice=(lambda o, tb: outf[o * 128:(o + 1) * 128, tb * 512:(tb + 1) * 512]) if out_f32 else None)
    else:
        nc = ctx.nc
    wT, xb, bias = A.get("wT"), A["xb"], A.get("bias")
    wTt = A.get("wTt")
    out_slice, outf_slice = A.get("out_slice"), A.get("outf_slice")
    fuse_add = A.get("fuse_add")
    fuse_out = A.get("fuse_out")
    no_bf16 = A.get("no_bf16", False) or fuse_add is not None
    fn = {"identity": AF.Identity, "gelu": AF.Gelu}[act]

    with ExitStack() as es:
        if ctx is not None and ctx.ges is not None:
            es = ctx.ges
        sb = lambda n, *a: es.enter_context(nc.sbuf_tensor(_u(n), *a))
        ps = lambda n, *a: es.enter_context(nc.psum_tensor(_u(n), *a))
        _semhs = []
        def sem(n):
            h = es.enter_context(nc.semaphore(_u(n)))
            _semhs.append(h)
            return h
        s_x = sb("s_x", [128, NK * Tn], BF16)
        s_w = sb("s_w", [128, 2 * NK * 128], BF16)
        s_b = sb("s_b", [128, NO], F32) if use_bias else None
        s_o = sb("s_o", [128, 2 * NTB * 512], BF16) if not no_bf16 else None
        s_fl = sb("s_fl", [128, 512], F32)
        s_of = sb("s_of", [128, 2 * NTB * 512], F32) if out_f32 else None
        if fuse_add is not None:
            s_a1 = sb("s_a1", [128, 2 * 512], F32)
            s_a2 = sb("s_a2", [128, 2 * 512], F32)
            s_f1 = sb("s_f1", [128, 2 * 512], F32)
            s_f2 = sb("s_f2", [128, 2 * 512], F32)
        ps8 = ps("ps8", [128, 2048], F32)
        bank = lambda i: ps8[:, i * 512:(i + 1) * 512]
        ld = sem("ld"); te = sem("te"); ev = sem("ev")
        gwp = [sem("gw0"), sem("gw1")]
        stp = [sem("st0"), sem("st1")]
        if "export" in A:
            A["export"].update(stp=stp, ev=ev, NTB=NTB, NSPT_=None)
        if fuse_add is not None:
            alp = [sem("al0"), sem("al1")]
            av = sem("av")
        NSPT = (0 if no_bf16 else 1) + (1 if out_f32 and fuse_add is None else 0)
        if ctx is None:
            _clean_sems(nc, _semhs)
            block = es.enter_context(nc.Block())
        else:
            ctx.clean(_semhs)
            block = ctx.block
        KOUT = (2 if out_f32 else 1)

        @block.sync
        def _(sy):
            if use_bias:
                sy.dma_start(out=s_b[:], in_=bias).then_inc(ld, 16)
            for c in range(NK):
                sy.dma_start(out=s_x[:, c * Tn:(c + 1) * Tn],
                             in_=xb[c * 128:(c + 1) * 128, :]).then_inc(ld, 16)

        @block.gpsimd
        def _(g):
            for o in range(NO):
                if o >= 2:
                    g.wait_ge(te, (o - 1) * NTB)
                src = (wTt[o].rearrange("p (c f) -> p c f", c=NK) if wTt is not None
                       else wT[:, o * 128:(o + 1) * 128].rearrange("(c p) f -> p c f", p=128))
                g.dma_start(out=s_w[:].rearrange("p (b c f) -> p b c f", b=2, c=NK)[:, o % 2],
                            in_=src).then_inc(gwp[o % 2], 16)

        @block.tensor
        def _(t):
            t.wait_ge(ld, 16 * (NK + (1 if use_bias else 0)))
            for o in range(NO):
                t.wait_ge(gwp[o % 2], 16 * (o // 2 + 1))
                wv = s_w[:].rearrange("p (b c f) -> p b c f", b=2, c=NK)[:, o % 2]
                for tb in range(NTB):
                    gidx = o * NTB + tb
                    if gidx >= 4:
                        t.wait_ge(ev, gidx - 3)
                    for c in range(NK):
                        mm = t.matmul(bank(gidx % 4), wv[:, c],
                                      s_x[:, c * Tn + tb * 512: c * Tn + (tb + 1) * 512],
                                      start=(c == 0), stop=(c == NK - 1))
                    mm.then_inc(te, 1)

        @block.scalar
        def _(s):
            for o in range(NO):
                for tb in range(NTB):
                    gidx = o * NTB + tb
                    s.wait_ge(te, gidx + 1)
                    if o >= 2:
                        if fuse_add is not None:
                            s.wait_ge(av, (o - 1) * NTB)
                        else:
                            s.wait_ge(stp[o % 2], 16 * NSPT * NTB * ((o - 2) // 2 + 1))
                    kw = dict(func=fn, scale=scale)
                    if use_bias:
                        kw["bias"] = s_b[:, o:o + 1]
                    outs = []
                    if not no_bf16:
                        outs.append(s_o[:, (o % 2) * NTB * 512 + tb * 512:(o % 2) * NTB * 512 + (tb + 1) * 512])
                    if out_f32:
                        outs.append(s_of[:, (o % 2) * NTB * 512 + tb * 512:(o % 2) * NTB * 512 + (tb + 1) * 512])
                    for dst_ in outs:
                        s.activation(out=dst_, in_=bank(gidx % 4), **kw)
                    s.activation(out=s_fl[:], in_=outs[0], func=AF.Identity, bias=0.0)
                    s.activation(out=s_fl[:], in_=outs[-1], func=AF.Identity,
                                 bias=0.0).then_inc(ev, 1)

        if fuse_add is None:
            @block.sync
            def _(sy):
                for o in range(NO):
                    sy.wait_ge(ev, o * NTB + NTB)
                    for tb in range(NTB):
                        if not no_bf16:
                            src = s_o[:, (o % 2) * NTB * 512 + tb * 512:(o % 2) * NTB * 512 + (tb + 1) * 512]
                            sy.dma_start(out=out_slice(o, tb), in_=src).then_inc(stp[o % 2], 16)
                        if out_f32:
                            srcf = s_of[:, (o % 2) * NTB * 512 + tb * 512:(o % 2) * NTB * 512 + (tb + 1) * 512]
                            sy.dma_start(out=outf_slice(o, tb), in_=srcf).then_inc(stp[o % 2], 16)
                sy.wait_ge(stp[0], 16 * NSPT * NTB * ((NO + 1) // 2))
                sy.wait_ge(stp[1], 16 * NSPT * NTB * (NO // 2))
        else:
            a1, a2 = fuse_add
            NG = NO * NTB
            s_flv = sb("s_flv", [128, 512], F32)

            @block.sync
            def _(sy):
                for g in range(NG + 1):
                    if g < NG:
                        o, tb = g // NTB, g % NTB
                        if g >= 2:
                            sy.wait_ge(av, g - 1)
                        for src_d, dst_s in ((a1, s_a1), (a2, s_a2)):
                            sy.dma_start(out=dst_s[:, (g % 2) * 512:(g % 2 + 1) * 512],
                                         in_=src_d[o * 128:(o + 1) * 128, tb * 512:(tb + 1) * 512]
                                         ).then_inc(alp[g % 2], 16)
                    if g >= 1:
                        j = g - 1
                        o, tb = j // NTB, j % NTB
                        sy.wait_ge(av, j + 1)
                        sy.dma_start(out=fuse_out[o * 128:(o + 1) * 128, tb * 512:(tb + 1) * 512],
                                     in_=s_f2[:, (j % 2) * 512:(j % 2 + 1) * 512]
                                     ).then_inc(stp[j % 2], 16)
                sy.wait_ge(stp[0], 16 * ((NG + 1) // 2))
                sy.wait_ge(stp[1], 16 * (NG // 2))

            @block.vector
            def _(v):
                for g in range(NG):
                    o, tb = g // NTB, g % NTB
                    v.wait_ge(ev, g + 1)
                    v.wait_ge(alp[g % 2], 32 * (g // 2 + 1))
                    if g >= 2:
                        v.wait_ge(stp[g % 2], 16 * (g // 2))
                    oft = s_of[:, (o % 2) * NTB * 512 + tb * 512:(o % 2) * NTB * 512 + (tb + 1) * 512]
                    a1t = s_a1[:, (g % 2) * 512:(g % 2 + 1) * 512]
                    a2t = s_a2[:, (g % 2) * 512:(g % 2 + 1) * 512]
                    f1t = s_f1[:, (g % 2) * 512:(g % 2 + 1) * 512]
                    f2t = s_f2[:, (g % 2) * 512:(g % 2 + 1) * 512]
                    v.tensor_tensor(out=f1t, in0=oft, in1=a1t, op=OP.add)
                    v.tensor_tensor(out=f2t, in0=f1t, in1=a2t, op=OP.add)
                    v.tensor_copy(out=s_flv[:], in_=f2t)
                    v.tensor_copy(out=s_flv[:, 0:64], in_=f2t[:, 448:512]).then_inc(av, 1)
    return nc


def build_addln(name, n_in, Tn=T, gb=False, out_sum=False, out_f32=True,
                dtypes=("f32", "f32", "f32"), ctx=None, A=None):
    if ctx is None:
        nc = _mk(name)
        A = dict(xs=[nc.dram_tensor(f"x{i}", [D, Tn], F32 if dtypes[i] == "f32" else BF16,
                                    kind="ExternalInput").ap() for i in range(n_in)],
                 params=nc.dram_tensor("params", [128, 16], F32, kind="ExternalInput").ap() if gb else None,
                 xnf=nc.dram_tensor("xnf", [D, Tn], F32, kind="ExternalOutput").ap() if out_f32 else None,
                 xnb=nc.dram_tensor("xnb", [D, Tn], BF16, kind="ExternalOutput").ap(),
                 sumf=nc.dram_tensor("sumf", [D, Tn], F32, kind="ExternalOutput").ap() if out_sum else None)
    else:
        nc = ctx.nc
    xs, params = A["xs"], A.get("params")
    xnf, xnb, sumf = A.get("xnf"), A.get("xnb"), A.get("sumf")
    NB = Tn // 512
    KTB = NCH * ((1 if out_f32 else 0) + (1 if xnb is not None else 0))

    with ExitStack() as es:
        sb = lambda n, *a: es.enter_context(nc.sbuf_tensor(_u(n), *a))
        ps = lambda n, *a: es.enter_context(nc.psum_tensor(_u(n), *a))
        _semhs = []
        def sem(n):
            h = es.enter_context(nc.semaphore(_u(n)))
            _semhs.append(h)
            return h
        s_x = sb("s_x", [128, NCH * Tn], F32)
        s_sq = sb("s_sq", [128, NCH * 512], F32)
        s_stat = sb("s_stat", [1, 7 * 512], F32)
        s_xnf = sb("s_xnf", [128, NCH * 512], F32)
        s_xnb = sb("s_xnb", [128, NCH * 512], BF16) if xnb is not None else None
        s_par = sb("s_par", [128, 16], F32) if gb else None
        s_one = sb("s_one", [128, 1], F32)
        s_onerow = sb("s_onerow", [1, 128], F32)
        s_spcv = sb("s_spcv", [128, 512], F32)
        s_spcs = sb("s_spcs", [128, 512], F32)
        p_sum = ps("p_sum", [1, 512], F32)
        p_sq = ps("p_sq", [1, 512], F32)
        p_r = ps("p_r", [128, 512], F32)
        p_mr = ps("p_mr", [128, 512], F32)
        ld = sem("ld"); sq = sem("sq"); te = sem("te"); ve = sem("ve")
        se = sem("se"); st = sem("st"); apd = sem("apd"); sm = sem("sm")
        if ctx is None:
            _clean_sems(nc, _semhs)
            block = es.enter_context(nc.Block())
        else:
            ctx.clean(_semhs)
            block = ctx.block
        SL = lambda i: s_stat[:, i * 512:(i + 1) * 512]
        nld = n_in * NCH + (1 if gb else 0)

        @block.sync
        def _(sy):
            if gb:
                sy.dma_start(out=s_par[:], in_=params).then_inc(ld, 16)

        @block.gpsimd
        def _(g):
            cnt = 0
            for c in range(NCH):
                for i in range(n_in):
                    kw = {} if i == 0 else dict(accum_op=OP.add)
                    g.dma_start(out=s_x[:, c * Tn:(c + 1) * Tn],
                                in_=xs[i][c * 128:(c + 1) * 128, :], **kw).then_inc(sm, 16)
                    cnt += 16
                    g.wait_ge(sm, cnt)

        @block.vector
        def _(v):
            v.memset(s_one[:], 1.0)
            v.memset(s_onerow[:], 1.0)
            if gb:
                v.wait_ge(ld, 16)
            v.wait_ge(sm, 16 * n_in * NCH)
            v.nop().then_inc(ve, 1)
            for tb in range(NB):
                vb = 1 + 3 * tb
                v.wait_ge(te, 3 * tb + 2)
                v.tensor_scalar(out=SL(0), in0=p_sum[:], scalar1=1.0 / D, scalar2=None, op0=OP.mult)
                v.tensor_scalar(out=SL(1), in0=p_sq[:], scalar1=1.0 / D, scalar2=None, op0=OP.mult)
                v.tensor_copy(out=s_spcv[:], in_=s_x[:, 0:512]).then_inc(ve, 1)
                v.wait_ge(se, 2 * tb + 1)
                v.scalar_tensor_tensor(out=SL(3), in0=SL(1), scalar=LN_EPS, in1=SL(2),
                                       op0=OP.add, op1=OP.subtract)
                v.tensor_copy(out=s_spcv[:], in_=s_x[:, 0:512]).then_inc(ve, 1)
                v.wait_ge(se, 2 * tb + 2)
                v.reciprocal(out=SL(5), in_=SL(4))
                v.tensor_copy(out=s_spcv[:], in_=s_x[:, 0:512])
                v.tensor_tensor(out=SL(6), in0=SL(0), in1=SL(5), op=OP.mult)
                v.tensor_copy(out=s_spcv[:], in_=s_x[:, 0:512]).then_inc(ve, 1)
                v.wait_ge(te, 3 * tb + 3)
                if tb > 0:
                    v.wait_ge(st, 16 * KTB * tb)
                for c in range(NCH):
                    xsl = s_x[:, c * Tn + tb * 512: c * Tn + (tb + 1) * 512]
                    o = s_xnf[:, c * 512:(c + 1) * 512]
                    v.tensor_tensor(out=o, in0=xsl, in1=p_r[:], op=OP.mult)
                    v.tensor_tensor(out=o, in0=o, in1=p_mr[:], op=OP.subtract)
                    if gb:
                        v.tensor_scalar(out=o, in0=o, scalar1=s_par[:, c:c + 1],
                                        scalar2=s_par[:, 8 + c:9 + c], op0=OP.mult, op1=OP.add)
                    if xnb is not None:
                        v.tensor_copy(out=s_xnb[:, c * 512:(c + 1) * 512], in_=o)
                if xnb is not None:
                    v.tensor_copy(out=s_spcv[:], in_=s_xnb[:, (NCH - 1) * 512:NCH * 512])
                v.tensor_copy(out=s_spcv[:], in_=s_xnf[:, (NCH - 1) * 512:NCH * 512]).then_inc(apd, 1)

        @block.scalar
        def _(s):
            s.wait_ge(ve, 1)
            for tb in range(NB):
                if tb > 0:
                    s.wait_ge(te, 3 * (tb - 1) + 2)
                for c in range(NCH):
                    xsl = s_x[:, c * Tn + tb * 512: c * Tn + (tb + 1) * 512]
                    s.activation(out=s_sq[:, c * 512:(c + 1) * 512], in_=xsl,
                                 func=AF.Square).then_inc(sq, 1)
                s.wait_ge(ve, 1 + 3 * tb + 1)
                s.activation(out=SL(2), in_=SL(0), func=AF.Square)
                s.activation(out=s_spcs[:], in_=s_x[:, 0:512], func=AF.Square).then_inc(se, 1)
                s.wait_ge(ve, 1 + 3 * tb + 2)
                s.activation(out=SL(4), in_=SL(3), func=AF.Sqrt)
                s.activation(out=s_spcs[:], in_=s_x[:, 0:512], func=AF.Square).then_inc(se, 1)

        @block.tensor
        def _(t):
            t.wait_ge(ve, 1)
            for tb in range(NB):
                if tb > 0:
                    t.wait_ge(ve, 1 + 3 * tb - 1)
                for c in range(NCH):
                    xsl = s_x[:, c * Tn + tb * 512: c * Tn + (tb + 1) * 512]
                    mm = t.matmul(p_sum[:], s_one[:], xsl, start=(c == 0), stop=(c == NCH - 1))
                mm.then_inc(te, 1)
                for c in range(NCH):
                    t.wait_ge(sq, NCH * tb + c + 1)
                    mm = t.matmul(p_sq[:], s_one[:], s_sq[:, c * 512:(c + 1) * 512],
                                  start=(c == 0), stop=(c == NCH - 1))
                mm.then_inc(te, 1)
                t.wait_ge(ve, 1 + 3 * tb + 3)
                if tb > 0:
                    t.wait_ge(apd, tb)
                t.matmul(p_r[:], s_onerow[:], SL(5), start=True, stop=True)
                t.matmul(p_mr[:], s_onerow[:], SL(6), start=True, stop=True).then_inc(te, 1)

        @block.gpsimd
        def _(g):
            for tb in range(NB):
                g.wait_ge(apd, tb + 1)
                if out_f32:
                    for c in range(NCH):
                        g.dma_start(out=xnf[c * 128:(c + 1) * 128, tb * 512:(tb + 1) * 512],
                                    in_=s_xnf[:, c * 512:(c + 1) * 512]).then_inc(st, 16)
                if xnb is not None:
                    for c in range(NCH):
                        g.dma_start(out=xnb[c * 128:(c + 1) * 128, tb * 512:(tb + 1) * 512],
                                    in_=s_xnb[:, c * 512:(c + 1) * 512]).then_inc(st, 16)
            if out_sum:
                g.wait_ge(ve, 1)
                for c in range(NCH):
                    g.dma_start(out=sumf[c * 128:(c + 1) * 128, :],
                                in_=s_x[:, c * Tn:(c + 1) * Tn]).then_inc(st, 16)
            g.wait_ge(st, 16 * (NB * KTB + (NCH if out_sum else 0)))
    return nc


def build_add(name, n_in, Tn=T, dtypes=("f32", "f32", "f32"), ctx=None, A=None):
    if ctx is None:
        nc = _mk(name)
        A = dict(xs=[nc.dram_tensor(f"x{i}", [D, Tn], F32 if dtypes[i] == "f32" else BF16,
                                    kind="ExternalInput").ap() for i in range(n_in)],
                 y=nc.dram_tensor("y", [D, Tn], F32, kind="ExternalOutput").ap())
    else:
        nc = ctx.nc
    xs, y = A["xs"], A["y"]
    with ExitStack() as es:
        sb = lambda *a: es.enter_context(nc.sbuf_tensor(*a))
        _semhs = []
        def sem(n):
            h = es.enter_context(nc.semaphore(_u(n)))
            _semhs.append(h)
            return h
        s_y = sb("s_y", [128, NCH * Tn], F32)
        sm = sem("sm"); st = sem("st")
        if ctx is None:
            _clean_sems(nc, _semhs)
            block = es.enter_context(nc.Block())
        else:
            ctx.clean(_semhs)
            block = ctx.block

        @block.gpsimd
        def _(g):
            cnt = 0
            for c in range(NCH):
                for i in range(n_in):
                    kw = {} if i == 0 else dict(accum_op=OP.add)
                    g.dma_start(out=s_y[:, c * Tn:(c + 1) * Tn],
                                in_=xs[i][c * 128:(c + 1) * 128, :], **kw).then_inc(sm, 16)
                    cnt += 16
                    g.wait_ge(sm, cnt)
                g.dma_start(out=y[c * 128:(c + 1) * 128, :],
                            in_=s_y[:, c * Tn:(c + 1) * Tn]).then_inc(st, 16)
            g.wait_ge(st, 16 * NCH)
    return nc


def build_moe(debug=False, ctx=None, A=None):
    if ctx is None:
        nc = _mk("moe")
        A = dict(xnb=nc.dram_tensor("xnb", [D, T], BF16, kind="ExternalInput").ap(),
                 xnf=nc.dram_tensor("xnf", [D, T], F32, kind="ExternalInput").ap(),
                 sw=nc.dram_tensor("sw", [D, 8], F32, kind="ExternalInput").ap(),
                 swb=nc.dram_tensor("swb", [128, 8], F32, kind="ExternalInput").ap(),
                 w8t=nc.dram_tensor("w8t", [128, 128], F32, kind="ExternalInput").ap(),
                 ident=nc.dram_tensor("ident", [128, 128], F32, kind="ExternalInput").ap(),
                 sel=nc.dram_tensor("sel", [8, 1024], F32, kind="ExternalInput").ap(),
                 w1=nc.dram_tensor("w1", [8, 32, 128, 1024], F32, kind="ExternalInput").ap(),
                 b1c=nc.dram_tensor("b1c", [128, 256], F32, kind="ExternalInput").ap(),
                 w2=nc.dram_tensor("w2", [8, 8, 128, 4096], F32, kind="ExternalInput").ap(),
                 b2c=nc.dram_tensor("b2c", [128, 64], F32, kind="ExternalInput").ap(),
                 moef=nc.dram_tensor("moef", [D, T], BF16, kind="ExternalOutput").ap())
    else:
        nc = ctx.nc
    xnb, xnf, sw, swb, w8t, ident, sel = (A["xnb"], A["xnf"], A["sw"], A["swb"],
                                          A["w8t"], A["ident"], A["sel"])
    w1, b1c, w2, b2c, moef = A["w1"], A["b1c"], A["w2"], A["b2c"], A["moef"]
    if debug:
        assert ctx is None
        dbg_lg = nc.dram_tensor("dbg_lg", [128, 128], F32, kind="ExternalOutput").ap()
        dbg_g8t = nc.dram_tensor("dbg_g8t", [128, 128], F32, kind="ExternalOutput").ap()
        dbg_g8 = nc.dram_tensor("dbg_g8", [8, 2048], F32, kind="ExternalOutput").ap()
        dbg_rp = nc.dram_tensor("dbg_rp", [128, 16], F32, kind="ExternalOutput").ap()
        dbg_lm = nc.dram_tensor("dbg_lm", [128, 16], F32, kind="ExternalOutput").ap()
        dbg_h = nc.dram_tensor("dbg_h", [128, 32 * 1024], BF16, kind="ExternalOutput").ap()
        dbg_esum = nc.dram_tensor("dbg_esum", [128, 16], F32, kind="ExternalOutput").ap()

    NTT = T // 128
    NF = 32
    NE = 8
    TE_RT = 16

    def te_h_base(e, H):
        return TE_RT + 84 * e + 4 + 40 * H
    def te_y_base(e, H):
        return te_h_base(e, H) + 32
    def gh_base(e, H):
        return 64 * (2 * e + H)
    def ev_base(e, H):
        return 32 * e + 16 * H

    with ExitStack() as es:
        sb = lambda n, *a: es.enter_context(nc.sbuf_tensor(_u(n), *a))
        ps = lambda n, *a: es.enter_context(nc.psum_tensor(_u(n), *a))
        _semhs = []
        def sem(n):
            h = es.enter_context(nc.semaphore(_u(n)))
            _semhs.append(h)
            return h
        s_moe = sb("s_moe", [128, NCH * T], BF16)
        s_h = sb("s_h", [128, NF * 1024], BF16)
        s_xnh = sb("s_xnh", [128, 2 * NCH * 1024], BF16)
        s_xr = sb("s_xr", [128, 2 * NCH * 128], F32)
        s_w1b = sb("s_w1b", [128, 2 * NCH * 128], BF16)
        s_w2b = sb("s_w2b", [128, 2 * 32 * 128], BF16)
        s_G = sb("s_G", [128, T], F32)
        s_sw = sb("s_sw", [128, NCH * 8], F32)
        s_swb = sb("s_swb", [128, 8], F32)
        s_w8t = sb("s_w8t", [128, 128], F32)
        s_id = sb("s_id", [128, 128], F32)
        s_sel = sb("s_sel", [8, 1024], F32)
        s_b1 = sb("s_b1", [128, 256], F32)
        s_b2 = sb("s_b2", [128, 64], F32)
        s_lg = sb("s_lg", [128, 128], F32)
        s_ex = sb("s_ex", [128, 128], F32)
        s_lm = sb("s_lm", [128, 16], F32)
        s_nlm = sb("s_nlm", [128, 16], F32)
        s_esum = sb("s_esum", [128, 16], F32)
        s_rp = sb("s_rp", [128, 16], F32)
        s_sc = sb("s_sc", [128, 128], F32)
        s_m2 = sb("s_m2", [128, 16], F32)
        s_mask = sb("s_mask", [128, 128], F32)
        s_g8t = sb("s_g8t", [128, 128], F32)
        s_g8 = sb("s_g8", [8, 2048], F32)
        s_tmp = sb("s_tmp", [128, 512], F32)
        s_spc = sb("s_spc", [128, 128], F32)
        ps8 = ps("ps8", [128, 4096], F32)
        bank = lambda i: ps8[:, i * 512:(i + 1) * 512]

        ld = sem("ld")
        xlp = [sem("xl0"), sem("xl1")]
        rlp = [sem("rl0"), sem("rl1")]
        g1p = [sem("g10"), sem("g11")]
        g2p = [sem("g20"), sem("g21")]
        rt = sem("rt"); te = sem("te"); gh = sem("gh"); gv = sem("gv")
        ev = sem("ev"); st = sem("st")
        if ctx is None:
            _clean_sems(nc, _semhs)
            block = es.enter_context(nc.Block())
        else:
            ctx.clean(_semhs)
            block = ctx.block

        @block.sync
        def _(sy):
            sy.dma_start(out=s_sw[:].rearrange("p (c e) -> p c e", c=NCH),
                         in_=sw.rearrange("(c p) e -> p c e", p=128)).then_inc(ld, 16)
            for t_, d_ in [(s_swb, swb), (s_w8t, w8t), (s_id, ident), (s_sel, sel),
                           (s_b1, b1c), (s_b2, b2c)]:
                sy.dma_start(out=t_[:], in_=d_).then_inc(ld, 16)
            for tt in range(NTT):
                if tt >= 2:
                    sy.wait_ge(te, tt - 1)
                sy.dma_start(out=s_xr[:].rearrange("p (b c t) -> p b c t", b=2, c=NCH)[:, tt % 2],
                             in_=xnf[:, tt * 128:(tt + 1) * 128].rearrange("(c p) t -> p c t", p=128)
                             ).then_inc(rlp[tt % 2], 16)
            for hidx in range(2 * NE):
                e, H = hidx // 2, hidx % 2
                if hidx >= 2:
                    sy.wait_ge(te, te_h_base(e - 1, H) + 32)
                sy.dma_start(out=s_xnh[:].rearrange("p (b c t) -> p b c t", b=2, c=NCH)[:, hidx % 2],
                             in_=xnb[:, H * 1024:(H + 1) * 1024].rearrange("(c p) t -> p c t", p=128)
                             ).then_inc(xlp[hidx % 2], 16)

        @block.gpsimd
        def _(g):
            for e in range(NE):
                for H in range(2):
                    for f in range(NF):
                        if f >= 2:
                            g.wait_ge(te, te_h_base(e, H) + f - 1)
                        else:
                            prev = te_h_base(e, H - 1) + 31 + f if H == 1 else \
                                   (te_h_base(e - 1, 1) + 31 + f if e > 0 else 0)
                            g.wait_ge(te, prev)
                        g.dma_start(out=s_w1b[:].rearrange("p (b c f) -> p b c f", b=2, c=NCH)[:, f % 2],
                                    in_=w1[e, f].rearrange("p (c f) -> p c f", c=NCH)
                                    ).then_inc(g1p[f % 2], 16)
                    for o in range(NCH):
                        if o >= 2:
                            g.wait_ge(te, te_y_base(e, H) + o - 1)
                        else:
                            prev = te_y_base(e, H - 1) + 7 + o if H == 1 else \
                                   (te_y_base(e - 1, 1) + 7 + o if e > 0 else 0)
                            g.wait_ge(te, prev)
                        g.dma_start(out=s_w2b[:].rearrange("p (b c f) -> p b c f", b=2, c=32)[:, o % 2],
                                    in_=w2[e, o].rearrange("p (c f) -> p c f", c=32)
                                    ).then_inc(g2p[o % 2], 16)

        @block.tensor
        def _(t):
            t.wait_ge(ld, 16 * 7)
            for tt in range(NTT):
                t.wait_ge(rlp[tt % 2], 16 * (tt // 2 + 1))
                xv = s_xr[:].rearrange("p (b c t) -> p b c t", b=2, c=NCH)[:, tt % 2]
                for c in range(NCH):
                    mm = t.matmul(ps8[:, tt * 8:(tt + 1) * 8], xv[:, c], s_sw[:, c * 8:(c + 1) * 8],
                                  start=(c == 0), stop=(c == NCH - 1))
                mm.then_inc(te, 1)
            t.wait_ge(rt, 3)
            for tt in range(NTT):
                tp = t.transpose(ps8[0:8, 2048 + tt * 128: 2048 + (tt + 1) * 128],
                                 s_g8t[:, tt * 8:(tt + 1) * 8], s_id[:])
            tp.then_inc(rt, 1)
            for e in range(NE):
                t.wait_ge(rt, 5)
                if e > 0:
                    t.wait_ge(ev, ev_base(e, 0))
                for blkg in range(4):
                    if blkg >= 2:
                        t.wait_ge(gv, 4 * e + blkg - 1)
                    mm = t.matmul(bank(5 + blkg % 2), s_sel[:, e * 128:(e + 1) * 128],
                                  s_g8[:, blkg * 512:(blkg + 1) * 512], start=True, stop=True)
                    mm.then_inc(te, 1)
                for H in range(2):
                    hidx = 2 * e + H
                    t.wait_ge(xlp[hidx % 2], 16 * (hidx // 2 + 1))
                    xv = s_xnh[:].rearrange("p (b c t) -> p b c t", b=2, c=NCH)[:, hidx % 2]
                    for f in range(NF):
                        t.wait_ge(g1p[f % 2], 16 * ((2 * e + H) * 16 + f // 2 + 1))
                        if f >= 2:
                            t.wait_ge(gh, gh_base(e, H) + 2 * (f - 1))
                        elif (e, H) != (0, 0):
                            t.wait_ge(gh, gh_base(e, H))
                        else:
                            t.wait_ge(rt, 1)
                        wv = s_w1b[:].rearrange("p (b c f) -> p b c f", b=2, c=NCH)[:, f % 2]
                        for c in range(NCH):
                            for tbh in range(2):
                                mm = t.matmul(bank((f % 2) * 2 + tbh), wv[:, c],
                                              xv[:, c, tbh * 512:(tbh + 1) * 512],
                                              start=(c == 0), stop=(c == NCH - 1))
                        mm.then_inc(te, 1)
                    for o in range(NCH):
                        t.wait_ge(g2p[o % 2], 16 * ((2 * e + H) * 4 + o // 2 + 1))
                        t.wait_ge(gh, gh_base(e, H) + 64)
                        if o >= 2:
                            t.wait_ge(ev, ev_base(e, H) + 2 * (o - 1))
                        elif e == 0 and H == 0:
                            t.wait_ge(rt, 5)
                        wv = s_w2b[:].rearrange("p (b c f) -> p b c f", b=2, c=32)[:, o % 2]
                        for fc in range(32):
                            for tbh in range(2):
                                mm = t.matmul(bank(4 + (o % 2) * 2 + tbh), wv[:, fc],
                                              s_h[:, fc * 1024 + tbh * 512: fc * 1024 + (tbh + 1) * 512],
                                              start=(fc == 0), stop=(fc == 31))
                        mm.then_inc(te, 1)

        @block.scalar
        def _(s):
            s.wait_ge(rt, 1)
            for tt in range(NTT):
                s.activation(out=s_ex[:, tt * 8:(tt + 1) * 8], in_=s_lg[:, tt * 8:(tt + 1) * 8],
                             func=AF.Exp, bias=s_nlm[:, tt:tt + 1],
                             accum_out=s_esum[:, tt:tt + 1])
            s.activation(out=s_spc[:], in_=s_w8t[:], func=AF.Identity,
                         bias=0.0).then_inc(rt, 1)
            s.wait_ge(rt, 4)
            for q4 in range(4):
                act = s.activation(out=s_g8[:, q4 * 512:(q4 + 1) * 512],
                                   in_=ps8[0:8, 2048 + q4 * 512: 2048 + (q4 + 1) * 512],
                                   func=AF.Identity, bias=0.0)
            act.then_inc(rt, 1)
            for e in range(NE):
                for blkg in range(4):
                    s.wait_ge(te, TE_RT + 84 * e + blkg + 1)
                    if e > 0 and blkg == 0:
                        s.wait_ge(ev, ev_base(e, 0))
                    s.activation(out=s_G[:, blkg * 512:(blkg + 1) * 512],
                                 in_=bank(5 + blkg % 2), func=AF.Identity,
                                 bias=0.0).then_inc(gv, 1)
                s.activation(out=s_spc[:, 0:128], in_=s_w8t[:, 0:128], func=AF.Identity,
                             bias=0.0).then_inc(rt, 1)
                for H in range(2):
                    for f in range(NF):
                        s.wait_ge(te, te_h_base(e, H) + f + 1)
                        for tbh in range(2):
                            s.activation(out=s_h[:, f * 1024 + tbh * 512: f * 1024 + (tbh + 1) * 512],
                                         in_=bank((f % 2) * 2 + tbh), func=AF.Gelu_apprx_tanh,
                                         bias=s_b1[:, e * 32 + f:e * 32 + f + 1]).then_inc(gh, 1)

        @block.vector
        def _(v):
            v.memset(s_moe[:], 0.0)
            v.wait_ge(te, 16)
            v.wait_ge(ld, 16 * 7)
            v.tensor_tensor(out=s_lg[:], in0=ps8[:, 0:128],
                            in1=s_w8t[:], op=OP.mult) if False else None
            for tt in range(NTT):
                v.tensor_tensor(out=s_lg[:, tt * 8:(tt + 1) * 8], in0=ps8[:, tt * 8:(tt + 1) * 8],
                                in1=s_swb[:], op=OP.add)
            for tt in range(NTT):
                v.reduce_max(out=s_lm[:, tt:tt + 1], in_=s_lg[:, tt * 8:(tt + 1) * 8], axis=AX.X)
            v.tensor_copy(out=s_spc[:], in_=s_w8t[:])
            v.tensor_scalar(out=s_nlm[:], in0=s_lm[:], scalar1=-1.0, scalar2=None, op0=OP.mult)
            v.tensor_copy(out=s_spc[:], in_=s_w8t[:])
            v.nop().then_inc(rt, 1)
            v.wait_ge(rt, 2)
            v.tensor_copy(out=s_spc[:], in_=s_w8t[:])
            for tt in range(NTT):
                v.tensor_scalar(out=s_sc[:, tt * 8:(tt + 1) * 8], in0=s_lg[:, tt * 8:(tt + 1) * 8],
                                scalar1=s_lm[:, tt:tt + 1], scalar2=None, op0=OP.is_equal)
            v.tensor_copy(out=s_spc[:], in_=s_w8t[:])
            v.tensor_tensor(out=s_sc[:], in0=s_sc[:], in1=s_w8t[:], op=OP.mult)
            for tt in range(NTT):
                v.reduce_max(out=s_m2[:, tt:tt + 1], in_=s_sc[:, tt * 8:(tt + 1) * 8], axis=AX.X)
            for tt in range(NTT):
                v.tensor_scalar(out=s_mask[:, tt * 8:(tt + 1) * 8], in0=s_sc[:, tt * 8:(tt + 1) * 8],
                                scalar1=s_m2[:, tt:tt + 1], scalar2=None, op0=OP.is_equal)
            v.reciprocal(out=s_rp[:], in_=s_esum[:])
            v.tensor_copy(out=s_spc[:], in_=s_w8t[:])
            for tt in range(NTT):
                v.tensor_scalar(out=s_g8t[:, tt * 8:(tt + 1) * 8], in0=s_mask[:, tt * 8:(tt + 1) * 8],
                                scalar1=s_rp[:, tt:tt + 1], scalar2=None, op0=OP.mult)
            v.tensor_copy(out=s_spc[:], in_=s_w8t[:]).then_inc(rt, 1)
            for e in range(NE):
                for H in range(2):
                    for o in range(NCH):
                        for tbh in range(2):
                            v.wait_ge(te, te_y_base(e, H) + o + 1)
                            if H == 0 and o == 0 and tbh == 0:
                                v.wait_ge(rt, 6 + e)
                            gslice = s_G[:, H * 1024 + tbh * 512: H * 1024 + (tbh + 1) * 512]
                            v.scalar_tensor_tensor(out=s_tmp[:], in0=bank(4 + (o % 2) * 2 + tbh),
                                                   scalar=s_b2[:, e * 8 + o:e * 8 + o + 1],
                                                   in1=gslice, op0=OP.add, op1=OP.mult)
                            dst = s_moe[:, o * T + H * 1024 + tbh * 512: o * T + H * 1024 + (tbh + 1) * 512]
                            v.tensor_tensor(out=dst, in0=dst, in1=s_tmp[:], op=OP.add).then_inc(ev, 1)
            v.tensor_copy(out=s_spc[:], in_=s_moe[:, NCH * T - 128:NCH * T]).then_inc(ev, 1)

        @block.gpsimd
        def _(g):
            g.wait_ge(ev, 32 * NE + 1)
            for c in range(NCH):
                g.dma_start(out=moef[c * 128:(c + 1) * 128, :],
                            in_=s_moe[:, c * T:(c + 1) * T]).then_inc(st, 16)
            nst = NCH
            if debug:
                for t_, d_ in [(s_lg, dbg_lg), (s_g8t, dbg_g8t), (s_g8, dbg_g8),
                               (s_rp, dbg_rp), (s_lm, dbg_lm), (s_h, dbg_h),
                               (s_esum, dbg_esum)]:
                    g.dma_start(out=d_, in_=t_[:]).then_inc(st, 16)
                    nst += 1
            g.wait_ge(st, 16 * nst)
    return nc


def build_l1_attn(odd, ctx=None, A=None, uniform=False):
    Sk = 4096 if (odd or uniform) else 2048
    IB0 = 4 if odd else 0
    NJ = Sk // 128
    NKB = Sk // 512
    if ctx is None:
        nc = _mk("l1attn_o" if odd else "l1attn_e")
        kT = nc.dram_tensor("kT", [512, Sk], BF16, kind="ExternalInput").ap()
        A = dict(qT=nc.dram_tensor("qT", [512, T], BF16, kind="ExternalInput").ap(),
                 k_slice=lambda c, j0, w: kT[c * 128:(c + 1) * 128, j0:j0 + w],
                 vtok=nc.dram_tensor("vtok", [Sk, 512], BF16, kind="ExternalInput").ap(),
                 w_outT=nc.dram_tensor("w_outT", [512, D], F32, kind="ExternalInput").ap(),
                 bout=nc.dram_tensor("bout", [128, 8], F32, kind="ExternalInput").ap(),
                 masks=nc.dram_tensor("masks", [128, (80 if uniform else 4) * 512], BF16,
                                      kind="ExternalInput").ap(),
                 attn1=nc.dram_tensor("attn1", [D, T], F32, kind="ExternalOutput").ap())
    else:
        nc = ctx.nc
    qT, k_slice, vtok = A["qT"], A["k_slice"], A["vtok"]
    w_outT, bout, masks, attn1 = A["w_outT"], A["bout"], A["masks"], A["attn1"]
    SC = float(512 ** -0.5)

    if uniform:
        NJT = [4 * (ib + 5) for ib in range(4)]
    else:
        NJT = [4 * (IB0 + ib + 1) for ib in range(4)]
    CUM = [sum(NJT[:i]) for i in range(5)]

    def diag(ib, jt):
        if uniform:
            return jt >= 4 * ib
        return jt >= 4 * (IB0 + ib)

    def nmasks_before(ib, jt):
        n = sum(1 for ib2 in range(ib) for j2 in range(NJT[ib2]) if diag(ib2, j2))
        return n + sum(1 for j2 in range(jt) if diag(ib, j2))

    with ExitStack() as es:
        sb = lambda n, *a: es.enter_context(nc.sbuf_tensor(_u(n), *a))
        ps = lambda n, *a: es.enter_context(nc.psum_tensor(_u(n), *a))
        _semhs = []
        def sem(n):
            h = es.enter_context(nc.semaphore(_u(n)))
            _semhs.append(h)
            return h
        s_q = sb("s_q", [128, 4 * T], BF16)
        s_k = sb("s_k", [128, 4 * Sk], BF16)
        s_v = sb("s_v", [128, NJ * 512], BF16)
        s_wo = sb("s_wo", [128, 4 * D], BF16)
        s_bout = sb("s_bout", [128, 8], F32)
        s_mask = sb("s_mask", [128, (2 * 20 if uniform else 4) * 512], BF16)
        s_sqsc = sb("s_sqsc", [128, 4 * Sk], BF16)
        s_nm = sb("s_nm", [1, 16], F32)
        s_nm2 = sb("s_nm2", [1, 16], F32)
        s_negM = sb("s_negM", [128, 1], F32)
        s_oneb = sb("s_oneb", [128, 1], BF16)
        s_onerow = sb("s_onerow", [1, 128], F32)
        s_E = sb("s_E", [128, 2 * 512], BF16)
        s_rr = sb("s_rr", [1, 512], F32)
        s_rbc = sb("s_rbc", [128, 512], F32)
        s_post = sb("s_post", [128, 4 * 512], BF16)
        s_tmp = sb("s_tmp", [128, 2 * 512], F32)
        s_og = sb("s_og", [128, 2 * 512], F32)
        s_spc = sb("s_spc", [128, 512], F32)
        ps8 = ps("ps8", [128, 4096], F32)
        bank = lambda i: ps8[:, i * 512:(i + 1) * 512]
        nb6 = lambda par: ps8[0:1, 3072 + par * 512: 3072 + par * 512 + 512]

        ld = sem("ld"); gld = sem("gld"); nr = sem("nr"); nrs = sem("nrs")
        mlp = [sem("ml0"), sem("ml1")]
        NLD = (4 + (8 if uniform else 4) + 2 + (0 if uniform else 1))
        tsc = sem("tsc"); xp = sem("xp"); msk = sem("msk"); pvc = sem("pvc")
        csd = sem("csd"); rbc = sem("rbc"); ope = sem("ope"); oevt = sem("oevt")
        stwp = [sem("stw0"), sem("stw1")]
        if ctx is None:
            _clean_sems(nc, _semhs)
            block = es.enter_context(nc.Block())
        else:
            ctx.clean(_semhs)
            block = ctx.block
        NNM = 4 + NKB

        @block.sync
        def _(sy):
            for c in range(4):
                sy.dma_start(out=s_q[:, c * T:(c + 1) * T],
                             in_=qT[c * 128:(c + 1) * 128, :]).then_inc(ld, 16)
                if uniform:
                    for half in range(2):
                        sy.dma_start(out=s_k[:, c * Sk + half * 2048: c * Sk + (half + 1) * 2048],
                                     in_=k_slice(c, half * 2048, 2048)).then_inc(ld, 16)
                else:
                    sy.dma_start(out=s_k[:, c * Sk:(c + 1) * Sk],
                                 in_=k_slice(c, 0, Sk)).then_inc(ld, 16)
            sy.dma_start(out=s_v[:].rearrange("p (j d) -> p j d", j=NJ),
                         in_=vtok.rearrange("(j p) d -> p j d", p=128)).then_inc(ld, 16)
            sy.dma_start(out=s_bout[:], in_=bout).then_inc(ld, 16)
            if uniform:
                for ib in range(4):
                    if ib >= 2:
                        sy.wait_ge(msk, 1 + NNM + 20 * (ib - 1))
                    sy.dma_start(out=s_mask[:, (ib % 2) * 20 * 512:(ib % 2 + 1) * 20 * 512],
                                 in_=masks[:, ib * 20 * 512:(ib + 1) * 20 * 512]).then_inc(mlp[ib % 2], 16)
            else:
                sy.dma_start(out=s_mask[:], in_=masks).then_inc(ld, 16)

        @block.gpsimd
        def _(g):
            for c in range(4):
                g.dma_start(out=s_wo[:, c * D:(c + 1) * D],
                            in_=w_outT[c * 128:(c + 1) * 128, :]).then_inc(gld, 16)
            for ib in range(4):
                for ot in range(NCH):
                    n = ib * NCH + ot
                    g.wait_ge(oevt, n + 1)
                    g.dma_start(out=attn1[ot * 128:(ot + 1) * 128, ib * 512:(ib + 1) * 512],
                                in_=s_og[:, (n % 2) * 512:(n % 2) * 512 + 512]
                                ).then_inc(stwp[n % 2], 16)
            g.wait_ge(stwp[0], 16 * 16)
            g.wait_ge(stwp[1], 16 * 16)

        @block.scalar
        def _(s):
            s.wait_ge(ld, 16 * NLD)
            s.wait_ge(gld, 16 * 4)
            s.activation(out=s_sqsc[:, 0:4 * T], in_=s_q[:], func=AF.Square).then_inc(nrs, 1)
            s.wait_ge(nr, 4)
            s.activation(out=s_sqsc[:, 0:4 * Sk], in_=s_k[:], func=AF.Square).then_inc(nrs, 1)
            for ib in range(4):
                for jt in range(NJT[ib]):
                    it = CUM[ib] + jt
                    s.wait_ge(tsc, it + 1)
                    if it >= 2:
                        s.wait_ge(pvc, it - 1)
                    s.activation(out=s_E[:, (it % 2) * 512:(it % 2) * 512 + 512],
                                 in_=bank(it % 2), func=AF.Exp, bias=s_negM[:],
                                 scale=SC).then_inc(xp, 1)
                s.wait_ge(pvc, CUM[ib + 1])
                for dc in range(4):
                    s.activation(out=s_post[:, dc * 512:(dc + 1) * 512], in_=bank(2 + dc),
                                 func=AF.Identity, bias=0.0).then_inc(csd, 1)
                s.wait_ge(rbc, ib * 2 + 1)
                s.activation(out=s_rbc[:], in_=bank(7), func=AF.Identity,
                             bias=0.0).then_inc(rbc, 1)

        @block.vector
        def _(v):
            v.memset(s_oneb[:], 1.0)
            v.memset(s_onerow[:], 1.0)
            v.memset(s_spc[:], 0.0)
            v.nop().then_inc(msk, 1)
            for i in range(NNM):
                v.wait_ge(nr, i + 1)
                v.reduce_max(out=s_nm[:, i:i + 1], in_=nb6(i % 2), axis=AX.X)
                v.nop().then_inc(msk, 1)
            v.tensor_copy(out=s_spc[:], in_=s_q[:, 0:512])
            v.reduce_max(out=s_nm2[:, 0:1], in_=s_nm[:, 0:4], axis=AX.X)
            v.reduce_max(out=s_nm2[:, 4:5], in_=s_nm[:, 4:4 + NKB], axis=AX.X)
            v.tensor_copy(out=s_spc[:], in_=s_q[:, 0:512])
            v.tensor_tensor(out=s_nm2[:, 8:9], in0=s_nm2[:, 0:1], in1=s_nm2[:, 4:5], op=OP.add)
            v.tensor_copy(out=s_spc[:], in_=s_q[:, 0:512])
            v.tensor_scalar(out=s_nm2[:, 12:13], in0=s_nm2[:, 8:9], scalar1=-0.5 * SC,
                            scalar2=None, op0=OP.mult)
            v.tensor_copy(out=s_spc[:], in_=s_q[:, 0:512]).then_inc(nr, 100)
            v.wait_ge(nr, 200 + NNM)
            v.tensor_copy(out=s_negM[:], in_=ps8[:, 3072:3073])
            v.tensor_copy(out=s_spc[:], in_=s_q[:, 0:512]).then_inc(nr, 100)
            for ib in range(4):
                for jt in range(NJT[ib]):
                    it = CUM[ib] + jt
                    if diag(ib, jt):
                        v.wait_ge(xp, it + 1)
                        if uniform:
                            if jt == 4 * ib:
                                v.wait_ge(mlp[ib % 2], 16 * (ib // 2 + 1))
                            dk = (ib % 2) * 20 + (jt - 4 * ib)
                        else:
                            dk = jt - 4 * (IB0 + ib)
                        epar = (it % 2) * 512
                        v.tensor_tensor(out=s_E[:, epar:epar + 512], in0=s_E[:, epar:epar + 512],
                                        in1=s_mask[:, dk * 512:(dk + 1) * 512],
                                        op=OP.mult).then_inc(msk, 1)
                v.wait_ge(pvc, CUM[ib + 1])
                v.tensor_copy(out=s_spc[:], in_=s_q[:, 0:512])
                v.reciprocal(out=s_rr[:], in_=ps8[0:1, 3072:3584])
                v.tensor_copy(out=s_spc[:], in_=s_q[:, 0:512]).then_inc(csd, 1)
                for ot in range(NCH):
                    n = ib * NCH + ot
                    v.wait_ge(ope, n + 1)
                    if n >= 2:
                        v.wait_ge(stwp[n % 2], 16 * (n // 2))
                    v.wait_ge(rbc, ib * 2 + 2)
                    v.scalar_tensor_tensor(out=s_tmp[:, (n % 2) * 512:(n % 2) * 512 + 512],
                                           in0=bank(2 + ot % 2), scalar=1.0, in1=s_rbc[:],
                                           op0=OP.mult, op1=OP.mult)
                    v.tensor_scalar(out=s_og[:, (n % 2) * 512:(n % 2) * 512 + 512],
                                    in0=s_tmp[:, (n % 2) * 512:(n % 2) * 512 + 512],
                                    scalar1=s_bout[:, ot:ot + 1], scalar2=None, op0=OP.add)
                    v.tensor_copy(out=s_spc[:], in_=s_og[:, (n % 2) * 512:(n % 2) * 512 + 512]
                                  ).then_inc(oevt, 1)

        @block.tensor
        def _(t):
            t.wait_ge(ld, 16 * NLD)
            t.wait_ge(gld, 16 * 4)
            t.wait_ge(nrs, 1)
            t.wait_ge(msk, 1)
            for i in range(NNM):
                if i == 4:
                    t.wait_ge(nrs, 2)
                if i >= 2:
                    t.wait_ge(msk, i)
                W = T if i < 4 else Sk
                blk = i if i < 4 else i - 4
                for c in range(4):
                    mm = t.matmul(nb6(i % 2), s_oneb[:],
                                  s_sqsc[:, c * W + blk * 512: c * W + blk * 512 + 512],
                                  start=(c == 0), stop=(c == 3))
                mm.then_inc(nr, 1)
            t.wait_ge(nr, 100 + NNM)
            t.matmul(ps8[:, 3072:3073], s_onerow[:], s_nm2[:, 12:13],
                     start=True, stop=True).then_inc(nr, 100)
            for ib in range(4):
                for jt in range(NJT[ib]):
                    it = CUM[ib] + jt
                    if it == 0:
                        t.wait_ge(nr, 300 + NNM)
                    if jt == 0 and ib >= 1:
                        t.wait_ge(oevt, 8 * ib)
                        t.wait_ge(csd, 5 * ib)
                    if it >= 2:
                        t.wait_ge(xp, it - 1)
                    for c in range(4):
                        mm = t.matmul(bank(it % 2),
                                      s_k[:, c * Sk + jt * 128: c * Sk + (jt + 1) * 128],
                                      s_q[:, c * T + ib * 512: c * T + (ib + 1) * 512],
                                      start=(c == 0), stop=(c == 3))
                    mm.then_inc(tsc, 1)
                    if diag(ib, jt):
                        t.wait_ge(msk, 1 + NNM + nmasks_before(ib, jt) + 1)
                    else:
                        t.wait_ge(xp, it + 1)
                    epar = (it % 2) * 512
                    t.matmul(ps8[0:1, 3072:3584], s_oneb[:], s_E[:, epar:epar + 512],
                             start=(jt == 0), stop=(jt == NJT[ib] - 1))
                    for dc in range(4):
                        mm = t.matmul(bank(2 + dc), s_v[:, jt * 512 + dc * 128: jt * 512 + (dc + 1) * 128],
                                      s_E[:, epar:epar + 512],
                                      start=(jt == 0), stop=(jt == NJT[ib] - 1))
                    mm.then_inc(pvc, 1)
                t.wait_ge(csd, ib * 5 + 5)
                t.matmul(bank(7), s_onerow[:], s_rr[:], start=True, stop=True).then_inc(rbc, 1)
                for ot in range(NCH):
                    n = ib * NCH + ot
                    if n >= 2:
                        t.wait_ge(oevt, n - 1)
                    t.wait_ge(rbc, ib * 2 + 2)
                    for c in range(4):
                        mm = t.matmul(bank(2 + ot % 2), s_wo[:, c * D + ot * 128: c * D + (ot + 1) * 128],
                                      s_post[:, c * 512:(c + 1) * 512],
                                      start=(c == 0), stop=(c == 3))
                    mm.then_inc(ope, 1)
    return nc


def build_tpartials(ctx=None, A=None):
    if ctx is None:
        nc = _mk("tpart")
        A = dict(w1s=nc.dram_tensor("w1s", [128, 4096], F32, kind="ExternalInput").ap(),
                 w2s=nc.dram_tensor("w2s", [128, 4096], F32, kind="ExternalInput").ap(),
                 part=nc.dram_tensor("part", [1, 2], F32, kind="ExternalOutput").ap())
    else:
        nc = ctx.nc
    w1s, w2s, part = A["w1s"], A["w2s"], A["part"]
    with ExitStack() as es:
        sb = lambda n, *a: es.enter_context(nc.sbuf_tensor(_u(n), *a))
        ps = lambda n, *a: es.enter_context(nc.psum_tensor(_u(n), *a))
        _semhs = []
        def sem(n):
            h = es.enter_context(nc.semaphore(_u(n)))
            _semhs.append(h)
            return h
        s_w = sb("s_w", [128, 2 * 4096], F32)
        s_r = sb("s_r", [128, 8], F32)
        s_one = sb("s_one", [128, 1], F32)
        s_o = sb("s_o", [1, 2], F32)
        s_spc = sb("s_spc", [128, 512], F32)
        p = ps("p", [1, 512], F32)
        ld = sem("ld"); ve = sem("ve"); te = sem("te"); se = sem("se"); st = sem("st")
        if ctx is None:
            _clean_sems(nc, _semhs)
            block = es.enter_context(nc.Block())
        else:
            ctx.clean(_semhs)
            block = ctx.block

        @block.sync
        def _(sy):
            sy.dma_start(out=s_w[:, 0:4096], in_=w1s).then_inc(ld, 16)
            sy.dma_start(out=s_w[:, 4096:8192], in_=w2s).then_inc(ld, 16)

        @block.vector
        def _(v):
            v.memset(s_one[:], 1.0)
            v.wait_ge(ld, 32)
            v.tensor_reduce(out=s_r[:, 0:1], in_=s_w[:, 0:4096], axis=AX.X, op=OP.add,
                            apply_absolute_value=True)
            v.tensor_reduce(out=s_r[:, 4:5], in_=s_w[:, 4096:8192], axis=AX.X, op=OP.add,
                            apply_absolute_value=True)
            v.tensor_copy(out=s_spc[:], in_=s_w[:, 0:512])
            v.tensor_copy(out=s_spc[:], in_=s_w[:, 0:512]).then_inc(ve, 1)

        @block.tensor
        def _(t):
            t.wait_ge(ve, 1)
            t.matmul(p[:, 0:1], s_one[:], s_r[:, 0:1], start=True, stop=True)
            t.matmul(p[:, 1:2], s_one[:], s_r[:, 4:5], start=True, stop=True).then_inc(te, 1)

        @block.scalar
        def _(s):
            s.wait_ge(te, 1)
            s.activation(out=s_o[:], in_=p[0:1, 0:2], func=AF.Identity, bias=0.0)
            s.activation(out=s_spc[:], in_=s_w[:, 0:512], func=AF.Identity, bias=0.0)
            s.activation(out=s_spc[0:1, 0:2], in_=s_o[:], func=AF.Identity, bias=0.0).then_inc(se, 1)

        @block.gpsimd
        def _(g):
            g.wait_ge(se, 1)
            g.dma_start(out=part, in_=s_o[:]).then_inc(st, 16)
            g.wait_ge(st, 16)
    return nc


def build_ternarize(debug=False, ctx=None, A=None, ptot_mode=False):
    if ctx is None:
        nc = _mk("tern")
        A = dict(w1s=nc.dram_tensor("w1s", [128, 4096], F32, kind="ExternalInput").ap(),
                 w2s=nc.dram_tensor("w2s", [128, 4096], F32, kind="ExternalInput").ap(),
                 pa=nc.dram_tensor("pa", [1, 8], F32, kind="ExternalInput").ap(),
                 pb=nc.dram_tensor("pb", [1, 8], F32, kind="ExternalInput").ap(),
                 t1=nc.dram_tensor("t1", [128, 4096], BF16, kind="ExternalOutput").ap(),
                 t2=nc.dram_tensor("t2", [128, 4096], BF16, kind="ExternalOutput").ap())
    else:
        nc = ctx.nc
    w1s, w2s, t1, t2 = A["w1s"], A["w2s"], A["t1"], A["t2"]
    pa, pb = A.get("pa"), A.get("pb")
    ptot = A.get("ptot")
    if debug:
        assert ctx is None
        dbg_st = nc.dram_tensor("dbg_st", [1, 32], F32, kind="ExternalOutput").ap()
        dbg_thr = nc.dram_tensor("dbg_thr", [128, 4], F32, kind="ExternalOutput").ap()
        dbg_ge = nc.dram_tensor("dbg_ge", [128, 512], F32, kind="ExternalOutput").ap()
        dbg_p = nc.dram_tensor("dbg_p", [1, 16], F32, kind="ExternalOutput").ap()
        dbg_w = nc.dram_tensor("dbg_w", [128, 512], F32, kind="ExternalOutput").ap()
    NTOT = float(1024 * 4096)
    with ExitStack() as es:
        sb = lambda n, *a: es.enter_context(nc.sbuf_tensor(_u(n), *a))
        ps = lambda n, *a: es.enter_context(nc.psum_tensor(_u(n), *a))
        _semhs = []
        def sem(n):
            h = es.enter_context(nc.semaphore(_u(n)))
            _semhs.append(h)
            return h
        s_w = sb("s_w", [128, 2 * 4096], F32)
        s_p = sb("s_p", [1, 16], F32)
        s_st = sb("s_st", [1, 32], F32)
        s_thr = sb("s_thr", [128, 4], F32)
        s_ge = sb("s_ge", [128, 4096], F32)
        s_le = sb("s_le", [128, 4096], F32)
        s_t = sb("s_t", [128, 2 * 4096], BF16)
        s_onerow = sb("s_onerow", [1, 128], F32)
        s_spc = sb("s_spc", [128, 512], F32)
        p8 = ps("p8", [128, 512], F32)
        ld = sem("ld"); ve = sem("ve"); te = sem("te"); se = sem("se"); st = sem("st")
        if ctx is None:
            _clean_sems(nc, _semhs)
            block = es.enter_context(nc.Block())
        else:
            ctx.clean(_semhs)
            block = ctx.block

        @block.sync
        def _(sy):
            sy.dma_start(out=s_w[:, 0:4096], in_=w1s).then_inc(ld, 16)
            sy.dma_start(out=s_w[:, 4096:8192], in_=w2s).then_inc(ld, 16)
            if ptot_mode:
                sy.dma_start(out=s_p[0:1, 0:2], in_=ptot).then_inc(ld, 16)
                sy.dma_start(out=s_p[0:1, 8:10], in_=ptot).then_inc(ld, 16)
            else:
                sy.dma_start(out=s_p[0:1, 0:8], in_=pa).then_inc(ld, 16)
                sy.dma_start(out=s_p[0:1, 8:16], in_=pb).then_inc(ld, 16)

        @block.vector
        def _(v):
            v.memset(s_onerow[:], 1.0)
            v.wait_ge(ld, 64)
            SP = lambda: v.tensor_copy(out=s_spc[:], in_=s_w[:, 0:512])
            if ptot_mode:
                v.tensor_copy(out=s_st[:, 0:1], in_=s_p[0:1, 0:1])
                v.tensor_copy(out=s_st[:, 4:5], in_=s_p[0:1, 1:2])
            else:
                v.reduce_sum(out=s_st[:, 0:1], in_=s_p[0:1, 0:8], axis=AX.X)
                v.reduce_sum(out=s_st[:, 4:5], in_=s_p[0:1, 8:16], axis=AX.X)
            SP()
            v.tensor_scalar(out=s_st[:, 8:9], in0=s_st[:, 0:1], scalar1=1.0 / NTOT,
                            scalar2=None, op0=OP.mult)
            v.tensor_scalar(out=s_st[:, 12:13], in0=s_st[:, 4:5], scalar1=1.0 / NTOT,
                            scalar2=None, op0=OP.mult)
            SP()
            v.tensor_scalar(out=s_st[:, 16:17], in0=s_st[:, 8:9], scalar1=1e-8, scalar2=0.5,
                            op0=OP.max, op1=OP.mult)
            v.tensor_scalar(out=s_st[:, 20:21], in0=s_st[:, 12:13], scalar1=1e-8, scalar2=0.5,
                            op0=OP.max, op1=OP.mult)
            SP()
            v.tensor_scalar(out=s_st[:, 24:25], in0=s_st[:, 16:17], scalar1=-1.0, scalar2=None,
                            op0=OP.mult)
            v.tensor_scalar(out=s_st[:, 28:29], in0=s_st[:, 20:21], scalar1=-1.0, scalar2=None,
                            op0=OP.mult)
            SP(); SP().then_inc(ve, 1)
            v.wait_ge(se, 1)
            for m in range(2):
                wsl = s_w[:, m * 4096:(m + 1) * 4096]
                v.tensor_scalar(out=s_ge[:], in0=wsl, scalar1=s_thr[:, 2 * m:2 * m + 1],
                                scalar2=None, op0=OP.is_ge)
                v.tensor_scalar(out=s_le[:], in0=wsl, scalar1=s_thr[:, 2 * m + 1:2 * m + 2],
                                scalar2=None, op0=OP.is_le)
                v.tensor_tensor(out=s_t[:, m * 4096:(m + 1) * 4096], in0=s_ge[:], in1=s_le[:],
                                op=OP.subtract)
            v.tensor_copy(out=s_spc[:], in_=s_t[:, 7680:8192])
            v.tensor_copy(out=s_spc[:], in_=s_t[:, 3584:4096]).then_inc(ve, 1)

        @block.tensor
        def _(t):
            t.wait_ge(ve, 1)
            for i, sl in enumerate([16, 24, 20, 28]):
                mm = t.matmul(p8[:, i:i + 1], s_onerow[:], s_st[:, sl:sl + 1],
                              start=True, stop=True)
            mm.then_inc(te, 1)

        @block.scalar
        def _(s):
            s.wait_ge(te, 1)
            s.activation(out=s_thr[:], in_=p8[:, 0:4], func=AF.Identity, bias=0.0).then_inc(se, 1)

        @block.gpsimd
        def _(g):
            g.wait_ge(ve, 2)
            g.dma_start(out=t1, in_=s_t[:, 0:4096]).then_inc(st, 16)
            g.dma_start(out=t2, in_=s_t[:, 4096:8192]).then_inc(st, 16)
            nst = 2
            if debug:
                for t_, d_ in [(s_st[:], dbg_st), (s_thr[:], dbg_thr), (s_ge[:, 0:512], dbg_ge),
                               (s_p[:], dbg_p), (s_w[:, 0:512], dbg_w)]:
                    g.dma_start(out=d_, in_=t_).then_inc(st, 16)
                    nst += 1
            g.wait_ge(st, 16 * nst)
    return nc


_NC_CACHE = {}

def _cached(key, fn):
    if key not in _NC_CACHE:
        _NC_CACHE[key] = fn()
    return _NC_CACHE[key]


def _run(nc, maps, ncores=None):
    res = run_bass_kernel_spmd(nc, maps, core_ids=list(range(len(maps))))
    return res.results


def _cols(v, k):
    v = np.asarray(v, dtype=np.float32)
    return np.ascontiguousarray(v.reshape(-1, 128).T)


def kernel_multilaunch(**inputs):
    gi = lambda n: np.asarray(inputs[n])
    x = gi("L0_attn_src").astype(np.float32)
    xsh = shard_x(x)

    ncA = _cached("A", build_ln)
    resA = _run(ncA, [{"xT": s} for s in xsh])
    xn0f = [r["xnf"] for r in resA]
    xn0b = [r["xnb"] for r in resA]

    ncB = _cached("B", build_l0_attn)
    resB = _run(ncB, host_l0_attn_inputs(inputs, xn0b))
    postg = [r["postg"] for r in resB]

    out_wT = np.ascontiguousarray(gi("L0_attn_out_w").T)
    boutc = _cols(gi("L0_attn_out_b"), 8)
    ncC1 = _cached("C1", lambda: build_gemm("outproj", D, D, T, act="identity",
                                            scale=1.0 / 64.0, out_f32=True, nbias=8))
    mapsC1 = []
    for c in range(NCORES):
        b, hh = c // 2, c % 2
        pg = np.concatenate([postg[2 * b], postg[2 * b + 1]], axis=0)[:, hh * T:(hh + 1) * T]
        mapsC1.append(dict(wT=out_wT, xb=np.ascontiguousarray(pg), bias=boutc))
    attnf = [r["outf"] for r in _run(ncC1, mapsC1)]

    ln1p = np.concatenate([_cols(gi("L0_attn_ln1_g"), 8), _cols(gi("L0_attn_ln1_b"), 8)], axis=1)
    ncC1b = _cached("C1b", lambda: build_addln("uln1", 2, gb=True, out_f32=True))
    resC1b = _run(ncC1b, [dict(x0=xn0f[c], x1=attnf[c], params=ln1p) for c in range(NCORES)])
    xpf = [r["xnf"] for r in resC1b]
    xpb = [r["xnb"] for r in resC1b]

    lin1_wT = _tile_wT(gi("L0_attn_lin1_w").T)
    lin2_wT = _tile_wT(gi("L0_attn_lin2_w").T)
    ncC2 = _cached("C2", lambda: build_gemm("lin1", D, 4096, T, act="gelu", nbias=32))
    b1c = _cols(gi("L0_attn_lin1_b"), 32)
    h = [r["out"] for r in _run(ncC2, [dict(wT=lin1_wT, xb=xpb[c], bias=b1c)
                                       for c in range(NCORES)])]
    ncC3 = _cached("C3", lambda: build_gemm("lin2", 4096, D, T, act="identity",
                                            out_f32=True, nbias=8))
    b2c = _cols(gi("L0_attn_lin2_b"), 8)
    fff = [r["outf"] for r in _run(ncC3, [dict(wT=lin2_wT, xb=h[c], bias=b2c)
                                          for c in range(NCORES)])]
    ln2p = np.concatenate([_cols(gi("L0_attn_ln2_g"), 8), _cols(gi("L0_attn_ln2_b"), 8)], axis=1)
    ncC3b = _cached("C3b", lambda: build_addln("ln2", 2, gb=True, out_f32=True))
    y0f = [r["xnf"] for r in _run(ncC3b, [dict(x0=xpf[c], x1=fff[c], params=ln2p)
                                          for c in range(NCORES)])]

    ncD = _cached("D", build_moe)
    swb_t = np.ascontiguousarray(np.tile(gi("L0_ffn_switch_b").astype(np.float32), (128, 1)))
    w8t = np.ascontiguousarray(np.tile(np.arange(8, 0, -1, dtype=np.float32), (128, 16)))
    ident = np.eye(128, dtype=np.float32)
    sel = np.ascontiguousarray(np.concatenate(
        [np.eye(8, dtype=np.float32)[:, e:e + 1] @ np.ones((1, 128), np.float32)
         for e in range(8)], axis=1))
    moe_b1c = np.ascontiguousarray(gi("L0_ffn_b1").astype(np.float32).reshape(256, 128).T)
    moe_b2c = np.ascontiguousarray(gi("L0_ffn_b2").astype(np.float32).reshape(64, 128).T)
    w1 = gi("L0_ffn_w1").astype(np.float32)
    w2 = gi("L0_ffn_w2").astype(np.float32)
    sw = gi("L0_ffn_switch_w").astype(np.float32)
    mapsD = [dict(xnb=xn0b[c], xnf=xn0f[c], sw=sw, swb=swb_t, w8t=w8t, ident=ident,
                  sel=sel, w1=w1, b1c=moe_b1c, w2=w2, b2c=moe_b2c)
             for c in range(NCORES)]
    moef = [r["moef"] for r in _run(ncD, mapsD)]

    ncX1 = _cached("X1", lambda: build_addln("lnx1", 3, gb=False, out_f32=False,
                                             out_sum=True, dtypes=("bf16", "f32", "f32")))
    resX1 = _run(ncX1, [dict(x0=moef[c], x1=xsh[c], x2=y0f[c]) for c in range(NCORES)])
    xn1b = [r["xnb"] for r in resX1]
    x1f = [r["sumf"] for r in resX1]

    wqkvT = np.ascontiguousarray(gi("L1_attn_w_qkv").T)
    ncE = _cached("E", lambda: build_gemm("qkv1", D, 1536, T, act="identity"))
    qkvT = [r["out"] for r in _run(ncE, [dict(wT=wqkvT, xb=xn1b[c]) for c in range(NCORES)])]

    w_outT1 = np.ascontiguousarray(gi("L1_attn_w_out").T)
    bout1 = _cols(gi("L1_attn_b_out"), 8)
    masks = np.zeros((128, 4 * 512), np.float32)
    for k_ in range(4):
        for pj in range(128):
            masks[pj, k_ * 512:(k_ + 1) * 512] = (np.arange(512) >= pj + 128 * k_)
    masks = masks.astype(bf16)
    ncFe = _cached("Fe", lambda: build_l1_attn(False))
    ncFo = _cached("Fo", lambda: build_l1_attn(True))
    mapsE, mapsO = [], []
    for b in range(B):
        qk0, qk1 = qkvT[2 * b], qkvT[2 * b + 1]
        kT_f = np.ascontiguousarray(np.concatenate([qk0[512:1024], qk1[512:1024]], axis=1))
        v_f = np.ascontiguousarray(np.concatenate([qk0[1024:1536].T, qk1[1024:1536].T], axis=0))
        mapsE.append(dict(qT=np.ascontiguousarray(qk0[0:512]),
                          kT=np.ascontiguousarray(qk0[512:1024]),
                          vtok=np.ascontiguousarray(qk0[1024:1536].T),
                          w_outT=w_outT1, bout=bout1, masks=masks))
        mapsO.append(dict(qT=np.ascontiguousarray(qk1[0:512]), kT=kT_f, vtok=v_f,
                          w_outT=w_outT1, bout=bout1, masks=masks))
    attn1_e = [r["attn1"] for r in _run(ncFe, mapsE)]
    attn1_o = [r["attn1"] for r in _run(ncFo, mapsO)]
    attn1 = []
    for c in range(NCORES):
        attn1.append(attn1_e[c // 2] if c % 2 == 0 else attn1_o[c // 2])

    W1T = np.ascontiguousarray(gi("L1_ffn_W1").astype(np.float32).T)
    W2T = np.ascontiguousarray(gi("L1_ffn_W2").astype(np.float32).T)
    ncG = _cached("G", build_tpartials)
    mapsG = [dict(w1s=np.ascontiguousarray(W1T[c * 128:(c + 1) * 128]),
                  w2s=np.ascontiguousarray(W2T[c * 512:(c + 1) * 512].reshape(128, 4096)))
             for c in range(NCORES)]
    parts = [r["part"] for r in _run(ncG, mapsG)]
    pa = np.ascontiguousarray(np.concatenate([p[:, 0:1] for p in parts], 1).astype(np.float32))
    pb = np.ascontiguousarray(np.concatenate([p[:, 1:2] for p in parts], 1).astype(np.float32))
    ncH = _cached("H", build_ternarize)
    resH = _run(ncH, [dict(pa=pa, pb=pb, **mapsG[c]) for c in range(NCORES)])
    W1tT = np.concatenate([r["t1"] for r in resH], axis=0)
    W2tT = np.concatenate([r["t2"].reshape(512, 1024) for r in resH], axis=0)

    ncL1a = _cached("L1a", lambda: build_gemm("l1ffn1", D, 4096, T, act="gelu",
                                              nbias=32, w_bf16=True))
    fb1 = _cols(gi("L1_ffn_b1"), 32)
    h1 = [r["out"] for r in _run(ncL1a, [dict(wT=W1tT, xb=xn1b[c], bias=fb1)
                                         for c in range(NCORES)])]
    ncL1b = _cached("L1b", lambda: build_gemm("l1ffn2", 4096, D, T, act="identity",
                                              out_f32=True, nbias=8, w_bf16=True))
    fb2 = _cols(gi("L1_ffn_b2"), 8)
    ffn1 = [r["outf"] for r in _run(ncL1b, [dict(wT=W2tT, xb=h1[c], bias=fb2)
                                            for c in range(NCORES)])]

    ncFin = _cached("Fin", lambda: build_add("fin", 3))
    resF = _run(ncFin, [dict(x0=x1f[c], x1=attn1[c], x2=ffn1[c]) for c in range(NCORES)])
    out = np.empty((B, S, D), np.float32)
    for c in range(NCORES):
        b, hh = c // 2, c % 2
        out[b, hh * T:(hh + 1) * T, :] = resF[c]["y"].T
    return out


def build_vtok(ctx=None, A=None):
    if ctx is None:
        nc = _mk("vtok")
        A = dict(xnb=nc.dram_tensor("xnb", [D, T], BF16, kind="ExternalInput").ap(),
                 wvT=nc.dram_tensor("wvT", [D, 512], F32, kind="ExternalInput").ap(),
                 vtok=nc.dram_tensor("vtok", [T, 512], BF16, kind="ExternalOutput").ap())
    else:
        nc = ctx.nc
    xnb, wvT, vtok = A["xnb"], A["wvT"], A["vtok"]
    NST = T // 128

    with ExitStack() as es:
        sb = lambda n, *a: es.enter_context(nc.sbuf_tensor(_u(n), *a))
        ps = lambda n, *a: es.enter_context(nc.psum_tensor(_u(n), *a))
        _semhs = []
        def sem(n):
            h = es.enter_context(nc.semaphore(_u(n)))
            _semhs.append(h)
            return h
        s_xv = sb("s_xv", [128, 2 * NCH * 128], BF16)
        s_w = sb("s_w", [128, NCH * 512], BF16)
        s_o = sb("s_o", [128, 2 * 512], BF16)
        p2 = ps("p2", [128, 1024], F32)
        xl = [sem("xl0"), sem("xl1")]
        wl = sem("wl"); te = sem("te"); ev = sem("ev")
        stv = [sem("sv0"), sem("sv1")]
        if ctx is None:
            _clean_sems(nc, _semhs)
            block = es.enter_context(nc.Block())
        else:
            ctx.clean(_semhs)
            block = ctx.block

        @block.gpsimd
        def _(g):
            for c in range(NCH):
                g.dma_start(out=s_w[:, c * 512:(c + 1) * 512],
                            in_=wvT[c * 128:(c + 1) * 128, :]).then_inc(wl, 16)
            for stile in range(NST):
                g.wait_ge(ev, stile + 1)
                g.dma_start(out=vtok[stile * 128:(stile + 1) * 128, :],
                            in_=s_o[:, (stile % 2) * 512:(stile % 2 + 1) * 512]
                            ).then_inc(stv[stile % 2], 16)
            g.wait_ge(stv[0], 16 * 8)
            g.wait_ge(stv[1], 16 * 8)

        @block.sync
        def _(sy):
            for stile in range(NST):
                if stile >= 2:
                    sy.wait_ge(te, stile - 1)
                sy.dma_start(out=s_xv[:].rearrange("p (b c t) -> p b c t", b=2, c=NCH)[:, stile % 2],
                             in_=xnb[:, stile * 128:(stile + 1) * 128].rearrange("(c p) t -> p c t", p=128)
                             ).then_inc(xl[stile % 2], 16)

        @block.tensor
        def _(t):
            t.wait_ge(wl, 16 * NCH)
            for stile in range(NST):
                t.wait_ge(xl[stile % 2], 16 * (stile // 2 + 1))
                if stile >= 2:
                    t.wait_ge(ev, stile - 1)
                xv = s_xv[:].rearrange("p (b c t) -> p b c t", b=2, c=NCH)[:, stile % 2]
                for c in range(NCH):
                    mm = t.matmul(p2[:, (stile % 2) * 512:(stile % 2 + 1) * 512],
                                  xv[:, c], s_w[:, c * 512:(c + 1) * 512],
                                  start=(c == 0), stop=(c == NCH - 1))
                mm.then_inc(te, 1)

        @block.scalar
        def _(s):
            for stile in range(NST):
                s.wait_ge(te, stile + 1)
                if stile >= 2:
                    s.wait_ge(stv[stile % 2], 16 * (stile // 2))
                s.activation(out=s_o[:, (stile % 2) * 512:(stile % 2 + 1) * 512],
                             in_=p2[:, (stile % 2) * 512:(stile % 2 + 1) * 512],
                             func=AF.Identity, bias=0.0)
                s.activation(out=s_o[:, (stile % 2) * 512:(stile % 2 + 1) * 512],
                             in_=p2[:, (stile % 2) * 512:(stile % 2 + 1) * 512],
                             func=AF.Identity, bias=0.0).then_inc(ev, 1)
    return nc


def _cc_phase(ctx, ops):
    nc = ctx.nc
    with ExitStack() as es:
        cc = es.enter_context(nc.semaphore(_u("cc")))
        ctx.clean([cc])

        @ctx.block.gpsimd
        def _(g):
            n = 0
            for kind, op, groups, i_, o_ in ops:
                g.collective_compute(kind, op, replica_groups=groups,
                                     ins=[i_.opt()], outs=[o_.opt()]).then_inc(cc)
                n += 1
            g.wait_ge(cc, n)


PAIRS = [[0, 1], [2, 3], [4, 5], [6, 7]]
ALL8 = [[0, 1, 2, 3, 4, 5, 6, 7]]


def build_ternchain_A(ctx, A):
    nc = ctx.nc
    w1s, w2s, rowd = A["w1s"], A["w2s"], A["rowd"]
    part, ptot, cc = A["part"], A["ptot"], A["cc"]
    es = ctx.ges
    sb = lambda n, *a: es.enter_context(nc.sbuf_tensor(_u(n), *a))
    _semhs = []
    def sem(n):
        h = es.enter_context(nc.semaphore(_u(n)))
        _semhs.append(h)
        return h
    s_w = sb("tca_w", [128, 2 * 4096], F32)
    s_r = sb("tca_r", [128, 8], F32)
    s_row = sb("tca_row", [1, 256], F32)
    s_pq = sb("tca_pq", [1, 8], F32)
    s_spc = sb("tca_spc", [128, 512], F32)
    ld = sem("ld"); ve = sem("ve"); st = sem("st")
    ctx.clean(_semhs)
    block = ctx.block

    @block.sync
    def _(sy):
        sy.dma_start(out=s_w[:, 0:4096], in_=w1s).then_inc(ld, 16)
        sy.dma_start(out=s_w[:, 4096:8192], in_=w2s).then_inc(ld, 16)
        sy.wait_ge(ve, 1)
        sy.dma_start(out=rowd[0:1, 0:128].rearrange("a (p b) -> (a p) b", p=128),
                     in_=s_r[:, 0:1]).then_inc(st, 16)
        sy.dma_start(out=rowd[0:1, 128:256].rearrange("a (p b) -> (a p) b", p=128),
                     in_=s_r[:, 4:5]).then_inc(st, 16)
        sy.wait_ge(st, 32)
        sy.dma_start(out=s_row[:], in_=rowd).then_inc(ld, 16)
        sy.wait_ge(ve, 2)
        sy.dma_start(out=part, in_=s_pq[0:1, 0:2]).then_inc(st, 16)

    @block.vector
    def _(v):
        v.wait_ge(ld, 32)
        v.tensor_reduce(out=s_r[:, 0:1], in_=s_w[:, 0:4096], axis=AX.X, op=OP.add,
                        apply_absolute_value=True)
        v.tensor_reduce(out=s_r[:, 4:5], in_=s_w[:, 4096:8192], axis=AX.X, op=OP.add,
                        apply_absolute_value=True)
        v.tensor_copy(out=s_spc[:], in_=s_w[:, 0:512])
        v.tensor_copy(out=s_spc[:, 0:1], in_=s_r[:, 0:1])
        v.tensor_copy(out=s_spc[:, 4:5], in_=s_r[:, 4:5])
        v.tensor_copy(out=s_spc[:], in_=s_w[:, 512:1024]).then_inc(ve, 1)
        v.wait_ge(ld, 48)
        v.reduce_sum(out=s_pq[0:1, 0:1], in_=s_row[0:1, 0:128], axis=AX.X)
        v.reduce_sum(out=s_pq[0:1, 4:5], in_=s_row[0:1, 128:256], axis=AX.X)
        v.tensor_copy(out=s_spc[:], in_=s_w[:, 0:512])
        v.tensor_copy(out=s_pq[0:1, 1:2], in_=s_pq[0:1, 4:5])
        v.tensor_copy(out=s_spc[:], in_=s_w[:, 512:1024])
        v.tensor_copy(out=s_spc[0:1, 0:2], in_=s_pq[0:1, 0:2])
        v.tensor_copy(out=s_spc[:], in_=s_w[:, 1024:1536]).then_inc(ve, 1)

    @block.gpsimd
    def _(g):
        g.wait_ge(st, 48)
        g.collective_compute("AllReduce", OP.add, replica_groups=ALL8,
                             ins=[part.opt()], outs=[ptot.opt()]).then_inc(cc)


def build_ternchain_B(ctx, A):
    nc = ctx.nc
    w1s, w2s, thrd = A["w1s"], A["w2s"], A["thrd"]
    ptot, cc = A["ptot"], A["cc"]
    t1, t2, w1tg, w2tg = A["t1"], A["t2"], A["w1tg"], A["w2tg"]
    NTOT = float(1024 * 4096)
    es = ctx.ges
    sb = lambda n, *a: es.enter_context(nc.sbuf_tensor(_u(n), *a))
    _semhs = []
    def sem(n):
        h = es.enter_context(nc.semaphore(_u(n)))
        _semhs.append(h)
        return h
    s_w = sb("tcb_w", [128, 2 * 512], F32)
    s_le = sb("tcb_le", [128, 512], F32)
    s_t = sb("tcb_t", [128, 2 * 512], BF16)
    s_p = sb("tcb_p", [1, 8], F32)
    s_st = sb("tcb_st", [1, 32], F32)
    s_throw = sb("tcb_throw", [1, 512], F32)
    s_thr = sb("tcb_thr", [128, 4], F32)
    s_onerow = sb("tcb_onerow", [1, 128], F32)
    s_spc = sb("tcb_spc", [128, 1024], F32)
    ld = sem("ld"); ve = sem("ve"); vc = sem("vc"); st = sem("st")
    ldw = [sem("lw0"), sem("lw1")]
    ctx.clean(_semhs)
    block = ctx.block
    NQ = 8

    @block.sync
    def _(sy):
        sy.wait_ge(cc, 1)
        sy.dma_start(out=s_p[0:1, 0:2], in_=ptot).then_inc(ld, 16)
        sy.wait_ge(ve, 1)
        sy.dma_start(out=thrd, in_=s_throw[:]).then_inc(st, 16)
        sy.wait_ge(st, 16)
        sy.dma_start(out=s_thr[:],
                     in_=thrd[0:1, :].rearrange("a (b p) -> (a p) b", p=128),
                     allow_slow_non_contiguous=True).then_inc(ld, 16)
        def t_store(j):
            tdst = t1 if j < NQ else t2
            sy.wait_ge(vc, j + 1)
            sy.dma_start(out=tdst[:, (j % NQ) * 512:(j % NQ + 1) * 512],
                         in_=s_t[:, (j % 2) * 512:(j % 2 + 1) * 512]
                         ).then_inc(st, 16)
        for idx in range(2 * NQ):
            if idx >= 2:
                sy.wait_ge(vc, idx - 1)
            wsrc = w1s if idx < NQ else w2s
            q = idx % NQ
            sy.dma_start(out=s_w[:, (idx % 2) * 512:(idx % 2 + 1) * 512],
                         in_=wsrc[:, q * 512:(q + 1) * 512]
                         ).then_inc(ldw[idx % 2], 16)
            if idx >= 1:
                t_store(idx - 1)
        t_store(2 * NQ - 1)

    @block.vector
    def _(v):
        v.memset(s_onerow[:], 1.0)
        v.memset(s_spc[:], 0.0)
        v.wait_ge(ld, 16)
        SP = lambda: v.tensor_copy(out=s_spc[:, 0:512], in_=s_spc[:, 512:1024])
        v.tensor_scalar(out=s_st[0:1, 0:1], in0=s_p[0:1, 0:1], scalar1=1.0 / NTOT,
                        scalar2=None, op0=OP.mult)
        v.tensor_scalar(out=s_st[0:1, 4:5], in0=s_p[0:1, 1:2], scalar1=1.0 / NTOT,
                        scalar2=None, op0=OP.mult)
        SP()
        v.tensor_scalar(out=s_st[0:1, 8:9], in0=s_st[0:1, 0:1], scalar1=1e-8,
                        scalar2=0.5, op0=OP.max, op1=OP.mult)
        v.tensor_scalar(out=s_st[0:1, 12:13], in0=s_st[0:1, 4:5], scalar1=1e-8,
                        scalar2=0.5, op0=OP.max, op1=OP.mult)
        SP()
        v.tensor_scalar(out=s_st[0:1, 16:17], in0=s_st[0:1, 8:9], scalar1=-1.0,
                        scalar2=None, op0=OP.mult)
        v.tensor_scalar(out=s_st[0:1, 20:21], in0=s_st[0:1, 12:13], scalar1=-1.0,
                        scalar2=None, op0=OP.mult)
        SP()
        for i, sl in enumerate([8, 16, 12, 20]):
            v.tensor_scalar(out=s_throw[0:1, i * 128:(i + 1) * 128],
                            in0=s_onerow[:], scalar1=s_st[0:1, sl:sl + 1],
                            scalar2=None, op0=OP.mult)
        v.tensor_copy(out=s_spc[0:1, 0:512], in_=s_throw[0:1, 0:512])
        SP().then_inc(ve, 1)
        v.wait_ge(ld, 32)
        for m in range(2):
            for q in range(NQ):
                idx = m * NQ + q
                v.wait_ge(ldw[idx % 2], 16 * (idx // 2 + 1))
                if idx >= 2:
                    v.wait_ge(st, 16 * idx)
                wv = s_w[:, (idx % 2) * 512:(idx % 2 + 1) * 512]
                tv = s_t[:, (idx % 2) * 512:(idx % 2 + 1) * 512]
                v.tensor_scalar(out=s_le[:], in0=wv, scalar1=s_thr[:, 2 * m + 1:2 * m + 2],
                                scalar2=None, op0=OP.is_le)
                v.scalar_tensor_tensor(out=tv, in0=wv, scalar=s_thr[:, 2 * m:2 * m + 1],
                                       in1=s_le[:], op0=OP.is_ge, op1=OP.subtract)
                v.tensor_copy(out=s_spc[:, 0:512], in_=tv)
                v.tensor_copy(out=s_spc[:, 0:64], in_=tv[:, 448:512]).then_inc(vc, 1)

    @block.gpsimd
    def _(g):
        g.wait_ge(st, 16 + 16 * 2 * NQ)
        g.collective_compute("AllGather", OP.bypass, replica_groups=ALL8,
                             ins=[t1.opt()], outs=[w1tg.opt()]).then_inc(cc)
        g.collective_compute("AllGather", OP.bypass, replica_groups=ALL8,
                             ins=[t2.opt()], outs=[w2tg.opt()]).then_inc(cc)


def build_mega(nphases=99):
    nc = _mk("mega")
    EI = lambda n, shp, dt: nc.dram_tensor(n, shp, dt, kind="ExternalInput").ap()
    IN = lambda n, shp, dt: nc.dram_tensor(n, shp, dt, kind="Internal").ap()
    xT = EI("xT", [D, T], F32)
    wqkvT = EI("wqkvT", [D, 1536], F32)
    bqkv = EI("bqkv", [128, 8], F32)
    bv_row = EI("bv_row", [1, 512], F32)
    owT = EI("owT", [8, 128, 512], F32)
    owb = EI("owb", [128, 8], F32)
    ln1p = EI("ln1p", [128, 16], F32)
    lin1_wT = EI("lin1_wT", [32, 128, 1024], F32)
    b1c = EI("b1c", [128, 32], F32)
    lin2_wT = EI("lin2_wT", [8, 128, 4096], F32)
    b2c = EI("b2c", [128, 8], F32)
    ln2p = EI("ln2p", [128, 16], F32)
    sw = EI("sw", [D, 8], F32)
    swb = EI("swb", [128, 8], F32)
    w8t = EI("w8t", [128, 128], F32)
    ident = EI("ident", [128, 128], F32)
    sel = EI("sel", [8, 1024], F32)
    mw1 = EI("mw1", [8, 32, 128, 1024], F32)
    mb1 = EI("mb1", [128, 256], F32)
    mw2 = EI("mw2", [8, 8, 128, 4096], F32)
    mb2 = EI("mb2", [128, 64], F32)
    qkvT1w = EI("qkvT1w", [D, 1536], F32)
    qkv1t = EI("qkv1t", [8, 128, 1024], F32)
    w_outT1 = EI("w_outT1", [512, D], F32)
    bout1 = EI("bout1", [128, 8], F32)
    l1masks = EI("l1masks", [128, 4 * 20 * 512], BF16)
    fb1 = EI("fb1", [128, 32], F32)
    fb2 = EI("fb2", [128, 8], F32)
    w1s = EI("w1s", [128, 4096], F32)
    w2s = EI("w2s", [128, 4096], F32)
    outT = nc.dram_tensor("outT", [D, T], F32, kind="ExternalOutput").ap()
    xn0f = IN("xn0f", [D, T], F32)
    xn0b = IN("xn0b", [D, T], BF16)
    xg = IN("xg", [2 * D, T], BF16)
    postg = IN("postg_i", [512, S], BF16)
    rsin = IN("rsin", [2 * D, T], F32)
    rsout = IN("rsout", [D, T], F32)
    dumpb = IN("dumpb", [D, S], BF16)
    xpf = IN("xpf", [D, T], F32)
    xpb = IN("xpb", [D, T], BF16)
    hbuf = IN("hbuf", [4096, T], BF16)
    fff = IN("fff", [D, T], F32)
    y0f = IN("y0f", [D, T], F32)
    moef = IN("moef_i", [D, T], BF16)
    x1f = IN("x1f", [D, T], F32)
    xn1b = IN("xn1b", [D, T], BF16)
    qTb = IN("qTb", [512, T], BF16)
    kpart = IN("kpart", [512, T], BF16)
    vtokb = IN("vtokb", [T, 512], BF16)
    kg = IN("kg", [2 * 512, T], BF16)
    vg = IN("vg", [S, 512], BF16)
    part = IN("part_i", [1, 2], F32)
    ptot = IN("ptot", [1, 2], F32)
    t1 = IN("t1_i", [128, 4096], BF16)
    t2 = IN("t2_i", [128, 4096], BF16)
    w1tg = IN("w1tg", [D, 4096], BF16)
    w2tg = IN("w2tg", [D, 4096], BF16)
    attn1 = IN("attn1_i", [D, T], F32)
    h1 = IN("h1", [4096, T], BF16)
    ffn1 = IN("ffn1", [D, T], F32)
    rowd = IN("rowd", [1, 256], F32)
    thrd = IN("thrd", [1, 512], F32)

    with ExitStack() as oes:
        bar = oes.enter_context(nc.semaphore("mega_bar"))
        cc_ar = oes.enter_context(nc.semaphore("cc_res"))
        assert cc_ar.num == bar.num + 1
        block = oes.enter_context(nc.Block())
        ctx = MegaCtx(nc, block, bar, reserved=[cc_ar])
        _ph = [0]
        def _go():
            _ph[0] += 1
            return _ph[0] <= nphases
        if _go():
            build_ln(ctx=ctx, A=dict(x=xT, xnf=xn0f, xnb=xn0b))
        if _go():
            _cc_phase(ctx, [("AllGather", OP.bypass, PAIRS, xn0b, xg)])
        def xn_slice(t0, w):
            r = t0 // T
            return xg[r * D:(r + 1) * D][:, t0 % T: t0 % T + w]
        if _go():
            build_l0_attn(ctx=ctx, A=dict(xn_slice=xn_slice, wqkvT=wqkvT, bqkv=bqkv,
                                          bv_row=bv_row, postg=postg))
        def op_out_slice(o, tb):
            return dumpb[o * 128:(o + 1) * 128, tb * 512:(tb + 1) * 512]
        def op_outf_slice(o, tb):
            h_ = tb // 4
            return rsin[h_ * D + o * 128: h_ * D + (o + 1) * 128,
                        (tb % 4) * 512:(tb % 4 + 1) * 512]
        if _go():
            build_gemm("outproj", 512, D, S, act="identity", scale=1.0 / 64.0, out_f32=True,
                       nbias=8, ctx=ctx,
                       A=dict(wTt=owT, xb=postg, bias=owb, no_bf16=True,
                              outf_slice=op_outf_slice))
        if _go():
            _cc_phase(ctx, [("ReduceScatter", OP.add, PAIRS,
                             rsin.rearrange("(h o) t -> h (o t)", h=2),
                             rsout.rearrange("(a o) t -> a (o t)", a=1))])
        if _go():
            build_addln("uln1", 2, gb=True, out_f32=True, ctx=ctx,
                        A=dict(xs=[xn0f, rsout], params=ln1p, xnf=xpf, xnb=xpb, sumf=None))
        if _go():
            build_gemm("lin1", D, 4096, T, act="gelu", nbias=32, ctx=ctx,
                       A=dict(wTt=lin1_wT, xb=xpb, bias=b1c,
                              out_slice=lambda o, tb: hbuf[o * 128:(o + 1) * 128, tb * 512:(tb + 1) * 512],
                              outf_slice=None))
        if _go():
            build_gemm("lin2", 4096, D, T, act="identity", out_f32=True, nbias=8, ctx=ctx,
                       A=dict(wTt=lin2_wT, xb=hbuf, bias=b2c, no_bf16=True,
                              outf_slice=lambda o, tb: fff[o * 128:(o + 1) * 128, tb * 512:(tb + 1) * 512]))
        if _go():
            build_addln("ln2", 2, gb=True, out_f32=True, ctx=ctx,
                        A=dict(xs=[xpf, fff], params=ln2p, xnf=y0f, xnb=None, sumf=None))
        if _go():
            build_moe(ctx=ctx, A=dict(xnb=xn0b, xnf=xn0f, sw=sw, swb=swb, w8t=w8t,
                                      ident=ident, sel=sel, w1=mw1, b1c=mb1, w2=mw2,
                                      b2c=mb2, moef=moef))
        if _go():
            ctx.begin_group()
            build_ternchain_A(ctx, A=dict(w1s=w1s, w2s=w2s, rowd=rowd,
                                          part=part, ptot=ptot, cc=cc_ar))
            build_addln("lnx1", 3, gb=False, out_f32=False, out_sum=True,
                        dtypes=("bf16", "f32", "f32"), ctx=ctx,
                        A=dict(xs=[moef, xT, y0f], params=None, xnf=None, xnb=xn1b, sumf=x1f))
            ctx.end_group()
        def qk_out_slice(o, tb):
            dst = qTb if o < 4 else kpart
            oo = o % 4
            return dst[oo * 128:(oo + 1) * 128, tb * 512:(tb + 1) * 512]
        if _go():
            ctx.begin_group()
            build_gemm("qkv1", D, 1024, T, act="identity", ctx=ctx,
                       A=dict(wTt=qkv1t, xb=xn1b, bias=None,
                              out_slice=qk_out_slice, outf_slice=None))
            build_vtok(ctx=ctx, A=dict(xnb=xn1b, wvT=qkvT1w[:, 1024:1536], vtok=vtokb))
            ctx.end_group()
        if _go():
            _cc_phase(ctx, [("AllGather", OP.bypass, PAIRS, kpart, kg),
                            ("AllGather", OP.bypass, PAIRS, vtokb, vg)])
        def k_slice(c, j0, w):
            hh = j0 // T
            return kg[hh * 512 + c * 128: hh * 512 + (c + 1) * 128,
                      j0 % T: j0 % T + w]
        if _go():
            ctx.begin_group()
            build_ternchain_B(ctx, A=dict(w1s=w1s, w2s=w2s, thrd=thrd, ptot=ptot,
                                          cc=cc_ar, t1=t1, t2=t2, w1tg=w1tg, w2tg=w2tg))
            build_l1_attn(False, uniform=True, ctx=ctx,
                          A=dict(qT=qTb, k_slice=k_slice, vtok=vg, w_outT=w_outT1,
                                 bout=bout1, masks=l1masks, attn1=attn1))

            @ctx.block.gpsimd
            def _(g):
                g.wait_ge(cc_ar, 3)
            ctx.end_group()
        if _go():
            build_gemm("l1ffn1", D, 4096, T, act="gelu", nbias=32, w_bf16=True, ctx=ctx,
                       A=dict(wT=w1tg, xb=xn1b, bias=fb1,
                              out_slice=lambda o, tb: h1[o * 128:(o + 1) * 128, tb * 512:(tb + 1) * 512],
                              outf_slice=None))
        if _go():
            build_gemm("l1ffn2", 4096, D, T, act="identity", out_f32=True, nbias=8,
                       w_bf16=True, ctx=ctx,
                       A=dict(wT=w2tg.rearrange("p (a c) -> (p a) c", a=4), xb=h1, bias=fb2,
                              fuse_add=[x1f, attn1], fuse_out=outT))
    return nc


def _uniform_masks(odd):
    m = np.zeros((128, 4 * 20 * 512), np.float32)
    ar = np.arange(512)
    for ib in range(4):
        for mi in range(20):
            sl = m[:, (ib * 20 + mi) * 512:(ib * 20 + mi + 1) * 512]
            if not odd:
                if mi < 4:
                    d = 128 * mi
                    for pj in range(128):
                        sl[pj, :] = (ar >= pj + d)
            else:
                if mi < 16:
                    sl[:, :] = 1.0
                else:
                    d = 128 * (mi - 16)
                    for pj in range(128):
                        sl[pj, :] = (ar >= pj + d)
    return np.ascontiguousarray(m.astype(bf16))


def _tile_wT(wT):
    K, M = wT.shape
    NK, NO = K // 128, M // 128
    return np.ascontiguousarray(
        wT.reshape(NK, 128, NO, 128).transpose(2, 1, 0, 3).reshape(NO, 128, NK * 128))


def kernel(**inputs):
    gi = lambda n: np.asarray(inputs[n])
    x = gi("L0_attn_src").astype(np.float32)
    xsh = shard_x(x)
    nc = _cached("MEGA", build_mega)

    W = gi("L0_attn_in_proj_w")
    bqkv_full = gi("L0_attn_in_proj_b")
    out_wT_full = np.ascontiguousarray(gi("L0_attn_out_w").T)
    boutc = _cols(gi("L0_attn_out_b"), 8)
    zbout = np.zeros_like(boutc)
    ln1p = np.concatenate([_cols(gi("L0_attn_ln1_g"), 8), _cols(gi("L0_attn_ln1_b"), 8)], axis=1)
    ln2p = np.concatenate([_cols(gi("L0_attn_ln2_g"), 8), _cols(gi("L0_attn_ln2_b"), 8)], axis=1)
    lin1_wT = _tile_wT(gi("L0_attn_lin1_w").T)
    lin2_wT = _tile_wT(gi("L0_attn_lin2_w").T)
    b1c = _cols(gi("L0_attn_lin1_b"), 32)
    b2c = _cols(gi("L0_attn_lin2_b"), 8)
    sw = gi("L0_ffn_switch_w").astype(np.float32)
    swb_t = np.ascontiguousarray(np.tile(gi("L0_ffn_switch_b").astype(np.float32), (128, 1)))
    w8t = np.ascontiguousarray(np.tile(np.arange(8, 0, -1, dtype=np.float32), (128, 16)))
    ident = np.eye(128, dtype=np.float32)
    sel = np.ascontiguousarray(np.concatenate(
        [np.eye(8, dtype=np.float32)[:, e:e + 1] @ np.ones((1, 128), np.float32)
         for e in range(8)], axis=1))
    mw1 = np.ascontiguousarray(gi("L0_ffn_w1").astype(np.float32)
                               .reshape(8, 8, 128, 32, 128).transpose(0, 3, 2, 1, 4)
                               .reshape(8, 32, 128, 1024))
    mw2 = np.ascontiguousarray(gi("L0_ffn_w2").astype(np.float32)
                               .reshape(8, 32, 128, 8, 128).transpose(0, 3, 2, 1, 4)
                               .reshape(8, 8, 128, 4096))
    mb1 = np.ascontiguousarray(gi("L0_ffn_b1").astype(np.float32).reshape(256, 128).T)
    mb2 = np.ascontiguousarray(gi("L0_ffn_b2").astype(np.float32).reshape(64, 128).T)
    qkvT1w = np.ascontiguousarray(gi("L1_attn_w_qkv").T)
    qkv1t = _tile_wT(qkvT1w[:, 0:1024])
    w_outT1 = np.ascontiguousarray(gi("L1_attn_w_out").T)
    bout1 = _cols(gi("L1_attn_b_out"), 8)
    fb1 = _cols(gi("L1_ffn_b1"), 32)
    fb2 = _cols(gi("L1_ffn_b2"), 8)
    W1T = np.ascontiguousarray(gi("L1_ffn_W1").astype(np.float32).T)
    W2T = np.ascontiguousarray(gi("L1_ffn_W2").astype(np.float32).T)
    me = _uniform_masks(False)
    mo = _uniform_masks(True)

    maps = []
    for c in range(NCORES):
        b, hh = c // 2, c % 2
        o = 512 * hh
        wqkvT = np.ascontiguousarray(np.concatenate(
            [W[o:o + 512], W[1024 + o:1024 + o + 512], W[2048 + o:2048 + o + 512]], axis=0).T)
        bq = np.ascontiguousarray(np.concatenate(
            [bqkv_full[o:o + 512], bqkv_full[1024 + o:1024 + o + 512]]).reshape(8, 128).T.astype(np.float32))
        bv = np.ascontiguousarray(
            bqkv_full[2048 + o:2048 + o + 512].reshape(1, 512).astype(np.float32))
        maps.append(dict(
            xT=xsh[c], wqkvT=wqkvT, bqkv=bq, bv_row=bv,
            owT=_tile_wT(out_wT_full[o:o + 512, :]),
            owb=(boutc if hh == 0 else zbout),
            ln1p=ln1p, lin1_wT=lin1_wT, b1c=b1c, lin2_wT=lin2_wT, b2c=b2c, ln2p=ln2p,
            sw=sw, swb=swb_t, w8t=w8t, ident=ident, sel=sel,
            mw1=mw1, mb1=mb1, mw2=mw2, mb2=mb2,
            qkvT1w=qkvT1w, qkv1t=qkv1t, w_outT1=w_outT1, bout1=bout1,
            l1masks=(me if hh == 0 else mo), fb1=fb1, fb2=fb2,
            w1s=np.ascontiguousarray(W1T[c * 128:(c + 1) * 128]),
            w2s=np.ascontiguousarray(W2T[c * 512:(c + 1) * 512].reshape(128, 4096)),
        ))
    res = run_bass_kernel_spmd(nc, maps, core_ids=list(range(NCORES)))
    out = np.empty((B, S, D), np.float32)
    for c in range(NCORES):
        b, hh = c // 2, c % 2
        out[b, hh * T:(hh + 1) * T, :] = res.results[c]["outT"].T
    return out
